# version 06
# baseline (speedup 1.0000x reference)
import numpy as np
import concourse.bass as bass
import concourse.mybir as mybir
from concourse.bass_utils import run_bass_kernel_spmd
from contextlib import ExitStack

F32 = mybir.dt.float32
BF16 = mybir.dt.bfloat16
U32 = mybir.dt.uint32
AF = mybir.ActivationFunctionType
ALU = mybir.AluOpType

B, L, D, E, N, R = 4, 2048, 2048, 4096, 16, 128
Ec = E // 2
Lc = 1024
NCH = L // Lc
DT = D // 128
ET = Ec // 128
EB = 8
NBLK = ET // EB
EPS = 1e-5
MAGIC = 777001000

def fl_mod(w): return w
def fl_zbc(w, c): return 2 + w * 2 + c
def fl_out(w, c): return 6 + w * 2 + c
NFLAGS = 10


class Ctr:
    def __init__(self):
        self.v = {}
    def inc(self, k, n=1):
        self.v[k] = self.v.get(k, 0) + n
        return self.v[k]
    def get(self, k):
        return self.v.get(k, 0)


class W:
    def __init__(self, e):
        object.__setattr__(self, "_e", e)
        object.__setattr__(self, "_last", None)
    def __getattr__(self, name):
        f = getattr(self._e, name)
        if not callable(f):
            return f
        def g(*a, **k):
            r = f(*a, **k)
            if hasattr(r, "then_inc"):
                object.__setattr__(self, "_last", r)
            return r
        return g
    def sem_inc(self, s, n):
        if self._last is not None:
            self._last.then_inc(s, n)
        else:
            self._e.sem_inc(s, n)


_STOP = [None]
_RACE = [False]
_DBG = [False]

def build_program():
    nc = bass.Bass("TRN2", target_bir_lowering=False, enable_asserts=False,
                   detect_race_conditions=_RACE[0])

    def I(name, shape, dt=F32):
        return nc.dram_tensor(name, shape, dt, kind="ExternalInput").ap()

    residT = I("residT", [D, L])
    role_in = I("role", [1, 1], U32)
    bid_in = I("bid", [1, 1], U32)
    c4_in = I("c4", [128, 32])
    wmod_in = I("wmod_s", [1024, 6 * D])
    in_wT = I("in_wT", [D, Ec])
    skip_wT = I("skip_wT", [D, Ec])
    out_wT = I("out_wT", [Ec, D])
    wd1T_in = I("wd1T", [Ec, R])
    wd2T_in = I("wd2T", [R, Ec])
    wBT_in = I("wBT", [Ec, N])
    wCT_in = I("wCT", [Ec, N])
    alog_in = I("alog", [128, ET * N])
    convw_in = I("convw", [128, 64])
    cvec_in = I("cvec", [128, 64])
    wbase_in = I("W_base", [D, D])
    fvec_in = I("fvec", [128, 32])
    as_in = I("A_s", [D, N])
    bs_in = I("B_s", [N, D])
    art_in = I("A_rT", [D, 96])
    brs_in = I("B_rs", [96, D])
    wrout_in = I("W_router", [D, 6])
    ident_in = I("ident", [128, 128])

    out_T = nc.dram_tensor("out_T", [D, Lc], F32, kind="ExternalOutput").ap()
    dbg = nc.dram_tensor("dbg", [16384, Lc], F32, kind="ExternalOutput").ap() if _DBG[0] else None

    spill_ss = nc.dram_tensor("spill_ss", [Ec, Lc], BF16, kind="Internal").ap()
    spill_yp = nc.dram_tensor("spill_yp", [Ec, Lc], BF16, kind="Internal").ap()
    spill_h = nc.dram_tensor("spill_h", [2 * D, Lc], BF16, kind="Internal").ap()
    rows_bc = nc.dram_tensor("rows_bc", [8, Lc], F32, kind="Internal").ap()
    bmcm_dr = nc.dram_tensor("bmcm_dr", [32, Lc], BF16, kind="Internal").ap()
    modp_dr = nc.dram_tensor("modp_dr", [4, 6 * D], F32, kind="Internal").ap()
    modrow_dr = nc.dram_tensor("modrow_dr", [1, 6 * D], F32, kind="Internal").ap()
    wrow_dr = nc.dram_tensor("wrow_dr", [6, Lc], F32, kind="Internal").ap()

    sh_zbc = nc.dram_tensor("sh_zbc", [4 * 160, Lc], F32, kind="Internal", addr_space="Shared").ap()
    sh_out = nc.dram_tensor("sh_out", [4 * D, Lc], F32, kind="Internal", addr_space="Shared").ap()
    sh_mod = nc.dram_tensor("sh_mod", [8, 6 * D], F32, kind="Internal", addr_space="Shared").ap()
    sh_flag = nc.dram_tensor("sh_flag", [NFLAGS, 16], U32, kind="Internal", addr_space="Shared").ap()

    C = Ctr()
    with ExitStack() as octx:
        sem = {}
        for s in ["LD", "GD", "PE", "ACT", "DVE", "PL", "ST", "PX", "MS", "BR",
                  "R0", "R1", "GB0", "GB1", "E0", "E1", "E2", "E3",
                  "BC0", "BC1", "SK", "YP", "CA", "CB", "CR", "CH"]:
            sem[s] = octx.enter_context(nc.semaphore("s" + s))
        SB = lambda name, shape, dt: octx.enter_context(nc.sbuf_tensor(name, shape, dt))
        pm = [octx.enter_context(nc.psum_tensor(f"pm{i}", [128, 512], F32)) for i in range(8)]

        t_role = SB("t_role", [1, 1], U32)
        t_bidv = SB("t_bidv", [1, 1], U32)
        t_c4 = SB("t_c4", [128, 32], F32)
        t_sc4 = SB("t_sc4", [128, 32], BF16)
        t_negA = SB("t_negA", [128, ET * N], F32)
        t_convw = SB("t_convw", [128, 64], F32)
        t_cvec = SB("t_cvec", [128, 64], F32)
        t_fvec = SB("t_fvec", [128, 32], F32)
        t_gain = SB("t_gain", [128, 96], F32)
        t_state = SB("t_state", [128, 256], F32)
        t_ident = SB("t_ident", [128, 128], F32)
        t_ones = SB("t_ones", [128, 1], BF16)
        t_eps1 = SB("t_eps1", [1, 1], F32)
        t_one1 = SB("t_one1", [128, 1], F32)
        t_flagv = SB("t_flagv", [1, 16 * NFLAGS], U32)
        t_flagrd = SB("t_flagrd", [1, 16], U32)
        t_rows = [SB(f"t_row{i}", [1, Lc], F32) for i in range(3)]
        t_tail = SB("t_tail", [128, ET * 4], F32)
        t_brs = SB("t_brs", [1, 8], F32)

        block = octx.enter_context(nc.Block())
        reg_role = octx.enter_context(nc.sync.register("r_role"))
        reg_bid = octx.enter_context(nc.sync.register("r_bid"))
        reg_ind = octx.enter_context(nc.sync.register("r_ind"))
        reg_fr = octx.enter_context(nc.sync.register("r_fr"))

        CONVB = lambda et: t_cvec[:, et:et + 1]
        WD2B = lambda et: t_cvec[:, 16 + et:17 + et]
        WDC = lambda et: t_cvec[:, 32 + et:33 + et]
        NRMW = lambda dt: t_cvec[:, 48 + dt:49 + dt]
        GSA = lambda dt: t_gain[:, 0 + dt:1 + dt]
        GCA = lambda dt: t_gain[:, 16 + dt:17 + dt]
        GGA = lambda dt: t_gain[:, 32 + dt:33 + dt]
        GSB = lambda dt: t_gain[:, 48 + dt:49 + dt]
        GCB = lambda dt: t_gain[:, 64 + dt:65 + dt]
        GGB = lambda dt: t_gain[:, 80 + dt:81 + dt]
        BBASE = lambda dt: t_fvec[:, dt:dt + 1]

        def barrier():
            @block.sync
            def _(sp):
                sp = W(sp)
                sp.wait_ge(sem["LD"], 16 * C.get("LD"))
                sp.wait_ge(sem["ST"], 16 * C.get("ST"))
                for s_ in ["R0", "R1", "E0", "E1", "E2", "E3", "BC0", "BC1", "SK", "YP", "CA", "CB", "CR", "CH"]:
                    if C.get(s_):
                        sp.wait_ge(sem[s_], 16 * C.get(s_))
            @block.gpsimd
            def _(g):
                g = W(g)
                g.wait_ge(sem["GD"], 16 * C.get("GD"))
                for s_ in ["GB0", "GB1"]:
                    if C.get(s_):
                        g.wait_ge(sem[s_], 16 * C.get(s_))
            tgt = 5 * C.inc("BRS")
            @block.sync
            def _(e):
                e = W(e)
                e.sem_inc(sem["BR"], 1)
                e.wait_ge(sem["BR"], tgt)
            @block.gpsimd
            def _(e):
                e = W(e)
                e.memset(t_brs[0:1, 0:1], 0.0)
                e.sem_inc(sem["BR"], 1)
                e.wait_ge(sem["BR"], tgt)
            @block.vector
            def _(e):
                e = W(e)
                e.memset(t_brs[0:1, 1:2], 0.0)
                e.sem_inc(sem["BR"], 1)
                e.wait_ge(sem["BR"], tgt)
            @block.scalar
            def _(e):
                e = W(e)
                e.memzero(t_brs[0:1, 2:3])
                e.sem_inc(sem["BR"], 1)
                e.wait_ge(sem["BR"], tgt)
            @block.tensor
            def _(e):
                e = W(e)
                e.sem_inc(sem["BR"], 1)
                e.wait_ge(sem["BR"], tgt)

        def poll(sp, f):
            sp.reg_mov(reg_ind, 1)
            with sp.While(reg_ind):
                sp.dma_start(out=t_flagrd[:], in_=sh_flag[f:f + 1, :]).then_inc(sem["PX"], 16)
                sp.drain()
                sp.reg_load(reg_fr, t_flagrd[0:1, 0:1])
                sp.reg_mov(reg_ind, 0)
                with sp.If_ne(reg_fr, MAGIC + f):
                    sp.reg_mov(reg_ind, 1)

        with ExitStack() as s0:
            HW = 3 * D
            slabM = s0.enter_context(nc.sbuf_tensor("slabM", [128, 8 * HW], BF16))
            t_mp = [s0.enter_context(nc.sbuf_tensor(f"t_mp{i}", [4, 512], F32)) for i in range(2)]
            n_fc = 6 * D // 512
            n_fch = n_fc // 2

            @block.sync
            def _(sp):
                sp = W(sp)
                for name, dst in [(role_in, t_role), (bid_in, t_bidv), (c4_in, t_c4),
                                  (alog_in, t_negA), (convw_in, t_convw), (cvec_in, t_cvec),
                                  (fvec_in, t_fvec), (ident_in, t_ident)]:
                    sp.dma_start(out=dst[:], in_=name).then_inc(sem["LD"], 16)
                    C.inc("LD")
                sp.wait_ge(sem["LD"], 16 * C.get("LD"))
                sp.reg_load(reg_role, t_role[0:1, 0:1])
                sp.reg_load(reg_bid, t_bidv[0:1, 0:1])
                sp.sem_inc(sem["MS"], 1)
            C.inc("MS")

            dve0 = C.get("DVE")
            @block.vector
            def _(v):
                v = W(v)
                v.wait_ge(sem["MS"], C.get("MS"))
                v.memset(t_ones[:], 1.0)
                v.memset(t_one1[:], 1.0)
                v.memset(t_eps1[:], EPS)
                for f in range(NFLAGS):
                    v.memset(t_flagv[:, 16 * f:16 * (f + 1)], MAGIC + f)
                v.memset(t_state[:], 0.0)
                v.wait_ge(sem["ACT"], C.get("ACT") + 2)
                v.tensor_scalar(out=t_negA[:], in0=t_negA[:], scalar1=-1.0, scalar2=None, op0=ALU.mult)
                v.tensor_scalar(out=t_sc4[:], in0=t_c4[:], scalar1=1.0, scalar2=None, op0=ALU.mult)
                v.sem_inc(sem["DVE"], 1)
            C.inc("DVE")

            act0 = C.get("ACT")
            @block.scalar
            def _(a):
                a = W(a)
                a.wait_ge(sem["MS"], C.get("MS"))
                a.activation(out=t_negA[:], in_=t_negA[:], func=AF.Exp).then_inc(sem["ACT"], 1)
                a.activation(out=t_c4[:], in_=t_c4[:], func=AF.Silu).then_inc(sem["ACT"], 1)
            C.inc("ACT", 2)

            if _STOP[0] == 'm1':
                barrier()
                return nc

            gd_m0 = C.get("GD")
            pe_m0 = C.get("PE")
            @block.gpsimd
            def _(g):
                g = W(g)
                for half in range(2):
                    if half == 1:
                        g.wait_ge(sem["PE"], pe_m0 + n_fch)
                    for kt in range(8):
                        g.dma_start(out=slabM[:, kt * HW:(kt + 1) * HW],
                                    in_=wmod_in[kt * 128:(kt + 1) * 128, half * HW:(half + 1) * HW]
                                    ).then_inc(sem["GD"], 16)
                        C.inc("GD")
            pe0 = C.get("PE")
            @block.tensor
            def _(t):
                t = W(t)
                t.wait_ge(sem["DVE"], C.get("DVE"))
                for half in range(2):
                    t.wait_ge(sem["GD"], 16 * (gd_m0 + 8 * (half + 1)))
                    for fcl in range(n_fch):
                        fc = half * n_fch + fcl
                        if fc >= 2:
                            t.wait_ge(sem["ACT"], act0 + 2 + (fc - 2) + 1)
                        for kt in range(8):
                            t.matmul(out=pm[fc % 2][0:4, 0:512],
                                     lhsT=t_sc4[:, kt * 4:(kt + 1) * 4],
                                     rhs=slabM[:, kt * HW + fcl * 512:kt * HW + (fcl + 1) * 512],
                                     start=(kt == 0), stop=(kt == 7))
                        t.sem_inc(sem["PE"], 1)
            C.inc("PE", n_fc)

            em0 = [C.get("E0"), C.get("E1")]
            @block.scalar
            def _(a):
                a = W(a)
                for fc in range(n_fc):
                    a.wait_ge(sem["PE"], pe0 + fc + 1)
                    if fc >= 2:
                        a.wait_ge(sem["E%d" % (fc % 2)], 16 * (em0[fc % 2] + (fc - 2) // 2 + 1))
                    a.activation(out=t_mp[fc % 2][:], in_=pm[fc % 2][0:4, 0:512],
                                 func=AF.Identity).then_inc(sem["ACT"], 1)
            act1 = C.get("ACT")
            C.inc("ACT", n_fc)

            @block.sync
            def _(sp):
                sp = W(sp)
                for fc in range(n_fc):
                    sp.wait_ge(sem["ACT"], act1 + fc + 1)
                    sp.dma_start(out=modp_dr[:, fc * 512:(fc + 1) * 512],
                                 in_=t_mp[fc % 2][:]).then_inc(sem["E%d" % (fc % 2)], 16)
                    C.inc("E%d" % (fc % 2))
                sp.wait_ge(sem["E0"], 16 * C.get("E0"))
                sp.wait_ge(sem["E1"], 16 * C.get("E1"))
            if _STOP[0] == 'm2':
                barrier()
                return nc
            @block.sync
            def _(sp):
                sp = W(sp)
                with sp.If_eq(reg_role, 0):
                    sp.dma_start(out=sh_mod[0:4, :], in_=modp_dr[:, :]).then_inc(sem["LD"], 16)
                with sp.Else():
                    sp.dma_start(out=sh_mod[4:8, :], in_=modp_dr[:, :]).then_inc(sem["LD"], 16)
                C.inc("LD")
                sp.wait_ge(sem["LD"], 16 * C.get("LD"))
                with sp.If_eq(reg_role, 0):
                    sp.dma_start(out=sh_flag[0:1, :], in_=t_flagv[:, 0:16]).then_inc(sem["LD"], 16)
                with sp.Else():
                    sp.dma_start(out=sh_flag[1:2, :], in_=t_flagv[:, 16:32]).then_inc(sem["LD"], 16)
                C.inc("LD")
                sp.wait_ge(sem["LD"], 16 * C.get("LD"))
                poll(sp, fl_mod(0))
                poll(sp, fl_mod(1))
            barrier()
            if _STOP[0] == 'm3':
                return nc

        with ExitStack() as s1:
            t_ma = s1.enter_context(nc.sbuf_tensor("t_ma", [4, 6 * D], F32))
            t_mb = s1.enter_context(nc.sbuf_tensor("t_mb", [4, 6 * D], F32))
            t_graw = s1.enter_context(nc.sbuf_tensor("t_graw", [128, 96], F32))
            t_gtmp = s1.enter_context(nc.sbuf_tensor("t_gtmp", [128, 32], F32))

            @block.sync
            def _(sp):
                sp = W(sp)
                sp.dma_start(out=t_ma[:], in_=sh_mod[0:4, :]).then_inc(sem["LD"], 16); C.inc("LD")
                sp.dma_start(out=t_mb[:], in_=sh_mod[4:8, :]).then_inc(sem["LD"], 16); C.inc("LD")

            @block.vector
            def _(v):
                v = W(v)
                v.wait_ge(sem["LD"], 16 * C.get("LD"))
                v.tensor_tensor(out=t_ma[:], in0=t_ma[:], in1=t_mb[:], op=ALU.add).then_inc(sem["DVE"], 1)
            C.inc("DVE")

            @block.sync
            def _(sp):
                sp = W(sp)
                sp.wait_ge(sem["DVE"], C.get("DVE"))
                for bb in range(4):
                    with sp.If_eq(reg_bid, bb):
                        sp.dma_start(out=modrow_dr[:, :], in_=t_ma[bb:bb + 1, :]).then_inc(sem["LD"], 16)
                C.inc("LD")
                sp.wait_ge(sem["LD"], 16 * C.get("LD"))
                for g in range(6):
                    for dt in range(DT):
                        src = modrow_dr[:, g * D + dt * 128:g * D + (dt + 1) * 128]
                        sp.dma_start(out=t_graw[:, g * 16 + dt:g * 16 + dt + 1],
                                     in_=src.rearrange("a b -> b a")).then_inc(sem["LD"], 16)
                        C.inc("LD")

            act2 = C.get("ACT")
            @block.scalar
            def _(a):
                a = W(a)
                a.wait_ge(sem["LD"], 16 * C.get("LD"))
                a.activation(out=t_gtmp[:, 0:16], in_=t_graw[:, 32:48], func=AF.Abs)
                a.activation(out=t_gtmp[:, 16:32], in_=t_graw[:, 80:96], func=AF.Abs)
                a.activation(out=t_graw[:, 32:48], in_=t_graw[:, 32:48], func=AF.Sign)
                a.activation(out=t_graw[:, 80:96], in_=t_graw[:, 80:96], func=AF.Sign).then_inc(sem["ACT"], 1)
            C.inc("ACT")

            @block.vector
            def _(v):
                v = W(v)
                v.wait_ge(sem["ACT"], C.get("ACT"))
                v.tensor_copy(out=t_gain[:, 0:16], in_=t_graw[:, 0:16])
                v.tensor_scalar(out=t_gain[:, 16:32], in0=t_graw[:, 16:32], scalar1=1.0, scalar2=None, op0=ALU.add)
                v.tensor_copy(out=t_gain[:, 48:64], in_=t_graw[:, 48:64])
                v.tensor_scalar(out=t_gain[:, 64:80], in0=t_graw[:, 64:80], scalar1=1.0, scalar2=None, op0=ALU.add)
                v.tensor_scalar(out=t_gtmp[:, 0:16], in0=t_gtmp[:, 0:16], scalar1=0.25, scalar2=None, op0=ALU.max)
                v.tensor_scalar(out=t_gtmp[:, 16:32], in0=t_gtmp[:, 16:32], scalar1=0.25, scalar2=None, op0=ALU.max)
                v.sem_inc(sem["DVE"], 1)
                v.wait_ge(sem["DVE"], C.get("DVE") + 1)
                v.tensor_tensor(out=t_gain[:, 32:48], in0=t_gtmp[:, 0:16], in1=t_graw[:, 32:48], op=ALU.mult)
                v.tensor_tensor(out=t_gain[:, 80:96], in0=t_gtmp[:, 16:32], in1=t_graw[:, 80:96], op=ALU.mult)
                v.sem_inc(sem["DVE"], 1)
            C.inc("DVE", 2)
            barrier()
            if _STOP[0] == 'mod':
                return nc

        for c in range(NCH):
            with ExitStack() as scc:
                xT = scc.enter_context(nc.sbuf_tensor(f"xT_{c}", [128, ET * Lc], F32))
                with ExitStack() as sph1:
                    rnT = sph1.enter_context(nc.sbuf_tensor(f"rnT_{c}", [128, ET * Lc], BF16))

                    with ExitStack() as sa:
                        bufH = sa.enter_context(nc.sbuf_tensor(f"bufH_{c}", [128, DT * Lc], BF16))
                        rR = sa.enter_context(nc.sbuf_tensor(f"rR_{c}", [128, 2 * Lc], F32))
                        b16 = sa.enter_context(nc.sbuf_tensor(f"b16_{c}", [128, 2 * Lc], BF16))
                        sqb = sa.enter_context(nc.sbuf_tensor(f"sqb_{c}", [128, 2 * Lc], BF16))
                        mB = sa.enter_context(nc.sbuf_tensor(f"mB_{c}", [128, Lc], F32))
                        rB = sa.enter_context(nc.sbuf_tensor(f"rB_{c}", [128, Lc], F32))
                        r2B = sa.enter_context(nc.sbuf_tensor(f"r2B_{c}", [128, Lc], F32))
                        tv = sa.enter_context(nc.sbuf_tensor(f"tv_{c}", [128, Lc], F32))

                        r0b, r1b = C.get("R0"), C.get("R1")
                        g0b, g1b = C.get("GB0"), C.get("GB1")
                        rcnt = lambda dt: 16 * ((r0b if dt % 2 == 0 else r1b) + dt // 2 + 1)
                        gcnt = lambda dt: 16 * ((g0b if dt % 2 == 0 else g1b) + dt // 2 + 1)
                        act0 = C.get("ACT")
                        pe0 = C.get("PE")
                        @block.sync
                        def _(sp):
                            sp = W(sp)
                            for dt in range(DT):
                                if dt >= 2:
                                    sp.wait_ge(sem["PE"], pe0 + dt - 1)
                                sp.dma_start(out=rR[:, (dt % 2) * Lc:(dt % 2 + 1) * Lc],
                                             in_=residT[dt * 128:(dt + 1) * 128, c * Lc:(c + 1) * Lc]
                                             ).then_inc(sem["R%d" % (dt % 2)], 16)
                                C.inc("R%d" % (dt % 2))

                        @block.gpsimd
                        def _(g):
                            g = W(g)
                            for dt in range(DT):
                                g.wait_ge(sem["R%d" % (dt % 2)], rcnt(dt))
                                if dt >= 2:
                                    g.wait_ge(sem["PE"], pe0 + dt - 1)
                                g.dma_start(out=b16[:, (dt % 2) * Lc:(dt % 2 + 1) * Lc],
                                            in_=rR[:, (dt % 2) * Lc:(dt % 2 + 1) * Lc]).then_inc(sem["GB%d" % (dt % 2)], 16)
                                C.inc("GB%d" % (dt % 2))

                        @block.scalar
                        def _(a):
                            a = W(a)
                            for dt in range(DT):
                                a.wait_ge(sem["R%d" % (dt % 2)], rcnt(dt))
                                if dt >= 2:
                                    a.wait_ge(sem["PE"], pe0 + dt - 1)
                                a.activation(out=sqb[:, (dt % 2) * Lc:(dt % 2 + 1) * Lc],
                                             in_=rR[:, (dt % 2) * Lc:(dt % 2 + 1) * Lc],
                                             func=AF.Square).then_inc(sem["ACT"], 1)
                                C.inc("ACT")

                        @block.tensor
                        def _(t):
                            t = W(t)
                            for dt in range(DT):
                                t.wait_ge(sem["GB%d" % (dt % 2)], gcnt(dt))
                                t.wait_ge(sem["ACT"], act0 + dt + 1)
                                for fc in range(2):
                                    sl = slice((dt % 2) * Lc + fc * 512, (dt % 2) * Lc + (fc + 1) * 512)
                                    t.matmul(out=pm[4 + fc][0:1, 0:512], lhsT=t_ones[:],
                                             rhs=b16[:, sl], start=(dt == 0), stop=(dt == DT - 1))
                                    t.matmul(out=pm[6 + fc][0:1, 0:512], lhsT=t_ones[:],
                                             rhs=sqb[:, sl], start=(dt == 0), stop=(dt == DT - 1))
                                t.sem_inc(sem["PE"], 1)
                        C.inc("PE", DT)

                        @block.vector
                        def _(v):
                            v = W(v)
                            v.wait_ge(sem["PE"], C.get("PE"))
                            for fc in range(2):
                                v.tensor_scalar(out=t_rows[0][:, fc * 512:(fc + 1) * 512],
                                                in0=pm[4 + fc][0:1, 0:512], scalar1=1.0 / D, scalar2=None, op0=ALU.mult)
                                v.tensor_scalar(out=t_rows[1][:, fc * 512:(fc + 1) * 512],
                                                in0=pm[6 + fc][0:1, 0:512], scalar1=1.0 / D, scalar2=None, op0=ALU.mult)
                            v.tensor_tensor(out=t_rows[2][:], in0=t_rows[0][:], in1=t_rows[0][:], op=ALU.mult)
                            v.tensor_tensor(out=t_rows[1][:], in0=t_rows[1][:], in1=t_rows[2][:], op=ALU.subtract)
                            v.sem_inc(sem["DVE"], 1)
                        C.inc("DVE")

                        @block.scalar
                        def _(a):
                            a = W(a)
                            a.wait_ge(sem["DVE"], C.get("DVE"))
                            a.activation(out=t_rows[1][:], in_=t_rows[1][:], func=AF.Sqrt,
                                         bias=t_eps1[0:1, 0:1]).then_inc(sem["ACT"], 1)
                        C.inc("ACT")

                        @block.vector
                        def _(v):
                            v = W(v)
                            v.wait_ge(sem["ACT"], C.get("ACT"))
                            v.reciprocal(out=t_rows[1][:], in_=t_rows[1][:]).then_inc(sem["DVE"], 1)
                        C.inc("DVE")

                        @block.sync
                        def _(sp):
                            sp = W(sp)
                            sp.wait_ge(sem["DVE"], C.get("DVE"))
                            sp.dma_start(out=rows_bc[0:1, :], in_=t_rows[0][:]).then_inc(sem["LD"], 16); C.inc("LD")
                            sp.dma_start(out=rows_bc[1:2, :], in_=t_rows[1][:]).then_inc(sem["LD"], 16); C.inc("LD")
                            sp.wait_ge(sem["LD"], 16 * C.get("LD"))
                            sp.dma_start(out=mB[:], in_=rows_bc[0:1, :].broadcast_to([128, Lc])).then_inc(sem["LD"], 16); C.inc("LD")
                            sp.dma_start(out=rB[:], in_=rows_bc[1:2, :].broadcast_to([128, Lc])).then_inc(sem["LD"], 16); C.inc("LD")

                        r0c, r1c = C.get("R0"), C.get("R1")
                        rcnt2 = lambda dt: 16 * ((r0c if dt % 2 == 0 else r1c) + dt // 2 + 1)
                        dve1 = C.get("DVE")
                        act3 = C.get("ACT")
                        pe1 = C.get("PE")
                        @block.sync
                        def _(sp):
                            sp = W(sp)
                            for dt in range(DT):
                                if dt >= 2:
                                    sp.wait_ge(sem["DVE"], dve1 + dt - 1)
                                sp.dma_start(out=rR[:, (dt % 2) * Lc:(dt % 2 + 1) * Lc],
                                             in_=residT[dt * 128:(dt + 1) * 128, c * Lc:(c + 1) * Lc]
                                             ).then_inc(sem["R%d" % (dt % 2)], 16)
                                C.inc("R%d" % (dt % 2))

                        @block.vector
                        def _(v):
                            v = W(v)
                            for dt in range(DT):
                                v.wait_ge(sem["R%d" % (dt % 2)], rcnt2(dt))
                                v.tensor_tensor(out=tv[:], in0=rR[:, (dt % 2) * Lc:(dt % 2 + 1) * Lc],
                                                in1=mB[:], op=ALU.subtract)
                                v.tensor_tensor(out=tv[:], in0=tv[:], in1=rB[:], op=ALU.mult)
                                v.tensor_scalar(out=bufH[:, dt * Lc:(dt + 1) * Lc], in0=tv[:],
                                                scalar1=GCA(dt), scalar2=GSA(dt), op0=ALU.mult, op1=ALU.add)
                                v.sem_inc(sem["DVE"], 1)
                        C.inc("DVE", DT)

                        @block.gpsimd
                        def _(g):
                            g = W(g)
                            for dt in range(DT):
                                g.wait_ge(sem["DVE"], dve1 + dt + 1)
                                g.dma_start(out=spill_h[c * D + dt * 128:c * D + (dt + 1) * 128, :],
                                            in_=bufH[:, dt * Lc:(dt + 1) * Lc]).then_inc(sem["GD"], 16)
                                C.inc("GD")

                        @block.scalar
                        def _(a):
                            a = W(a)
                            for dt in range(DT):
                                a.wait_ge(sem["DVE"], dve1 + dt + 1)
                                if dt >= 2:
                                    a.wait_ge(sem["PE"], pe1 + dt - 1)
                                a.activation(out=sqb[:, (dt % 2) * Lc:(dt % 2 + 1) * Lc],
                                             in_=bufH[:, dt * Lc:(dt + 1) * Lc],
                                             func=AF.Square).then_inc(sem["ACT"], 1)
                                C.inc("ACT")

                        @block.tensor
                        def _(t):
                            t = W(t)
                            for dt in range(DT):
                                t.wait_ge(sem["ACT"], act3 + dt + 1)
                                for fc in range(2):
                                    sl = slice((dt % 2) * Lc + fc * 512, (dt % 2) * Lc + (fc + 1) * 512)
                                    t.matmul(out=pm[4 + fc][0:1, 0:512], lhsT=t_ones[:],
                                             rhs=sqb[:, sl], start=(dt == 0), stop=(dt == DT - 1))
                                t.sem_inc(sem["PE"], 1)
                        C.inc("PE", DT)

                        @block.vector
                        def _(v):
                            v = W(v)
                            v.wait_ge(sem["PE"], C.get("PE"))
                            for fc in range(2):
                                v.tensor_scalar(out=t_rows[0][:, fc * 512:(fc + 1) * 512],
                                                in0=pm[4 + fc][0:1, 0:512], scalar1=1.0 / D, scalar2=None, op0=ALU.mult)
                            v.sem_inc(sem["DVE"], 1)
                        C.inc("DVE")

                        @block.scalar
                        def _(a):
                            a = W(a)
                            a.wait_ge(sem["DVE"], C.get("DVE"))
                            a.activation(out=t_rows[0][:], in_=t_rows[0][:], func=AF.Sqrt,
                                         bias=t_eps1[0:1, 0:1]).then_inc(sem["ACT"], 1)
                        C.inc("ACT")

                        @block.vector
                        def _(v):
                            v = W(v)
                            v.wait_ge(sem["ACT"], C.get("ACT"))
                            v.reciprocal(out=t_rows[0][:], in_=t_rows[0][:]).then_inc(sem["DVE"], 1)
                        C.inc("DVE")

                        @block.sync
                        def _(sp):
                            sp = W(sp)
                            sp.wait_ge(sem["DVE"], C.get("DVE"))
                            sp.dma_start(out=rows_bc[2:3, :], in_=t_rows[0][:]).then_inc(sem["LD"], 16); C.inc("LD")
                            sp.wait_ge(sem["LD"], 16 * C.get("LD"))
                            sp.dma_start(out=r2B[:], in_=rows_bc[2:3, :].broadcast_to([128, Lc])).then_inc(sem["LD"], 16); C.inc("LD")

                        @block.vector
                        def _(v):
                            v = W(v)
                            v.wait_ge(sem["LD"], 16 * C.get("LD"))
                            for dt in range(DT):
                                v.scalar_tensor_tensor(out=rnT[:, dt * Lc:(dt + 1) * Lc],
                                                       in0=bufH[:, dt * Lc:(dt + 1) * Lc],
                                                       scalar=NRMW(dt), in1=r2B[:],
                                                       op0=ALU.mult, op1=ALU.mult)
                            v.sem_inc(sem["DVE"], 1)
                        C.inc("DVE")
                        barrier()
                        if _STOP[0] == 'A':
                            return nc

                    with ExitStack() as sbk:
                        slabW = sbk.enter_context(nc.sbuf_tensor(f"slabW_{c}", [128, ET * Ec], BF16))
                        ebuf = [sbk.enter_context(nc.sbuf_tensor(f"ebB{i}_{c}", [128, 512], BF16)) for i in range(4)]

                        gd1 = C.get("GD")
                        @block.gpsimd
                        def _(g):
                            g = W(g)
                            for kt in range(DT):
                                g.dma_start(out=slabW[:, kt * Ec:(kt + 1) * Ec],
                                            in_=skip_wT[kt * 128:(kt + 1) * 128, :]).then_inc(sem["GD"], 16)
                                C.inc("GD")

                        units = [(fc, et) for fc in range(2) for et in range(ET)]
                        pe1 = C.get("PE")
                        act4 = C.get("ACT")
                        eb0 = [C.get("E%d" % i) for i in range(4)]
                        @block.tensor
                        def _(t):
                            t = W(t)
                            t.wait_ge(sem["GD"], 16 * C.get("GD"))
                            for u, (fc, et) in enumerate(units):
                                if u >= 4:
                                    t.wait_ge(sem["ACT"], act4 + (u - 3))
                                for kt in range(DT):
                                    t.matmul(out=pm[u % 4][:, 0:512],
                                             lhsT=slabW[:, kt * Ec + et * 128:kt * Ec + (et + 1) * 128],
                                             rhs=rnT[:, kt * Lc + fc * 512:kt * Lc + (fc + 1) * 512],
                                             start=(kt == 0), stop=(kt == DT - 1))
                                t.sem_inc(sem["PE"], 1)
                        C.inc("PE", len(units))

                        @block.scalar
                        def _(a):
                            a = W(a)
                            for u, (fc, et) in enumerate(units):
                                a.wait_ge(sem["PE"], pe1 + u + 1)
                                if u >= 4:
                                    a.wait_ge(sem["E%d" % (u % 4)], 16 * (eb0[u % 4] + (u - 4) // 4 + 1))
                                a.activation(out=ebuf[u % 4][:], in_=pm[u % 4][:, 0:512],
                                             func=AF.Silu).then_inc(sem["ACT"], 1)
                                C.inc("ACT")

                        @block.sync
                        def _(sp):
                            sp = W(sp)
                            for u, (fc, et) in enumerate(units):
                                sp.wait_ge(sem["ACT"], act4 + u + 1)
                                sp.dma_start(out=spill_ss[et * 128:(et + 1) * 128, fc * 512:(fc + 1) * 512],
                                             in_=ebuf[u % 4][:]).then_inc(sem["E%d" % (u % 4)], 16)
                                C.inc("E%d" % (u % 4))
                        barrier()
                        if _STOP[0] == 'B':
                            return nc

                    with ExitStack() as sck:
                        slabW = sck.enter_context(nc.sbuf_tensor(f"slabWc_{c}", [128, ET * Ec], BF16))
                        xin = [sck.enter_context(nc.sbuf_tensor(f"xin{i}_{c}", [128, Lc + 3], F32)) for i in range(2)]
                        cv1 = sck.enter_context(nc.sbuf_tensor(f"cv1_{c}", [128, Lc], F32))
                        cv2 = sck.enter_context(nc.sbuf_tensor(f"cv2_{c}", [128, Lc], F32))

                        @block.gpsimd
                        def _(g):
                            g = W(g)
                            for kt in range(DT):
                                g.dma_start(out=slabW[:, kt * Ec:(kt + 1) * Ec],
                                            in_=in_wT[kt * 128:(kt + 1) * 128, :]).then_inc(sem["GD"], 16)
                                C.inc("GD")

                        pe2 = C.get("PE")
                        act5 = C.get("ACT")
                        dve2 = C.get("DVE")
                        @block.tensor
                        def _(t):
                            t = W(t)
                            t.wait_ge(sem["GD"], 16 * C.get("GD"))
                            for et in range(ET):
                                for fc in range(2):
                                    u = et * 2 + fc
                                    if u >= 4:
                                        up = u - 4
                                        t.wait_ge(sem["ACT"], act5 + 4 * (up // 2) + (up % 2) + 1)
                                    for kt in range(DT):
                                        t.matmul(out=pm[u % 4][:, 0:512],
                                                 lhsT=slabW[:, kt * Ec + et * 128:kt * Ec + (et + 1) * 128],
                                                 rhs=rnT[:, kt * Lc + fc * 512:kt * Lc + (fc + 1) * 512],
                                                 start=(kt == 0), stop=(kt == DT - 1))
                                    t.sem_inc(sem["PE"], 1)
                        C.inc("PE", ET * 2)

                        @block.scalar
                        def _(a):
                            a = W(a)
                            for et in range(ET):
                                sl = et % 2
                                for fc in range(2):
                                    u = et * 2 + fc
                                    a.wait_ge(sem["PE"], pe2 + u + 1)
                                    if et >= 2:
                                        a.wait_ge(sem["DVE"], dve2 + 3 * (et - 1))
                                    a.activation(out=xin[sl][:, 3 + fc * 512:3 + (fc + 1) * 512],
                                                 in_=pm[u % 4][:, 0:512], func=AF.Copy).then_inc(sem["ACT"], 1)
                                    C.inc("ACT")
                                a.wait_ge(sem["DVE"], dve2 + 3 * et + 1)
                                a.activation(out=cv1[:], in_=xin[sl][:, 0:Lc], func=AF.Copy,
                                             scale=t_convw[:, et * 4:et * 4 + 1]).then_inc(sem["ACT"], 1)
                                C.inc("ACT")
                                a.wait_ge(sem["DVE"], dve2 + 3 * et + 3)
                                a.activation(out=xT[:, et * Lc:(et + 1) * Lc], in_=cv1[:],
                                             func=AF.Silu, bias=CONVB(et)).then_inc(sem["ACT"], 1)
                                C.inc("ACT")

                        @block.vector
                        def _(v):
                            v = W(v)
                            for et in range(ET):
                                sl = et % 2
                                v.wait_ge(sem["ACT"], act5 + 4 * et + 2)
                                if c == 0:
                                    v.memset(xin[sl][:, 0:3], 0.0)
                                else:
                                    v.tensor_copy(out=xin[sl][:, 0:3], in_=t_tail[:, et * 4:et * 4 + 3])
                                v.sem_inc(sem["DVE"], 1)
                                v.wait_ge(sem["ACT"], act5 + 4 * et + 3)
                                v.scalar_tensor_tensor(out=cv2[:], in0=xin[sl][:, 1:1 + Lc],
                                                       scalar=t_convw[:, et * 4 + 1:et * 4 + 2], in1=cv1[:],
                                                       op0=ALU.mult, op1=ALU.add)
                                v.scalar_tensor_tensor(out=cv1[:], in0=xin[sl][:, 2:2 + Lc],
                                                       scalar=t_convw[:, et * 4 + 2:et * 4 + 3], in1=cv2[:],
                                                       op0=ALU.mult, op1=ALU.add)
                                v.sem_inc(sem["DVE"], 1)
                                v.scalar_tensor_tensor(out=cv1[:], in0=xin[sl][:, 3:3 + Lc],
                                                       scalar=t_convw[:, et * 4 + 3:et * 4 + 4], in1=cv1[:],
                                                       op0=ALU.mult, op1=ALU.add)
                                if c == 0:
                                    v.tensor_copy(out=t_tail[:, et * 4:et * 4 + 3], in_=xin[sl][:, Lc:Lc + 3])
                                v.sem_inc(sem["DVE"], 1)
                        C.inc("DVE", 3 * ET)
                        barrier()
                        if _STOP[0] == 'C':
                            return nc

                with ExitStack() as sdk:
                    t_zTb = sdk.enter_context(nc.sbuf_tensor(f"t_zTb_{c}", [128, Lc], BF16))
                    sd1 = sdk.enter_context(ExitStack())
                    xTb = sd1.enter_context(nc.sbuf_tensor(f"xTb_{c}", [128, ET * Lc], BF16))
                    t_w1s = sd1.enter_context(nc.sbuf_tensor(f"t_w1s_{c}", [128, ET * 128], BF16))
                    t_wbs = sd1.enter_context(nc.sbuf_tensor(f"t_wbs_{c}", [128, ET * N], BF16))
                    t_wcs = sd1.enter_context(nc.sbuf_tensor(f"t_wcs_{c}", [128, ET * N], BF16))
                    t_zT = sd1.enter_context(nc.sbuf_tensor(f"t_zT_{c}", [128, Lc], F32))
                    t_bm = sd1.enter_context(nc.sbuf_tensor(f"t_bm_{c}", [16, Lc], F32))
                    t_cm = sd1.enter_context(nc.sbuf_tensor(f"t_cm_{c}", [16, Lc], F32))
                    t_bm16 = sd1.enter_context(nc.sbuf_tensor(f"t_bm16_{c}", [16, Lc], BF16))
                    t_cm16 = sd1.enter_context(nc.sbuf_tensor(f"t_cm16_{c}", [16, Lc], BF16))
                    t_zp = sd1.enter_context(nc.sbuf_tensor(f"t_zp_{c}", [128, Lc], F32))
                    t_bp = sd1.enter_context(nc.sbuf_tensor(f"t_bp_{c}", [16, Lc], F32))
                    t_cp = sd1.enter_context(nc.sbuf_tensor(f"t_cp_{c}", [16, Lc], F32))

                    @block.gpsimd
                    def _(g):
                        g = W(g)
                        for et in range(ET):
                            g.dma_start(out=xTb[:, et * Lc:(et + 1) * Lc],
                                        in_=xT[:, et * Lc:(et + 1) * Lc]).then_inc(sem["GD"], 16)
                            C.inc("GD")
                        for kt in range(ET):
                            g.dma_start(out=t_w1s[:, kt * 128:(kt + 1) * 128],
                                        in_=wd1T_in[kt * 128:(kt + 1) * 128, :]).then_inc(sem["GD"], 16)
                            C.inc("GD")
                            g.dma_start(out=t_wbs[:, kt * N:(kt + 1) * N],
                                        in_=wBT_in[kt * 128:(kt + 1) * 128, :]).then_inc(sem["GD"], 16)
                            C.inc("GD")
                            g.dma_start(out=t_wcs[:, kt * N:(kt + 1) * N],
                                        in_=wCT_in[kt * 128:(kt + 1) * 128, :]).then_inc(sem["GD"], 16)
                            C.inc("GD")

                    pe3 = C.get("PE")
                    dve3 = C.get("DVE")
                    @block.tensor
                    def _(t):
                        t = W(t)
                        t.wait_ge(sem["GD"], 16 * C.get("GD"))
                        for fc in range(2):
                            for kt in range(ET):
                                t.matmul(out=pm[4 + fc][:, 0:512],
                                         lhsT=t_w1s[:, kt * 128:(kt + 1) * 128],
                                         rhs=xTb[:, kt * Lc + fc * 512:kt * Lc + (fc + 1) * 512],
                                         start=(kt == 0), stop=(kt == ET - 1))
                            t.sem_inc(sem["PE"], 1)
                        for fc in range(2):
                            if fc == 1:
                                t.wait_ge(sem["DVE"], dve3 + 3)
                            for kt in range(ET):
                                t.matmul(out=pm[6][0:16, 0:512],
                                         lhsT=t_wbs[:, kt * N:(kt + 1) * N],
                                         rhs=xTb[:, kt * Lc + fc * 512:kt * Lc + (fc + 1) * 512],
                                         start=(kt == 0), stop=(kt == ET - 1))
                                t.matmul(out=pm[7][0:16, 0:512],
                                         lhsT=t_wcs[:, kt * N:(kt + 1) * N],
                                         rhs=xTb[:, kt * Lc + fc * 512:kt * Lc + (fc + 1) * 512],
                                         start=(kt == 0), stop=(kt == ET - 1))
                            t.sem_inc(sem["PE"], 1)
                    C.inc("PE", 4)

                    @block.vector
                    def _(v):
                        v = W(v)
                        v.wait_ge(sem["PE"], pe3 + 1)
                        v.tensor_copy(out=t_zT[:, 0:512], in_=pm[4][:, 0:512])
                        v.sem_inc(sem["DVE"], 1)
                        v.wait_ge(sem["PE"], pe3 + 2)
                        v.tensor_copy(out=t_zT[:, 512:1024], in_=pm[5][:, 0:512])
                        v.sem_inc(sem["DVE"], 1)
                        v.wait_ge(sem["PE"], pe3 + 3)
                        v.tensor_copy(out=t_bm[:, 0:512], in_=pm[6][0:16, 0:512])
                        v.tensor_copy(out=t_cm[:, 0:512], in_=pm[7][0:16, 0:512])
                        v.sem_inc(sem["DVE"], 1)
                        v.wait_ge(sem["PE"], pe3 + 4)
                        v.tensor_copy(out=t_bm[:, 512:1024], in_=pm[6][0:16, 0:512])
                        v.tensor_copy(out=t_cm[:, 512:1024], in_=pm[7][0:16, 0:512])
                        v.sem_inc(sem["DVE"], 1)
                    C.inc("DVE", 4)

                    @block.sync
                    def _(sp):
                        sp = W(sp)
                        sp.wait_ge(sem["DVE"], C.get("DVE"))
                        base0 = (0 * 2 + c) * 160
                        base1 = (1 * 2 + c) * 160
                        with sp.If_eq(reg_role, 0):
                            sp.dma_start(out=sh_zbc[base0:base0 + 128, :], in_=t_zT[:]).then_inc(sem["LD"], 16)
                            sp.dma_start(out=sh_zbc[base0 + 128:base0 + 144, :], in_=t_bm[:]).then_inc(sem["LD"], 16)
                            sp.dma_start(out=sh_zbc[base0 + 144:base0 + 160, :], in_=t_cm[:]).then_inc(sem["LD"], 16)
                        with sp.Else():
                            sp.dma_start(out=sh_zbc[base1:base1 + 128, :], in_=t_zT[:]).then_inc(sem["LD"], 16)
                            sp.dma_start(out=sh_zbc[base1 + 128:base1 + 144, :], in_=t_bm[:]).then_inc(sem["LD"], 16)
                            sp.dma_start(out=sh_zbc[base1 + 144:base1 + 160, :], in_=t_cm[:]).then_inc(sem["LD"], 16)
                        C.inc("LD", 3)
                        sp.wait_ge(sem["LD"], 16 * C.get("LD"))
                        f0, f1 = fl_zbc(0, c), fl_zbc(1, c)
                        with sp.If_eq(reg_role, 0):
                            sp.dma_start(out=sh_flag[f0:f0 + 1, :], in_=t_flagv[:, 16 * f0:16 * (f0 + 1)]).then_inc(sem["LD"], 16)
                        with sp.Else():
                            sp.dma_start(out=sh_flag[f1:f1 + 1, :], in_=t_flagv[:, 16 * f1:16 * (f1 + 1)]).then_inc(sem["LD"], 16)
                        C.inc("LD")
                        sp.wait_ge(sem["LD"], 16 * C.get("LD"))
                        poll(sp, f0)
                        poll(sp, f1)
                        with sp.If_eq(reg_role, 0):
                            sp.dma_start(out=t_zp[:], in_=sh_zbc[base1:base1 + 128, :]).then_inc(sem["LD"], 16)
                            sp.dma_start(out=t_bp[:], in_=sh_zbc[base1 + 128:base1 + 144, :]).then_inc(sem["LD"], 16)
                            sp.dma_start(out=t_cp[:], in_=sh_zbc[base1 + 144:base1 + 160, :]).then_inc(sem["LD"], 16)
                        with sp.Else():
                            sp.dma_start(out=t_zp[:], in_=sh_zbc[base0:base0 + 128, :]).then_inc(sem["LD"], 16)
                            sp.dma_start(out=t_bp[:], in_=sh_zbc[base0 + 128:base0 + 144, :]).then_inc(sem["LD"], 16)
                            sp.dma_start(out=t_cp[:], in_=sh_zbc[base0 + 144:base0 + 160, :]).then_inc(sem["LD"], 16)
                        C.inc("LD", 3)

                    @block.vector
                    def _(v):
                        v = W(v)
                        v.wait_ge(sem["LD"], 16 * C.get("LD"))
                        v.tensor_tensor(out=t_zT[:], in0=t_zT[:], in1=t_zp[:], op=ALU.add)
                        v.tensor_tensor(out=t_bm[:], in0=t_bm[:], in1=t_bp[:], op=ALU.add)
                        v.tensor_tensor(out=t_cm[:], in0=t_cm[:], in1=t_cp[:], op=ALU.add)
                        v.tensor_scalar(out=t_zTb[:], in0=t_zT[:], scalar1=1.0, scalar2=None, op0=ALU.mult)
                        v.tensor_scalar(out=t_bm16[:], in0=t_bm[:], scalar1=1.0, scalar2=None, op0=ALU.mult)
                        v.tensor_scalar(out=t_cm16[:], in0=t_cm[:], scalar1=1.0, scalar2=None, op0=ALU.mult)
                        v.sem_inc(sem["DVE"], 1)
                    C.inc("DVE")

                    @block.sync
                    def _(sp):
                        sp = W(sp)
                        sp.wait_ge(sem["DVE"], C.get("DVE"))
                        sp.dma_start(out=bmcm_dr[0:16, :], in_=t_bm16[:]).then_inc(sem["LD"], 16)
                        sp.dma_start(out=bmcm_dr[16:32, :], in_=t_cm16[:]).then_inc(sem["LD"], 16)
                        C.inc("LD", 2)
                    barrier()
                    sd1.close()
                    if _STOP[0] == 'D':
                        return nc

                    t_w2s = sdk.enter_context(nc.sbuf_tensor(f"t_w2s_{c}", [128, ET * 128], BF16))
                    dT = sdk.enter_context(nc.sbuf_tensor(f"dT_{c}", [128, EB * Lc], F32))
                    dx16 = sdk.enter_context(nc.sbuf_tensor(f"dx16_{c}", [128, EB * Lc], BF16))
                    yblk = sdk.enter_context(nc.sbuf_tensor(f"yblk_{c}", [128, EB * Lc], F32))
                    abuf = [sdk.enter_context(nc.sbuf_tensor(f"abuf{i}_{c}", [128, Lc], F32)) for i in range(2)]
                    bbuf = [sdk.enter_context(nc.sbuf_tensor(f"bbuf{i}_{c}", [128, Lc], BF16)) for i in range(2)]
                    hbuf = [sdk.enter_context(nc.sbuf_tensor(f"hbuf{i}_{c}", [128, Lc], BF16)) for i in range(2)]
                    tmpb = [sdk.enter_context(nc.sbuf_tensor(f"tmpb{i}_{c}", [128, Lc], BF16)) for i in range(2)]
                    bcB = [sdk.enter_context(nc.sbuf_tensor(f"bcB{i}_{c}", [128, Lc], BF16)) for i in range(2)]
                    bcC = [sdk.enter_context(nc.sbuf_tensor(f"bcC{i}_{c}", [128, Lc], BF16)) for i in range(2)]
                    relub = sdk.enter_context(nc.sbuf_tensor(f"relub_{c}", [128, 512], F32))
                    actb = sdk.enter_context(nc.sbuf_tensor(f"actb_{c}", [128, 512], F32))
                    ssk = sdk.enter_context(nc.sbuf_tensor(f"ssk_{c}", [128, Lc], BF16))
                    ywt = sdk.enter_context(nc.sbuf_tensor(f"ywt_{c}", [128, Lc], F32))
                    ypt = sdk.enter_context(nc.sbuf_tensor(f"ypt_{c}", [128, Lc], BF16))

                    @block.gpsimd
                    def _(g):
                        g = W(g)
                        for kt in range(ET):
                            g.dma_start(out=t_w2s[:, kt * 128:(kt + 1) * 128],
                                        in_=wd2T_in[:, kt * 128:(kt + 1) * 128]).then_inc(sem["GD"], 16)
                            C.inc("GD")

                    for blk in range(NBLK):
                        pe4 = C.get("PE")
                        act6 = C.get("ACT")
                        dve4 = C.get("DVE")
                        drelu = lambda u: 5 * (u // 2) + 2 * (u % 2) + 1
                        dadd = lambda u: 5 * (u // 2) + 2 * (u % 2) + 2
                        @block.tensor
                        def _(t):
                            t = W(t)
                            t.wait_ge(sem["GD"], 16 * C.get("GD"))
                            t.wait_ge(sem["DVE"], dve4)
                            for el in range(EB):
                                et = blk * EB + el
                                for fc in range(2):
                                    u = el * 2 + fc
                                    if u >= 2:
                                        t.wait_ge(sem["DVE"], dve4 + drelu(u - 2))
                                    t.matmul(out=pm[4 + u % 2][:, 0:512],
                                             lhsT=t_w2s[:, et * 128:(et + 1) * 128],
                                             rhs=t_zTb[:, fc * 512:(fc + 1) * 512],
                                             start=True, stop=True)
                                    t.sem_inc(sem["PE"], 1)
                        C.inc("PE", EB * 2)

                        @block.scalar
                        def _(a):
                            a = W(a)
                            for el in range(EB):
                                et = blk * EB + el
                                for fc in range(2):
                                    u = el * 2 + fc
                                    a.wait_ge(sem["PE"], pe4 + u + 1)
                                    if u >= 1:
                                        a.wait_ge(sem["DVE"], dve4 + dadd(u - 1))
                                    a.activation(out=actb[:], in_=pm[4 + u % 2][:, 0:512],
                                                 func=AF.Abs, bias=WD2B(et)).then_inc(sem["ACT"], 1)
                                    a.activation(out=actb[:], in_=actb[:], func=AF.Exp,
                                                 scale=-1.0).then_inc(sem["ACT"], 1)
                                    a.activation(out=actb[:], in_=actb[:], func=AF.Ln,
                                                 bias=t_one1[:]).then_inc(sem["ACT"], 1)
                                    C.inc("ACT", 3)

                        @block.vector
                        def _(v):
                            v = W(v)
                            for el in range(EB):
                                et = blk * EB + el
                                for fc in range(2):
                                    u = el * 2 + fc
                                    v.wait_ge(sem["ACT"], act6 + 3 * u + 3)
                                    v.tensor_scalar(out=relub[:], in0=pm[4 + u % 2][:, 0:512],
                                                    scalar1=WD2B(et), scalar2=0.0, op0=ALU.add, op1=ALU.max)
                                    v.sem_inc(sem["DVE"], 1)
                                    v.tensor_tensor(out=dT[:, el * Lc + fc * 512:el * Lc + (fc + 1) * 512],
                                                    in0=relub[:], in1=actb[:], op=ALU.add)
                                    v.sem_inc(sem["DVE"], 1)
                                v.tensor_tensor(out=dx16[:, el * Lc:(el + 1) * Lc],
                                                in0=dT[:, el * Lc:(el + 1) * Lc],
                                                in1=xT[:, (blk * EB + el) * Lc:(blk * EB + el + 1) * Lc],
                                                op=ALU.mult)
                                v.sem_inc(sem["DVE"], 1)
                        C.inc("DVE", EB * 5)

                        dve5 = C.get("DVE")
                        act7 = C.get("ACT")
                        pl0 = C.get("PL")
                        bc0 = [C.get("BC0"), C.get("BC1")]
                        bccnt = lambda n: 16 * 2 * (bc0[n % 2] // 2 + n // 2 + 1)
                        @block.sync
                        def _(sp):
                            sp = W(sp)
                            for n in range(N):
                                if n >= 2:
                                    sp.wait_ge(sem["DVE"], dve5 + (n - 1) * EB * 3)
                                sp.dma_start(out=bcB[n % 2][:],
                                             in_=bmcm_dr[n:n + 1, :].broadcast_to([128, Lc])).then_inc(sem["BC%d" % (n % 2)], 16)
                                sp.dma_start(out=bcC[n % 2][:],
                                             in_=bmcm_dr[16 + n:17 + n, :].broadcast_to([128, Lc])).then_inc(sem["BC%d" % (n % 2)], 16)
                                C.inc("BC%d" % (n % 2), 2)

                        @block.scalar
                        def _(a):
                            a = W(a)
                            a.wait_ge(sem["DVE"], dve5)
                            for n in range(N):
                                for el in range(EB):
                                    u = n * EB + el
                                    et = blk * EB + el
                                    if u >= 2:
                                        a.wait_ge(sem["DVE"], dve5 + 3 * (u - 1) - 1)
                                    a.activation(out=abuf[u % 2][:], in_=dT[:, el * Lc:(el + 1) * Lc],
                                                 func=AF.Exp, scale=t_negA[:, et * N + n:et * N + n + 1]
                                                 ).then_inc(sem["ACT"], 1)
                                    C.inc("ACT")

                        @block.vector
                        def _(v):
                            v = W(v)
                            for n in range(N):
                                for el in range(EB):
                                    u = n * EB + el
                                    et = blk * EB + el
                                    if el == 0:
                                        v.wait_ge(sem["BC%d" % (n % 2)], bccnt(n))
                                    v.tensor_tensor(out=bbuf[u % 2][:], in0=dx16[:, el * Lc:(el + 1) * Lc],
                                                    in1=bcB[n % 2][:], op=ALU.mult)
                                    v.sem_inc(sem["DVE"], 1)
                                    v.wait_ge(sem["ACT"], act7 + u + 1)
                                    init = 0.0 if c == 0 else t_state[:, et * N + n:et * N + n + 1]
                                    v.tensor_tensor_scan(out=hbuf[u % 2][:], data0=abuf[u % 2][:],
                                                         data1=bbuf[u % 2][:], initial=init,
                                                         op0=ALU.mult, op1=ALU.add)
                                    v.sem_inc(sem["DVE"], 1)
                                    if c == 0 and NCH > 1:
                                        v.wait_ge(sem["DVE"], dve5 + 3 * u + 2)
                                        v.tensor_copy(out=t_state[:, et * N + n:et * N + n + 1],
                                                      in_=hbuf[u % 2][:, Lc - 1:Lc])
                                    if u >= 2:
                                        v.wait_ge(sem["PL"], pl0 + (u - 1))
                                    v.tensor_tensor(out=tmpb[u % 2][:], in0=hbuf[u % 2][:],
                                                    in1=bcC[n % 2][:], op=ALU.mult)
                                    v.sem_inc(sem["DVE"], 1)
                        C.inc("DVE", N * EB * 3)

                        @block.gpsimd
                        def _(g):
                            g = W(g)
                            for n in range(N):
                                for el in range(EB):
                                    u = n * EB + el
                                    g.wait_ge(sem["DVE"], dve5 + 3 * u + 3)
                                    if n == 0:
                                        g.tensor_copy(out=yblk[:, el * Lc:(el + 1) * Lc], in_=tmpb[u % 2][:])
                                    else:
                                        g.tensor_tensor(out=yblk[:, el * Lc:(el + 1) * Lc],
                                                        in0=yblk[:, el * Lc:(el + 1) * Lc],
                                                        in1=tmpb[u % 2][:], op=ALU.add)
                                    g.sem_inc(sem["PL"], 1)
                        C.inc("PL", N * EB)

                        sk0 = C.get("SK")
                        yp0 = C.get("YP")
                        dve6 = C.get("DVE")
                        @block.sync
                        def _(sp):
                            sp = W(sp)
                            for el in range(EB):
                                et = blk * EB + el
                                if el >= 1:
                                    sp.wait_ge(sem["DVE"], dve6 + 2 * el)
                                sp.dma_start(out=ssk[:], in_=spill_ss[et * 128:(et + 1) * 128, :]).then_inc(sem["SK"], 16)
                                C.inc("SK")
                                sp.wait_ge(sem["DVE"], dve6 + 2 * el + 2)
                                sp.dma_start(out=spill_yp[et * 128:(et + 1) * 128, :], in_=ypt[:]).then_inc(sem["YP"], 16)
                                C.inc("YP")

                        @block.vector
                        def _(v):
                            v = W(v)
                            v.wait_ge(sem["PL"], C.get("PL"))
                            for el in range(EB):
                                et = blk * EB + el
                                v.scalar_tensor_tensor(out=ywt[:], in0=xT[:, et * Lc:(et + 1) * Lc],
                                                       scalar=WDC(et), in1=yblk[:, el * Lc:(el + 1) * Lc],
                                                       op0=ALU.mult, op1=ALU.add)
                                v.sem_inc(sem["DVE"], 1)
                                v.wait_ge(sem["SK"], 16 * (sk0 + el + 1))
                                if el >= 1:
                                    v.wait_ge(sem["YP"], 16 * (yp0 + el))
                                v.tensor_tensor(out=ypt[:], in0=ywt[:], in1=ssk[:], op=ALU.mult)
                                v.sem_inc(sem["DVE"], 1)
                        C.inc("DVE", 2 * EB)
                    barrier()
                    if _STOP[0] == 'E':
                        return nc

                with ExitStack() as sfk:
                    slabO = sfk.enter_context(nc.sbuf_tensor(f"slabO_{c}", [128, ET * D], BF16))
                    ypb = sfk.enter_context(nc.sbuf_tensor(f"ypb_{c}", [128, ET * Lc], BF16))
                    ofb = [sfk.enter_context(nc.sbuf_tensor(f"ofb{i}_{c}", [128, 512], F32)) for i in range(4)]

                    @block.gpsimd
                    def _(g):
                        g = W(g)
                        for kt in range(ET):
                            g.dma_start(out=slabO[:, kt * D:(kt + 1) * D],
                                        in_=out_wT[kt * 128:(kt + 1) * 128, :]).then_inc(sem["GD"], 16)
                            C.inc("GD")

                    ld4 = C.get("LD")
                    @block.sync
                    def _(sp):
                        sp = W(sp)
                        for kt in range(ET):
                            sp.dma_start(out=ypb[:, kt * Lc:(kt + 1) * Lc],
                                         in_=spill_yp[kt * 128:(kt + 1) * 128, :]).then_inc(sem["LD"], 16)
                            C.inc("LD")

                    units = [(fc, dt) for fc in range(2) for dt in range(DT)]
                    pe5 = C.get("PE")
                    dve7 = C.get("DVE")
                    ef0 = [C.get("E%d" % i) for i in range(4)]
                    @block.tensor
                    def _(t):
                        t = W(t)
                        t.wait_ge(sem["GD"], 16 * C.get("GD"))
                        t.wait_ge(sem["LD"], 16 * (ld4 + ET))
                        for u, (fc, dt) in enumerate(units):
                            if u >= 4:
                                t.wait_ge(sem["DVE"], dve7 + (u - 3))
                            for kt in range(ET):
                                t.matmul(out=pm[u % 4][:, 0:512],
                                         lhsT=slabO[:, kt * D + dt * 128:kt * D + (dt + 1) * 128],
                                         rhs=ypb[:, kt * Lc + fc * 512:kt * Lc + (fc + 1) * 512],
                                         start=(kt == 0), stop=(kt == ET - 1))
                            t.sem_inc(sem["PE"], 1)
                    C.inc("PE", len(units))

                    @block.vector
                    def _(v):
                        v = W(v)
                        for u, (fc, dt) in enumerate(units):
                            v.wait_ge(sem["PE"], pe5 + u + 1)
                            if u >= 4:
                                v.wait_ge(sem["E%d" % (u % 4)], 16 * (ef0[u % 4] + (u - 4) // 4 + 1))
                            v.tensor_copy(out=ofb[u % 4][:], in_=pm[u % 4][:, 0:512])
                            v.sem_inc(sem["DVE"], 1)
                    C.inc("DVE", len(units))

                    @block.sync
                    def _(sp):
                        sp = W(sp)
                        for u, (fc, dt) in enumerate(units):
                            sp.wait_ge(sem["DVE"], dve7 + u + 1)
                            r0 = (0 * 2 + c) * D + dt * 128
                            r1 = (1 * 2 + c) * D + dt * 128
                            with sp.If_eq(reg_role, 0):
                                sp.dma_start(out=sh_out[r0:r0 + 128, fc * 512:(fc + 1) * 512],
                                             in_=ofb[u % 4][:]).then_inc(sem["E%d" % (u % 4)], 16)
                            with sp.Else():
                                sp.dma_start(out=sh_out[r1:r1 + 128, fc * 512:(fc + 1) * 512],
                                             in_=ofb[u % 4][:]).then_inc(sem["E%d" % (u % 4)], 16)
                            C.inc("E%d" % (u % 4))
                        for i in range(4):
                            sp.wait_ge(sem["E%d" % i], 16 * C.get("E%d" % i))
                        f0, f1 = fl_out(0, c), fl_out(1, c)
                        with sp.If_eq(reg_role, 0):
                            sp.dma_start(out=sh_flag[f0:f0 + 1, :], in_=t_flagv[:, 16 * f0:16 * (f0 + 1)]).then_inc(sem["LD"], 16)
                        with sp.Else():
                            sp.dma_start(out=sh_flag[f1:f1 + 1, :], in_=t_flagv[:, 16 * f1:16 * (f1 + 1)]).then_inc(sem["LD"], 16)
                        C.inc("LD")
                    barrier()
                    if _STOP[0] == 'F' and c == NCH - 1:
                        return nc

        with ExitStack() as sgk:
            x1T = sgk.enter_context(nc.sbuf_tensor("x1T", [128, DT * Lc], F32))
            scons = sgk.enter_context(ExitStack())
            tA = scons.enter_context(nc.sbuf_tensor("tA", [128, Lc], F32))
            tB = scons.enter_context(nc.sbuf_tensor("tB", [128, Lc], F32))
            tR = scons.enter_context(nc.sbuf_tensor("tR", [128, Lc], F32))
            tH = scons.enter_context(nc.sbuf_tensor("tH", [128, Lc], BF16))

            @block.sync
            def _(sp):
                sp = W(sp)
                sp.wait_ge(sem["LD"], 16 * C.get("LD"))
                for f in [fl_out(0, 0), fl_out(0, 1), fl_out(1, 0), fl_out(1, 1)]:
                    poll(sp, f)

            dve8 = C.get("DVE")
            ca0, cb0, cr0, ch0 = C.get("CA"), C.get("CB"), C.get("CR"), C.get("CH")
            @block.sync
            def _(sp):
                sp = W(sp)
                for dt in range(DT):
                    if dt >= 1:
                        sp.wait_ge(sem["DVE"], dve8 + 3 * dt)
                    with sp.If_eq(reg_role, 0):
                        sp.dma_start(out=tA[:], in_=sh_out[0 * D + dt * 128:0 * D + (dt + 1) * 128, :]).then_inc(sem["CA"], 16)
                        sp.dma_start(out=tB[:], in_=sh_out[2 * D + dt * 128:2 * D + (dt + 1) * 128, :]).then_inc(sem["CB"], 16)
                        sp.dma_start(out=tR[:], in_=residT[dt * 128:(dt + 1) * 128, 0:Lc]).then_inc(sem["CR"], 16)
                        sp.dma_start(out=tH[:], in_=spill_h[dt * 128:(dt + 1) * 128, :]).then_inc(sem["CH"], 16)
                    with sp.Else():
                        sp.dma_start(out=tA[:], in_=sh_out[3 * D + dt * 128:3 * D + (dt + 1) * 128, :]).then_inc(sem["CA"], 16)
                        sp.dma_start(out=tB[:], in_=sh_out[1 * D + dt * 128:1 * D + (dt + 1) * 128, :]).then_inc(sem["CB"], 16)
                        sp.dma_start(out=tR[:], in_=residT[dt * 128:(dt + 1) * 128, Lc:2 * Lc]).then_inc(sem["CR"], 16)
                        sp.dma_start(out=tH[:], in_=spill_h[D + dt * 128:D + (dt + 1) * 128, :]).then_inc(sem["CH"], 16)
                    C.inc("CA"); C.inc("CB"); C.inc("CR"); C.inc("CH")

            @block.vector
            def _(v):
                v = W(v)
                for dt in range(DT):
                    v.wait_ge(sem["CA"], 16 * (ca0 + dt + 1))
                    v.wait_ge(sem["CB"], 16 * (cb0 + dt + 1))
                    v.wait_ge(sem["CR"], 16 * (cr0 + dt + 1))
                    v.wait_ge(sem["CH"], 16 * (ch0 + dt + 1))
                    v.tensor_tensor(out=tA[:], in0=tA[:], in1=tB[:], op=ALU.add)
                    v.sem_inc(sem["DVE"], 1)
                    v.tensor_tensor(out=tA[:], in0=tA[:], in1=tH[:], op=ALU.add)
                    v.sem_inc(sem["DVE"], 1)
                    v.scalar_tensor_tensor(out=x1T[:, dt * Lc:(dt + 1) * Lc], in0=tA[:],
                                           scalar=GGA(dt), in1=tR[:], op0=ALU.mult, op1=ALU.add)
                    v.sem_inc(sem["DVE"], 1)
            C.inc("DVE", 3 * DT)
            barrier()
            scons.close()
            if _STOP[0] == 'X':
                return nc

            with ExitStack() as sg2:
                h2 = sg2.enter_context(nc.sbuf_tensor("h2", [128, DT * Lc], BF16))
                sgLN = sg2.enter_context(ExitStack())
                b16g = sgLN.enter_context(nc.sbuf_tensor("b16g", [128, 2 * Lc], BF16))
                sqg = sgLN.enter_context(nc.sbuf_tensor("sqg", [128, 2 * Lc], BF16))
                mB2 = sgLN.enter_context(nc.sbuf_tensor("mB2", [128, Lc], F32))
                rB2 = sgLN.enter_context(nc.sbuf_tensor("rB2", [128, Lc], F32))
                tvg = sgLN.enter_context(nc.sbuf_tensor("tvg", [128, Lc], F32))

                pe6 = C.get("PE")
                gg0 = [C.get("GB0"), C.get("GB1")]
                gcnt2 = lambda dt: 16 * (gg0[dt % 2] + dt // 2 + 1)
                @block.gpsimd
                def _(g):
                    g = W(g)
                    for dt in range(DT):
                        if dt >= 2:
                            g.wait_ge(sem["PE"], pe6 + dt - 1)
                        g.dma_start(out=b16g[:, (dt % 2) * Lc:(dt % 2 + 1) * Lc],
                                    in_=x1T[:, dt * Lc:(dt + 1) * Lc]).then_inc(sem["GB%d" % (dt % 2)], 16)
                        C.inc("GB%d" % (dt % 2))

                act8 = C.get("ACT")
                @block.scalar
                def _(a):
                    a = W(a)
                    for dt in range(DT):
                        if dt >= 2:
                            a.wait_ge(sem["PE"], pe6 + dt - 1)
                        a.activation(out=sqg[:, (dt % 2) * Lc:(dt % 2 + 1) * Lc],
                                     in_=x1T[:, dt * Lc:(dt + 1) * Lc],
                                     func=AF.Square).then_inc(sem["ACT"], 1)
                        C.inc("ACT")

                @block.tensor
                def _(t):
                    t = W(t)
                    for dt in range(DT):
                        t.wait_ge(sem["GB%d" % (dt % 2)], gcnt2(dt))
                        t.wait_ge(sem["ACT"], act8 + dt + 1)
                        for fc in range(2):
                            sl = slice((dt % 2) * Lc + fc * 512, (dt % 2) * Lc + (fc + 1) * 512)
                            t.matmul(out=pm[4 + fc][0:1, 0:512], lhsT=t_ones[:],
                                     rhs=b16g[:, sl], start=(dt == 0), stop=(dt == DT - 1))
                            t.matmul(out=pm[6 + fc][0:1, 0:512], lhsT=t_ones[:],
                                     rhs=sqg[:, sl], start=(dt == 0), stop=(dt == DT - 1))
                        t.sem_inc(sem["PE"], 1)
                C.inc("PE", DT)

                @block.vector
                def _(v):
                    v = W(v)
                    v.wait_ge(sem["PE"], C.get("PE"))
                    for fc in range(2):
                        v.tensor_scalar(out=t_rows[0][:, fc * 512:(fc + 1) * 512],
                                        in0=pm[4 + fc][0:1, 0:512], scalar1=1.0 / D, scalar2=None, op0=ALU.mult)
                        v.tensor_scalar(out=t_rows[1][:, fc * 512:(fc + 1) * 512],
                                        in0=pm[6 + fc][0:1, 0:512], scalar1=1.0 / D, scalar2=None, op0=ALU.mult)
                    v.tensor_tensor(out=t_rows[2][:], in0=t_rows[0][:], in1=t_rows[0][:], op=ALU.mult)
                    v.tensor_tensor(out=t_rows[1][:], in0=t_rows[1][:], in1=t_rows[2][:], op=ALU.subtract)
                    v.sem_inc(sem["DVE"], 1)
                C.inc("DVE")

                @block.scalar
                def _(a):
                    a = W(a)
                    a.wait_ge(sem["DVE"], C.get("DVE"))
                    a.activation(out=t_rows[1][:], in_=t_rows[1][:], func=AF.Sqrt,
                                 bias=t_eps1[0:1, 0:1]).then_inc(sem["ACT"], 1)
                C.inc("ACT")

                @block.vector
                def _(v):
                    v = W(v)
                    v.wait_ge(sem["ACT"], C.get("ACT"))
                    v.reciprocal(out=t_rows[1][:], in_=t_rows[1][:]).then_inc(sem["DVE"], 1)
                C.inc("DVE")

                @block.sync
                def _(sp):
                    sp = W(sp)
                    sp.wait_ge(sem["DVE"], C.get("DVE"))
                    sp.dma_start(out=rows_bc[3:4, :], in_=t_rows[0][:]).then_inc(sem["LD"], 16); C.inc("LD")
                    sp.dma_start(out=rows_bc[4:5, :], in_=t_rows[1][:]).then_inc(sem["LD"], 16); C.inc("LD")
                    sp.wait_ge(sem["LD"], 16 * C.get("LD"))
                    sp.dma_start(out=mB2[:], in_=rows_bc[3:4, :].broadcast_to([128, Lc])).then_inc(sem["LD"], 16); C.inc("LD")
                    sp.dma_start(out=rB2[:], in_=rows_bc[4:5, :].broadcast_to([128, Lc])).then_inc(sem["LD"], 16); C.inc("LD")

                @block.vector
                def _(v):
                    v = W(v)
                    v.wait_ge(sem["LD"], 16 * C.get("LD"))
                    for dt in range(DT):
                        v.tensor_tensor(out=tvg[:], in0=x1T[:, dt * Lc:(dt + 1) * Lc], in1=mB2[:], op=ALU.subtract)
                        v.tensor_tensor(out=tvg[:], in0=tvg[:], in1=rB2[:], op=ALU.mult)
                        v.tensor_scalar(out=h2[:, dt * Lc:(dt + 1) * Lc], in0=tvg[:],
                                        scalar1=GCB(dt), scalar2=GSB(dt), op0=ALU.mult, op1=ALU.add)
                    v.sem_inc(sem["DVE"], 1)
                C.inc("DVE")
                barrier()
                sgLN.close()
                if _STOP[0] == 'G1':
                    return nc

                with ExitStack() as sg3:
                    bss = sg3.enter_context(nc.sbuf_tensor("bss", [16, D], BF16))
                    brss = sg3.enter_context(nc.sbuf_tensor("brss", [96, D], BF16))
                    s16 = sg3.enter_context(nc.sbuf_tensor("s16", [16, Lc], BF16))
                    h96 = sg3.enter_context(nc.sbuf_tensor("h96", [96, Lc], BF16))
                    sg3b = sg3.enter_context(ExitStack())
                    wrs = sg3b.enter_context(nc.sbuf_tensor("wrs", [128, ET * 6], BF16))
                    ass = sg3b.enter_context(nc.sbuf_tensor("ass", [128, ET * N], BF16))
                    arts = sg3b.enter_context(nc.sbuf_tensor("arts", [128, ET * 96], BF16))
                    lgt = sg3b.enter_context(nc.sbuf_tensor("lgt", [128, Lc], F32))
                    gtok = sg3b.enter_context(nc.sbuf_tensor("gtok", [128, 48], F32))
                    gm = sg3b.enter_context(nc.sbuf_tensor("gm", [128, 24], F32))
                    ge_ = sg3b.enter_context(nc.sbuf_tensor("ge", [128, 48], F32))
                    gw = sg3b.enter_context(nc.sbuf_tensor("gw", [128, 48], F32))
                    gpad = sg3b.enter_context(nc.sbuf_tensor("gpad", [128, 128], F32))
                    wT6 = sg3b.enter_context(nc.sbuf_tensor("wT6", [6, Lc], F32))
                    w96 = sg3b.enter_context(nc.sbuf_tensor("w96", [96, Lc], F32))

                    @block.gpsimd
                    def _(g):
                        g = W(g)
                        for kt in range(DT):
                            g.dma_start(out=wrs[:, kt * 6:(kt + 1) * 6],
                                        in_=wrout_in[kt * 128:(kt + 1) * 128, :]).then_inc(sem["GD"], 16)
                            g.dma_start(out=ass[:, kt * N:(kt + 1) * N],
                                        in_=as_in[kt * 128:(kt + 1) * 128, :]).then_inc(sem["GD"], 16)
                            g.dma_start(out=arts[:, kt * 96:(kt + 1) * 96],
                                        in_=art_in[kt * 128:(kt + 1) * 128, :]).then_inc(sem["GD"], 16)
                            C.inc("GD", 3)
                        g.dma_start(out=bss[:], in_=bs_in).then_inc(sem["GD"], 16); C.inc("GD")
                        g.dma_start(out=brss[:], in_=brs_in).then_inc(sem["GD"], 16); C.inc("GD")

                    pe7 = C.get("PE")
                    @block.tensor
                    def _(t):
                        t = W(t)
                        t.wait_ge(sem["GD"], 16 * C.get("GD"))
                        for fc in range(2):
                            for kt in range(DT):
                                t.matmul(out=pm[4][0:6, 0:512], lhsT=wrs[:, kt * 6:(kt + 1) * 6],
                                         rhs=h2[:, kt * Lc + fc * 512:kt * Lc + (fc + 1) * 512],
                                         start=(kt == 0), stop=(kt == DT - 1))
                                t.matmul(out=pm[5][0:16, 0:512], lhsT=ass[:, kt * N:(kt + 1) * N],
                                         rhs=h2[:, kt * Lc + fc * 512:kt * Lc + (fc + 1) * 512],
                                         start=(kt == 0), stop=(kt == DT - 1))
                                t.matmul(out=pm[6][0:96, 0:512], lhsT=arts[:, kt * 96:(kt + 1) * 96],
                                         rhs=h2[:, kt * Lc + fc * 512:kt * Lc + (fc + 1) * 512],
                                         start=(kt == 0), stop=(kt == DT - 1))
                            t.sem_inc(sem["PE"], 1)
                            if fc == 0:
                                t.wait_ge(sem["DVE"], C.get("DVE") + 1)
                    C.inc("PE", 2)

                    @block.vector
                    def _(v):
                        v = W(v)
                        v.memset(lgt[:], 0.0)
                        for fc in range(2):
                            v.wait_ge(sem["PE"], pe7 + fc + 1)
                            v.tensor_scalar(out=lgt[0:6, fc * 512:(fc + 1) * 512], in0=pm[4][0:6, 0:512],
                                            scalar1=t_fvec[0:6, 16:17], scalar2=None, op0=ALU.add)
                            v.tensor_scalar(out=s16[:, fc * 512:(fc + 1) * 512], in0=pm[5][0:16, 0:512],
                                            scalar1=1.0, scalar2=None, op0=ALU.mult)
                            v.tensor_scalar(out=h96[:, fc * 512:(fc + 1) * 512], in0=pm[6][0:96, 0:512],
                                            scalar1=1.0, scalar2=None, op0=ALU.mult)
                            v.sem_inc(sem["DVE"], 1)
                    C.inc("DVE", 2)

                    pe8 = C.get("PE")
                    dve_tr0 = C.get("DVE")
                    @block.tensor
                    def _(t):
                        t = W(t)
                        t.wait_ge(sem["DVE"], dve_tr0)
                        for lb in range(8):
                            if lb >= 1:
                                t.wait_ge(sem["DVE"], dve_tr0 + lb)
                            t.transpose(out=pm[7][:, 0:128], in_=lgt[:, lb * 128:(lb + 1) * 128],
                                        identity=t_ident[:])
                            t.sem_inc(sem["PE"], 1)
                    C.inc("PE", 8)

                    @block.vector
                    def _(v):
                        v = W(v)
                        for lb in range(8):
                            v.wait_ge(sem["PE"], pe8 + lb + 1)
                            v.tensor_copy(out=gtok[:, lb * 6:(lb + 1) * 6], in_=pm[7][:, 0:6])
                            v.sem_inc(sem["DVE"], 1)
                    C.inc("DVE", 8)

                    act9 = C.get("ACT")
                    dve_mark = []
                    @block.vector
                    def _(v):
                        v = W(v)
                        def fence():
                            v.sem_inc(sem["DVE"], 1)
                            C.inc("DVE")
                            v.wait_ge(sem["DVE"], C.get("DVE"))
                        v.wait_ge(sem["DVE"], C.get("DVE"))
                        v.tensor_reduce(out=gm[:, 0:8], in_=gtok[:].rearrange("p (a b) -> p a b", b=6),
                                        axis=mybir.AxisListType.X, op=ALU.max)
                        fence()
                        v.tensor_scalar(out=gm[:, 0:8], in0=gm[:, 0:8], scalar1=-1.0, scalar2=None, op0=ALU.mult)
                        v.sem_inc(sem["DVE"], 1)
                        C.inc("DVE")
                        dve_mark.append(C.get("DVE"))
                        v.wait_ge(sem["ACT"], act9 + 8)
                        v.tensor_reduce(out=gm[:, 8:16], in_=ge_[:].rearrange("p (a b) -> p a b", b=6),
                                        axis=mybir.AxisListType.X, op=ALU.add)
                        fence()
                        v.reciprocal(out=gm[:, 8:16], in_=gm[:, 8:16])
                        fence()
                        for lb in range(8):
                            v.tensor_scalar(out=ge_[:, lb * 6:(lb + 1) * 6], in0=ge_[:, lb * 6:(lb + 1) * 6],
                                            scalar1=gm[:, 8 + lb:9 + lb], scalar2=None, op0=ALU.mult)
                        fence()
                        v.tensor_reduce(out=gm[:, 16:24], in_=ge_[:].rearrange("p (a b) -> p a b", b=6),
                                        axis=mybir.AxisListType.X, op=ALU.max)
                        fence()
                        for lb in range(8):
                            v.tensor_scalar(out=gw[:, lb * 6:(lb + 1) * 6], in0=ge_[:, lb * 6:(lb + 1) * 6],
                                            scalar1=gm[:, 16 + lb:17 + lb], scalar2=None, op0=ALU.is_ge)
                        fence()
                        v.tensor_tensor(out=gw[:], in0=gw[:], in1=ge_[:], op=ALU.mult)
                        fence()
                        v.tensor_tensor(out=gtok[:], in0=ge_[:], in1=gw[:], op=ALU.subtract)
                        fence()
                        v.tensor_reduce(out=gm[:, 16:24], in_=gtok[:].rearrange("p (a b) -> p a b", b=6),
                                        axis=mybir.AxisListType.X, op=ALU.max)
                        fence()
                        for lb in range(8):
                            v.tensor_scalar(out=gw[:, lb * 6:(lb + 1) * 6], in0=ge_[:, lb * 6:(lb + 1) * 6],
                                            scalar1=gm[:, 16 + lb:17 + lb], scalar2=None, op0=ALU.is_ge)
                        fence()
                        v.tensor_tensor(out=gw[:], in0=gw[:], in1=ge_[:], op=ALU.mult)
                        v.sem_inc(sem["DVE"], 1)
                    C.inc("DVE", 1)

                    @block.scalar
                    def _(a):
                        a = W(a)
                        a.wait_ge(sem["DVE"], dve_mark[0])
                        for lb in range(8):
                            a.activation(out=ge_[:, lb * 6:(lb + 1) * 6], in_=gtok[:, lb * 6:(lb + 1) * 6],
                                         func=AF.Exp, bias=gm[:, lb:lb + 1]).then_inc(sem["ACT"], 1)
                        C.inc("ACT", 8)

                    pe9 = C.get("PE")
                    dve9 = C.get("DVE")
                    @block.vector
                    def _(v):
                        v = W(v)
                        v.wait_ge(sem["DVE"], dve9)
                        for lb in range(8):
                            if lb >= 1:
                                v.wait_ge(sem["PE"], pe9 + lb)
                            v.tensor_copy(out=gpad[:, 0:6], in_=gw[:, lb * 6:(lb + 1) * 6])
                            v.sem_inc(sem["DVE"], 1)
                            v.wait_ge(sem["PE"], pe9 + lb + 1)
                            v.tensor_copy(out=wT6[:, lb * 128:(lb + 1) * 128], in_=pm[7][0:6, 0:128])
                            v.sem_inc(sem["DVE"], 1)
                    C.inc("DVE", 16)

                    @block.tensor
                    def _(t):
                        t = W(t)
                        for lb in range(8):
                            t.wait_ge(sem["DVE"], dve9 + 2 * lb + 1)
                            t.transpose(out=pm[7][:, 0:128], in_=gpad[:], identity=t_ident[:])
                            t.sem_inc(sem["PE"], 1)
                    C.inc("PE", 8)

                    @block.sync
                    def _(sp):
                        sp = W(sp)
                        sp.wait_ge(sem["DVE"], C.get("DVE"))
                        sp.dma_start(out=wrow_dr[:, :], in_=wT6[:]).then_inc(sem["LD"], 16); C.inc("LD")
                        sp.wait_ge(sem["LD"], 16 * C.get("LD"))
                        for e6 in range(6):
                            sp.dma_start(out=w96[e6 * 16:(e6 + 1) * 16, :],
                                         in_=wrow_dr[e6:e6 + 1, :].broadcast_to([16, Lc])
                                         ).then_inc(sem["LD"], 16)
                            C.inc("LD")

                    @block.vector
                    def _(v):
                        v = W(v)
                        v.wait_ge(sem["LD"], 16 * C.get("LD"))
                        v.tensor_tensor(out=h96[:], in0=h96[:], in1=w96[:], op=ALU.mult)
                        v.sem_inc(sem["DVE"], 1)
                    C.inc("DVE")
                    barrier()
                    sg3b.close()
                    if _STOP[0] == 'G2':
                        return nc

                    with ExitStack() as sg4:
                        slabB = sg4.enter_context(nc.sbuf_tensor("slabB", [128, DT * D], BF16))
                        evb = [sg4.enter_context(nc.sbuf_tensor(f"evb{i}", [128, 512], F32)) for i in range(3)]

                        @block.gpsimd
                        def _(g):
                            g = W(g)
                            for kt in range(DT):
                                g.dma_start(out=slabB[:, kt * D:(kt + 1) * D],
                                            in_=wbase_in[kt * 128:(kt + 1) * 128, :]).then_inc(sem["GD"], 16)
                                C.inc("GD")

                        units = [(fc, dt) for fc in range(2) for dt in range(DT)]
                        pe10 = C.get("PE")
                        act10 = C.get("ACT")
                        dve10 = C.get("DVE")
                        eg0 = [C.get("E%d" % i) for i in range(3)]
                        @block.tensor
                        def _(t):
                            t = W(t)
                            t.wait_ge(sem["GD"], 16 * C.get("GD"))
                            for u, (fc, dt) in enumerate(units):
                                if u >= 4:
                                    t.wait_ge(sem["DVE"], dve10 + (u - 3))
                                for kt in range(DT):
                                    t.matmul(out=pm[u % 4][:, 0:512],
                                             lhsT=slabB[:, kt * D + dt * 128:kt * D + (dt + 1) * 128],
                                             rhs=h2[:, kt * Lc + fc * 512:kt * Lc + (fc + 1) * 512],
                                             start=(kt == 0), stop=False)
                                t.matmul(out=pm[u % 4][:, 0:512],
                                         lhsT=bss[:, dt * 128:(dt + 1) * 128],
                                         rhs=s16[:, fc * 512:(fc + 1) * 512],
                                         start=False, stop=False)
                                t.matmul(out=pm[u % 4][:, 0:512],
                                         lhsT=brss[:, dt * 128:(dt + 1) * 128],
                                         rhs=h96[:, fc * 512:(fc + 1) * 512],
                                         start=False, stop=True)
                                t.sem_inc(sem["PE"], 1)
                        C.inc("PE", len(units))

                        @block.vector
                        def _(v):
                            v = W(v)
                            for u, (fc, dt) in enumerate(units):
                                v.wait_ge(sem["PE"], pe10 + u + 1)
                                if u >= 3:
                                    v.wait_ge(sem["E%d" % (u % 3)], 16 * (eg0[u % 3] + (u - 3) // 3 + 1))
                                v.tensor_scalar(out=evb[u % 3][:], in0=pm[u % 4][:, 0:512],
                                                scalar1=BBASE(dt), scalar2=GGB(dt),
                                                op0=ALU.add, op1=ALU.mult)
                                v.tensor_tensor(out=evb[u % 3][:], in0=evb[u % 3][:],
                                                in1=x1T[:, dt * Lc + fc * 512:dt * Lc + (fc + 1) * 512],
                                                op=ALU.add)
                                v.sem_inc(sem["DVE"], 1)
                        C.inc("DVE", len(units))

                        @block.sync
                        def _(sp):
                            sp = W(sp)
                            for u, (fc, dt) in enumerate(units):
                                sp.wait_ge(sem["DVE"], dve10 + u + 1)
                                sp.dma_start(out=out_T[dt * 128:(dt + 1) * 128, fc * 512:(fc + 1) * 512],
                                             in_=evb[u % 3][:]).then_inc(sem["E%d" % (u % 3)], 16)
                                C.inc("E%d" % (u % 3))
                            for i in range(3):
                                sp.wait_ge(sem["E%d" % i], 16 * C.get("E%d" % i))
                            if _DBG[0]:
                                sp.dma_start(out=dbg[0:Ec, :], in_=spill_ss[:, :].bitcast(F32)[:, 0:512].rearrange("a b -> a b") if False else spill_ss[:, :]).then_inc(sem["ST"], 16) if False else None
                                pass
                        if _DBG[0]:
                            @block.gpsimd
                            def _(g):
                                g = W(g)
                                g.dma_start(out=dbg[0:Ec, :], in_=spill_ss[:, :]).then_inc(sem["GD"], 16); C.inc("GD")
                                g.dma_start(out=dbg[Ec:2 * Ec, :], in_=spill_yp[:, :]).then_inc(sem["GD"], 16); C.inc("GD")
                                for dt in range(DT):
                                    g.dma_start(out=dbg[10496 + dt * 128:10496 + (dt + 1) * 128, :],
                                                in_=h2[:, dt * Lc:(dt + 1) * Lc]).then_inc(sem["GD"], 16); C.inc("GD")
                                g.dma_start(out=dbg[12544:12550, :], in_=wT6[:, :]).then_inc(sem["GD"], 16); C.inc("GD")
                                g.dma_start(out=dbg[12550:12566, :], in_=s16[:, :]).then_inc(sem["GD"], 16); C.inc("GD")
                                g.dma_start(out=dbg[12566:12662, :], in_=h96[:, :]).then_inc(sem["GD"], 16); C.inc("GD")
                                g.dma_start(out=dbg[12662:12758, :], in_=w96[:, :]).then_inc(sem["GD"], 16); C.inc("GD")
                                g.dma_start(out=dbg[12758:12886, 0:48], in_=gw[:, :]).then_inc(sem["GD"], 16); C.inc("GD")
                                g.dma_start(out=dbg[12886:13014, 0:256], in_=t_state[:, :]).then_inc(sem["GD"], 16); C.inc("GD")
                            @block.sync
                            def _(sp):
                                sp = W(sp)
                                sp.dma_start(out=dbg[2 * Ec:2 * Ec + D, :], in_=sh_out[0:D, :]).then_inc(sem["LD"], 16); C.inc("LD")
                                sp.dma_start(out=dbg[2 * Ec + D:2 * Ec + 2 * D, :], in_=sh_out[2 * D:3 * D, :]).then_inc(sem["LD"], 16); C.inc("LD")
                                for dt in range(DT):
                                    sp.dma_start(out=dbg[2 * Ec + 2 * D + dt * 128:2 * Ec + 2 * D + (dt + 1) * 128, :],
                                                 in_=x1T[:, dt * Lc:(dt + 1) * Lc]).then_inc(sem["LD"], 16); C.inc("LD")
                                sp.dma_start(out=dbg[6 * D:6 * D + 1, 0:1024], in_=modrow_dr[:, 0:1024]).then_inc(sem["LD"], 16); C.inc("LD")
                        barrier()
    return nc


_NC_CACHE = {}

def kernel(**inputs):
    f32 = np.float32
    resid = np.asarray(inputs["L0_attn_resid"], f32)
    norm_w = np.asarray(inputs["L0_attn_norm_w"], f32)
    skip_w = np.asarray(inputs["L0_attn_skip_w"], f32)
    in_w = np.asarray(inputs["L0_attn_in_w"], f32)
    conv_w = np.asarray(inputs["L0_attn_conv_w"], f32)
    conv_b = np.asarray(inputs["L0_attn_conv_b"], f32)
    wd1 = np.asarray(inputs["L0_attn_wd1"], f32)
    wd2 = np.asarray(inputs["L0_attn_wd2"], f32)
    wd2_b = np.asarray(inputs["L0_attn_wd2_b"], f32)
    wB = np.asarray(inputs["L0_attn_wB"], f32)
    wC = np.asarray(inputs["L0_attn_wC"], f32)
    A_log = np.asarray(inputs["L0_attn_A_log"], f32)
    W_D = np.asarray(inputs["L0_attn_W_D"], f32)
    out_w = np.asarray(inputs["L0_attn_out_w"], f32)
    W_base = np.asarray(inputs["L0_ffn_W_base"], f32)
    b_base = np.asarray(inputs["L0_ffn_b_base"], f32)
    A_s = np.asarray(inputs["L0_ffn_A_s"], f32)
    B_s = np.asarray(inputs["L0_ffn_B_s"], f32)
    A_r = np.asarray(inputs["L0_ffn_A_r"], f32)
    B_r = np.asarray(inputs["L0_ffn_B_r"], f32)
    W_router = np.asarray(inputs["L0_ffn_W_router"], f32)
    b_router = np.asarray(inputs["L0_ffn_b_router"], f32)
    c_in = np.asarray(inputs["L0_c"], f32)
    wmod = np.asarray(inputs["L0_wmod"], f32)

    def colmaj(vec):
        return np.ascontiguousarray(vec.reshape(16, 128).T)

    cvec = np.zeros((128, 64), f32)
    fvec = np.zeros((128, 32), f32)
    ident = np.eye(128, dtype=f32)
    A_rT = np.ascontiguousarray(A_r.transpose(1, 0, 2).reshape(D, 96))
    B_rs = np.ascontiguousarray(B_r.reshape(96, D))
    fvec[:, 0:16] = colmaj(b_base)
    fvec[0:6, 16] = b_router

    in_maps = []
    for core in range(8):
        b, h = core // 2, core % 2
        sl = slice(h * Ec, (h + 1) * Ec)
        cv = cvec.copy()
        cv[:, 0:16] = colmaj(conv_b[sl])
        cv[:, 16:32] = colmaj(wd2_b[sl])
        cv[:, 32:48] = colmaj(W_D[sl])
        cv[:, 48:64] = colmaj(norm_w)
        m = {
            "residT": np.ascontiguousarray(resid[b].T),
            "role": np.array([[h]], np.uint32),
            "bid": np.array([[b]], np.uint32),
            "c4": np.ascontiguousarray(c_in.T[h * 1024:(h + 1) * 1024].reshape(8, 128, 4).transpose(1, 0, 2).reshape(128, 32)),
            "wmod_s": np.ascontiguousarray(wmod[h * 1024:(h + 1) * 1024, :]),
            "in_wT": np.ascontiguousarray(in_w[sl, :].T),
            "skip_wT": np.ascontiguousarray(skip_w[sl, :].T),
            "out_wT": np.ascontiguousarray(out_w[:, sl].T),
            "wd1T": np.ascontiguousarray(wd1[:, sl].T),
            "wd2T": np.ascontiguousarray(wd2[sl, :].T),
            "wBT": np.ascontiguousarray(wB[:, sl].T),
            "wCT": np.ascontiguousarray(wC[:, sl].T),
            "alog": np.ascontiguousarray(A_log[sl].reshape(16, 128, 16).transpose(1, 0, 2).reshape(128, 256)),
            "convw": np.ascontiguousarray(conv_w[sl].reshape(16, 128, 4).transpose(1, 0, 2).reshape(128, 64)),
            "cvec": cv,
            "W_base": W_base,
            "fvec": fvec,
            "A_s": A_s,
            "B_s": B_s,
            "A_rT": A_rT,
            "B_rs": B_rs,
            "W_router": W_router,
            "ident": ident,
        }
        in_maps.append(m)

    if "nc" not in _NC_CACHE:
        _NC_CACHE["nc"] = build_program()
    nc = _NC_CACHE["nc"]
    res = run_bass_kernel_spmd(nc, in_maps, core_ids=list(range(8)))
    full = np.empty((B, L, D), f32)
    for core in range(8):
        b, h = core // 2, core % 2
        full[b, h * Lc:(h + 1) * Lc, :] = np.asarray(res.results[core]["out_T"]).T
    return full


# version 07
# speedup vs baseline: 1.0087x; 1.0087x over previous
import numpy as np
import concourse.bass as bass
import concourse.mybir as mybir
from concourse.bass_utils import run_bass_kernel_spmd
from contextlib import ExitStack

F32 = mybir.dt.float32
BF16 = mybir.dt.bfloat16
U32 = mybir.dt.uint32
AF = mybir.ActivationFunctionType
ALU = mybir.AluOpType

B, L, D, E, N, R = 4, 2048, 2048, 4096, 16, 128
Ec = E // 2
Lc = 1024
NCH = L // Lc
DT = D // 128
ET = Ec // 128
EB = 8
NBLK = ET // EB
EPS = 1e-5
MAGIC = 777001000

def fl_mod(w): return w
def fl_zbc(w, c): return 2 + w * 2 + c
def fl_out(w, c): return 6 + w * 2 + c
NFLAGS = 10


class Ctr:
    def __init__(self):
        self.v = {}
    def inc(self, k, n=1):
        self.v[k] = self.v.get(k, 0) + n
        return self.v[k]
    def get(self, k):
        return self.v.get(k, 0)


class W:
    def __init__(self, e):
        object.__setattr__(self, "_e", e)
        object.__setattr__(self, "_last", None)
    def __getattr__(self, name):
        f = getattr(self._e, name)
        if not callable(f):
            return f
        def g(*a, **k):
            r = f(*a, **k)
            if hasattr(r, "then_inc"):
                object.__setattr__(self, "_last", r)
            return r
        return g
    def sem_inc(self, s, n):
        if self._last is not None:
            self._last.then_inc(s, n)
        else:
            self._e.sem_inc(s, n)


_STOP = [None]
_RACE = [False]
_DBG = [False]

def build_program():
    nc = bass.Bass("TRN2", target_bir_lowering=False, enable_asserts=False,
                   detect_race_conditions=_RACE[0])

    def I(name, shape, dt=F32):
        return nc.dram_tensor(name, shape, dt, kind="ExternalInput").ap()

    residT = I("residT", [D, L])
    role_in = I("role", [1, 1], U32)
    bid_in = I("bid", [1, 1], U32)
    c4_in = I("c4", [128, 32])
    wmod_in = I("wmod_s", [1024, 6 * D])
    in_wT = I("in_wT", [D, Ec])
    skip_wT = I("skip_wT", [D, Ec])
    out_wT = I("out_wT", [Ec, D])
    wd1T_in = I("wd1T", [Ec, R])
    wd2T_in = I("wd2T", [R, Ec])
    wBT_in = I("wBT", [Ec, N])
    wCT_in = I("wCT", [Ec, N])
    alog_in = I("alog", [128, ET * N])
    convw_in = I("convw", [128, 64])
    cvec_in = I("cvec", [128, 64])
    wbase_in = I("W_base", [D, D])
    fvec_in = I("fvec", [128, 32])
    as_in = I("A_s", [D, N])
    bs_in = I("B_s", [N, D])
    art_in = I("A_rT", [D, 96])
    brs_in = I("B_rs", [96, D])
    wrout_in = I("W_router", [D, 6])
    ident_in = I("ident", [128, 128])

    out_T = nc.dram_tensor("out_T", [D, Lc], F32, kind="ExternalOutput").ap()
    dbg = nc.dram_tensor("dbg", [16384, Lc], F32, kind="ExternalOutput").ap() if _DBG[0] else None

    spill_ss = nc.dram_tensor("spill_ss", [Ec, Lc], BF16, kind="Internal").ap()
    spill_yp = nc.dram_tensor("spill_yp", [Ec, Lc], BF16, kind="Internal").ap()
    spill_h = nc.dram_tensor("spill_h", [2 * D, Lc], BF16, kind="Internal").ap()
    rows_bc = nc.dram_tensor("rows_bc", [8, Lc], F32, kind="Internal").ap()
    bmcm_dr = nc.dram_tensor("bmcm_dr", [32, Lc], BF16, kind="Internal").ap()
    modp_dr = nc.dram_tensor("modp_dr", [4, 6 * D], F32, kind="Internal").ap()
    modrow_dr = nc.dram_tensor("modrow_dr", [1, 6 * D], F32, kind="Internal").ap()
    wrow_dr = nc.dram_tensor("wrow_dr", [6, Lc], F32, kind="Internal").ap()

    sh_zbc = nc.dram_tensor("sh_zbc", [4 * 160, Lc], F32, kind="Internal", addr_space="Shared").ap()
    sh_out = nc.dram_tensor("sh_out", [4 * D, Lc], F32, kind="Internal", addr_space="Shared").ap()
    sh_mod = nc.dram_tensor("sh_mod", [8, 6 * D], F32, kind="Internal", addr_space="Shared").ap()
    sh_flag = nc.dram_tensor("sh_flag", [NFLAGS, 16], U32, kind="Internal", addr_space="Shared").ap()

    C = Ctr()
    with ExitStack() as octx:
        sem = {}
        for s in ["LD", "GD", "PE", "ACT", "DVE", "PL", "ST", "PX", "MS", "BR",
                  "R0", "R1", "GB0", "GB1", "E0", "E1", "E2", "E3",
                  "BC0", "BC1", "SK", "YP", "CA", "CB", "CR", "CH"]:
            sem[s] = octx.enter_context(nc.semaphore("s" + s))
        SB = lambda name, shape, dt: octx.enter_context(nc.sbuf_tensor(name, shape, dt))
        pm = [octx.enter_context(nc.psum_tensor(f"pm{i}", [128, 512], F32)) for i in range(8)]

        t_role = SB("t_role", [1, 1], U32)
        t_bidv = SB("t_bidv", [1, 1], U32)
        t_c4 = SB("t_c4", [128, 32], F32)
        t_sc4 = SB("t_sc4", [128, 32], BF16)
        t_negA = SB("t_negA", [128, ET * N], F32)
        t_convw = SB("t_convw", [128, 64], F32)
        t_cvec = SB("t_cvec", [128, 64], F32)
        t_fvec = SB("t_fvec", [128, 32], F32)
        t_gain = SB("t_gain", [128, 96], F32)
        t_state = SB("t_state", [128, 256], F32)
        t_ident = SB("t_ident", [128, 128], F32)
        t_ones = SB("t_ones", [128, 1], BF16)
        t_eps1 = SB("t_eps1", [1, 1], F32)
        t_one1 = SB("t_one1", [128, 1], F32)
        t_flagv = SB("t_flagv", [1, 16 * NFLAGS], U32)
        t_flagrd = SB("t_flagrd", [1, 16], U32)
        t_rows = [SB(f"t_row{i}", [1, Lc], F32) for i in range(3)]
        t_tail = SB("t_tail", [128, ET * 4], F32)
        t_brs = SB("t_brs", [1, 8], F32)

        block = octx.enter_context(nc.Block())
        reg_role = octx.enter_context(nc.sync.register("r_role"))
        reg_bid = octx.enter_context(nc.sync.register("r_bid"))
        reg_ind = octx.enter_context(nc.sync.register("r_ind"))
        reg_fr = octx.enter_context(nc.sync.register("r_fr"))
        reg_px = octx.enter_context(nc.sync.register("r_px"))

        CONVB = lambda et: t_cvec[:, et:et + 1]
        WD2B = lambda et: t_cvec[:, 16 + et:17 + et]
        WDC = lambda et: t_cvec[:, 32 + et:33 + et]
        NRMW = lambda dt: t_cvec[:, 48 + dt:49 + dt]
        GSA = lambda dt: t_gain[:, 0 + dt:1 + dt]
        GCA = lambda dt: t_gain[:, 16 + dt:17 + dt]
        GGA = lambda dt: t_gain[:, 32 + dt:33 + dt]
        GSB = lambda dt: t_gain[:, 48 + dt:49 + dt]
        GCB = lambda dt: t_gain[:, 64 + dt:65 + dt]
        GGB = lambda dt: t_gain[:, 80 + dt:81 + dt]
        BBASE = lambda dt: t_fvec[:, dt:dt + 1]

        def barrier():
            @block.sync
            def _(sp):
                sp = W(sp)
                sp.wait_ge(sem["LD"], 16 * C.get("LD"))
                sp.wait_ge(sem["ST"], 16 * C.get("ST"))
                for s_ in ["R0", "R1", "E0", "E1", "E2", "E3", "BC0", "BC1", "SK", "YP", "CA", "CB", "CR", "CH"]:
                    if C.get(s_):
                        sp.wait_ge(sem[s_], 16 * C.get(s_))
            @block.gpsimd
            def _(g):
                g = W(g)
                g.wait_ge(sem["GD"], 16 * C.get("GD"))
                for s_ in ["GB0", "GB1"]:
                    if C.get(s_):
                        g.wait_ge(sem[s_], 16 * C.get(s_))
            tgt = 5 * C.inc("BRS")
            @block.sync
            def _(e):
                e = W(e)
                e.sem_inc(sem["BR"], 1)
                e.wait_ge(sem["BR"], tgt)
            @block.gpsimd
            def _(e):
                e = W(e)
                e.memset(t_brs[0:1, 0:1], 0.0)
                e.sem_inc(sem["BR"], 1)
                e.wait_ge(sem["BR"], tgt)
            @block.vector
            def _(e):
                e = W(e)
                e.memset(t_brs[0:1, 1:2], 0.0)
                e.sem_inc(sem["BR"], 1)
                e.wait_ge(sem["BR"], tgt)
            @block.scalar
            def _(e):
                e = W(e)
                e.memzero(t_brs[0:1, 2:3])
                e.sem_inc(sem["BR"], 1)
                e.wait_ge(sem["BR"], tgt)
            @block.tensor
            def _(e):
                e = W(e)
                e.sem_inc(sem["BR"], 1)
                e.wait_ge(sem["BR"], tgt)

        def poll(sp, f):
            sp.reg_mov(reg_ind, 1)
            with sp.While(reg_ind):
                sp.dma_start(out=t_flagrd[:], in_=sh_flag[f:f + 1, :]).then_inc(sem["PX"], 16)
                sp.reg_add(reg_px, reg_px, 16)
                sp.wait_ge(sem["PX"], reg_px)
                sp.reg_load(reg_fr, t_flagrd[0:1, 0:1])
                sp.reg_mov(reg_ind, 0)
                with sp.If_ne(reg_fr, MAGIC + f):
                    sp.reg_mov(reg_ind, 1)

        with ExitStack() as s0:
            HW = 3 * D
            slabM = s0.enter_context(nc.sbuf_tensor("slabM", [128, 8 * HW], BF16))
            t_mp = [s0.enter_context(nc.sbuf_tensor(f"t_mp{i}", [4, 512], F32)) for i in range(2)]
            n_fc = 6 * D // 512
            n_fch = n_fc // 2

            @block.sync
            def _(sp):
                sp = W(sp)
                for name, dst in [(role_in, t_role), (bid_in, t_bidv), (c4_in, t_c4),
                                  (alog_in, t_negA), (convw_in, t_convw), (cvec_in, t_cvec),
                                  (fvec_in, t_fvec), (ident_in, t_ident)]:
                    sp.dma_start(out=dst[:], in_=name).then_inc(sem["LD"], 16)
                    C.inc("LD")
                sp.wait_ge(sem["LD"], 16 * C.get("LD"))
                sp.reg_load(reg_role, t_role[0:1, 0:1])
                sp.reg_load(reg_bid, t_bidv[0:1, 0:1])
                sp.reg_mov(reg_px, 0)
                sp.sem_inc(sem["MS"], 1)
            C.inc("MS")

            dve0 = C.get("DVE")
            @block.vector
            def _(v):
                v = W(v)
                v.wait_ge(sem["MS"], C.get("MS"))
                v.memset(t_ones[:], 1.0)
                v.memset(t_one1[:], 1.0)
                v.memset(t_eps1[:], EPS)
                for f in range(NFLAGS):
                    v.memset(t_flagv[:, 16 * f:16 * (f + 1)], MAGIC + f)
                v.memset(t_state[:], 0.0)
                v.wait_ge(sem["ACT"], C.get("ACT") + 2)
                v.tensor_scalar(out=t_negA[:], in0=t_negA[:], scalar1=-1.0, scalar2=None, op0=ALU.mult)
                v.tensor_scalar(out=t_sc4[:], in0=t_c4[:], scalar1=1.0, scalar2=None, op0=ALU.mult)
                v.sem_inc(sem["DVE"], 1)
            C.inc("DVE")

            act0 = C.get("ACT")
            @block.scalar
            def _(a):
                a = W(a)
                a.wait_ge(sem["MS"], C.get("MS"))
                a.activation(out=t_negA[:], in_=t_negA[:], func=AF.Exp).then_inc(sem["ACT"], 1)
                a.activation(out=t_c4[:], in_=t_c4[:], func=AF.Silu).then_inc(sem["ACT"], 1)
            C.inc("ACT", 2)

            if _STOP[0] == 'm1':
                barrier()
                return nc

            gd_m0 = C.get("GD")
            pe_m0 = C.get("PE")
            @block.gpsimd
            def _(g):
                g = W(g)
                for half in range(2):
                    if half == 1:
                        g.wait_ge(sem["PE"], pe_m0 + n_fch)
                    for kt in range(8):
                        g.dma_start(out=slabM[:, kt * HW:(kt + 1) * HW],
                                    in_=wmod_in[kt * 128:(kt + 1) * 128, half * HW:(half + 1) * HW]
                                    ).then_inc(sem["GD"], 16)
                        C.inc("GD")
            pe0 = C.get("PE")
            @block.tensor
            def _(t):
                t = W(t)
                t.wait_ge(sem["DVE"], C.get("DVE"))
                for half in range(2):
                    t.wait_ge(sem["GD"], 16 * (gd_m0 + 8 * (half + 1)))
                    for fcl in range(n_fch):
                        fc = half * n_fch + fcl
                        if fc >= 2:
                            t.wait_ge(sem["ACT"], act0 + 2 + (fc - 2) + 1)
                        for kt in range(8):
                            t.matmul(out=pm[fc % 2][0:4, 0:512],
                                     lhsT=t_sc4[:, kt * 4:(kt + 1) * 4],
                                     rhs=slabM[:, kt * HW + fcl * 512:kt * HW + (fcl + 1) * 512],
                                     start=(kt == 0), stop=(kt == 7))
                        t.sem_inc(sem["PE"], 1)
            C.inc("PE", n_fc)

            em0 = [C.get("E0"), C.get("E1")]
            @block.scalar
            def _(a):
                a = W(a)
                for fc in range(n_fc):
                    a.wait_ge(sem["PE"], pe0 + fc + 1)
                    if fc >= 2:
                        a.wait_ge(sem["E%d" % (fc % 2)], 16 * (em0[fc % 2] + (fc - 2) // 2 + 1))
                    a.activation(out=t_mp[fc % 2][:], in_=pm[fc % 2][0:4, 0:512],
                                 func=AF.Identity).then_inc(sem["ACT"], 1)
            act1 = C.get("ACT")
            C.inc("ACT", n_fc)

            @block.sync
            def _(sp):
                sp = W(sp)
                for fc in range(n_fc):
                    sp.wait_ge(sem["ACT"], act1 + fc + 1)
                    sp.dma_start(out=modp_dr[:, fc * 512:(fc + 1) * 512],
                                 in_=t_mp[fc % 2][:]).then_inc(sem["E%d" % (fc % 2)], 16)
                    C.inc("E%d" % (fc % 2))
                sp.wait_ge(sem["E0"], 16 * C.get("E0"))
                sp.wait_ge(sem["E1"], 16 * C.get("E1"))
            if _STOP[0] == 'm2':
                barrier()
                return nc
            @block.sync
            def _(sp):
                sp = W(sp)
                with sp.If_eq(reg_role, 0):
                    sp.dma_start(out=sh_mod[0:4, :], in_=modp_dr[:, :]).then_inc(sem["LD"], 16)
                with sp.Else():
                    sp.dma_start(out=sh_mod[4:8, :], in_=modp_dr[:, :]).then_inc(sem["LD"], 16)
                C.inc("LD")
                sp.wait_ge(sem["LD"], 16 * C.get("LD"))
                with sp.If_eq(reg_role, 0):
                    sp.dma_start(out=sh_flag[0:1, :], in_=t_flagv[:, 0:16]).then_inc(sem["LD"], 16)
                with sp.Else():
                    sp.dma_start(out=sh_flag[1:2, :], in_=t_flagv[:, 16:32]).then_inc(sem["LD"], 16)
                C.inc("LD")
                sp.wait_ge(sem["LD"], 16 * C.get("LD"))
                poll(sp, fl_mod(0))
                poll(sp, fl_mod(1))
            barrier()
            if _STOP[0] == 'm3':
                return nc

        with ExitStack() as s1:
            t_ma = s1.enter_context(nc.sbuf_tensor("t_ma", [4, 6 * D], F32))
            t_mb = s1.enter_context(nc.sbuf_tensor("t_mb", [4, 6 * D], F32))
            t_graw = s1.enter_context(nc.sbuf_tensor("t_graw", [128, 96], F32))
            t_gtmp = s1.enter_context(nc.sbuf_tensor("t_gtmp", [128, 32], F32))

            @block.sync
            def _(sp):
                sp = W(sp)
                sp.dma_start(out=t_ma[:], in_=sh_mod[0:4, :]).then_inc(sem["LD"], 16); C.inc("LD")
                sp.dma_start(out=t_mb[:], in_=sh_mod[4:8, :]).then_inc(sem["LD"], 16); C.inc("LD")

            @block.vector
            def _(v):
                v = W(v)
                v.wait_ge(sem["LD"], 16 * C.get("LD"))
                v.tensor_tensor(out=t_ma[:], in0=t_ma[:], in1=t_mb[:], op=ALU.add).then_inc(sem["DVE"], 1)
            C.inc("DVE")

            @block.sync
            def _(sp):
                sp = W(sp)
                sp.wait_ge(sem["DVE"], C.get("DVE"))
                for bb in range(4):
                    with sp.If_eq(reg_bid, bb):
                        sp.dma_start(out=modrow_dr[:, :], in_=t_ma[bb:bb + 1, :]).then_inc(sem["LD"], 16)
                C.inc("LD")
                sp.wait_ge(sem["LD"], 16 * C.get("LD"))
                for g in range(6):
                    for dt in range(DT):
                        src = modrow_dr[:, g * D + dt * 128:g * D + (dt + 1) * 128]
                        sp.dma_start(out=t_graw[:, g * 16 + dt:g * 16 + dt + 1],
                                     in_=src.rearrange("a b -> b a")).then_inc(sem["LD"], 16)
                        C.inc("LD")

            act2 = C.get("ACT")
            @block.scalar
            def _(a):
                a = W(a)
                a.wait_ge(sem["LD"], 16 * C.get("LD"))
                a.activation(out=t_gtmp[:, 0:16], in_=t_graw[:, 32:48], func=AF.Abs)
                a.activation(out=t_gtmp[:, 16:32], in_=t_graw[:, 80:96], func=AF.Abs)
                a.activation(out=t_graw[:, 32:48], in_=t_graw[:, 32:48], func=AF.Sign)
                a.activation(out=t_graw[:, 80:96], in_=t_graw[:, 80:96], func=AF.Sign).then_inc(sem["ACT"], 1)
            C.inc("ACT")

            @block.vector
            def _(v):
                v = W(v)
                v.wait_ge(sem["ACT"], C.get("ACT"))
                v.tensor_copy(out=t_gain[:, 0:16], in_=t_graw[:, 0:16])
                v.tensor_scalar(out=t_gain[:, 16:32], in0=t_graw[:, 16:32], scalar1=1.0, scalar2=None, op0=ALU.add)
                v.tensor_copy(out=t_gain[:, 48:64], in_=t_graw[:, 48:64])
                v.tensor_scalar(out=t_gain[:, 64:80], in0=t_graw[:, 64:80], scalar1=1.0, scalar2=None, op0=ALU.add)
                v.tensor_scalar(out=t_gtmp[:, 0:16], in0=t_gtmp[:, 0:16], scalar1=0.25, scalar2=None, op0=ALU.max)
                v.tensor_scalar(out=t_gtmp[:, 16:32], in0=t_gtmp[:, 16:32], scalar1=0.25, scalar2=None, op0=ALU.max)
                v.sem_inc(sem["DVE"], 1)
                v.wait_ge(sem["DVE"], C.get("DVE") + 1)
                v.tensor_tensor(out=t_gain[:, 32:48], in0=t_gtmp[:, 0:16], in1=t_graw[:, 32:48], op=ALU.mult)
                v.tensor_tensor(out=t_gain[:, 80:96], in0=t_gtmp[:, 16:32], in1=t_graw[:, 80:96], op=ALU.mult)
                v.sem_inc(sem["DVE"], 1)
            C.inc("DVE", 2)
            barrier()
            if _STOP[0] == 'mod':
                return nc

        for c in range(NCH):
            with ExitStack() as scc:
                xT = scc.enter_context(nc.sbuf_tensor(f"xT_{c}", [128, ET * Lc], F32))
                with ExitStack() as sph1:
                    rnT = sph1.enter_context(nc.sbuf_tensor(f"rnT_{c}", [128, ET * Lc], BF16))

                    with ExitStack() as sa:
                        bufH = sa.enter_context(nc.sbuf_tensor(f"bufH_{c}", [128, DT * Lc], BF16))
                        rR = sa.enter_context(nc.sbuf_tensor(f"rR_{c}", [128, 2 * Lc], F32))
                        b16 = sa.enter_context(nc.sbuf_tensor(f"b16_{c}", [128, 2 * Lc], BF16))
                        sqb = sa.enter_context(nc.sbuf_tensor(f"sqb_{c}", [128, 2 * Lc], BF16))
                        mB = sa.enter_context(nc.sbuf_tensor(f"mB_{c}", [128, Lc], F32))
                        rB = sa.enter_context(nc.sbuf_tensor(f"rB_{c}", [128, Lc], F32))
                        r2B = sa.enter_context(nc.sbuf_tensor(f"r2B_{c}", [128, Lc], F32))
                        tv = sa.enter_context(nc.sbuf_tensor(f"tv_{c}", [128, Lc], F32))

                        r0b, r1b = C.get("R0"), C.get("R1")
                        g0b, g1b = C.get("GB0"), C.get("GB1")
                        rcnt = lambda dt: 16 * ((r0b if dt % 2 == 0 else r1b) + dt // 2 + 1)
                        gcnt = lambda dt: 16 * ((g0b if dt % 2 == 0 else g1b) + dt // 2 + 1)
                        act0 = C.get("ACT")
                        pe0 = C.get("PE")
                        @block.sync
                        def _(sp):
                            sp = W(sp)
                            for dt in range(DT):
                                if dt >= 2:
                                    sp.wait_ge(sem["PE"], pe0 + dt - 1)
                                sp.dma_start(out=rR[:, (dt % 2) * Lc:(dt % 2 + 1) * Lc],
                                             in_=residT[dt * 128:(dt + 1) * 128, c * Lc:(c + 1) * Lc]
                                             ).then_inc(sem["R%d" % (dt % 2)], 16)
                                C.inc("R%d" % (dt % 2))

                        @block.gpsimd
                        def _(g):
                            g = W(g)
                            for dt in range(DT):
                                g.wait_ge(sem["R%d" % (dt % 2)], rcnt(dt))
                                if dt >= 2:
                                    g.wait_ge(sem["PE"], pe0 + dt - 1)
                                g.dma_start(out=b16[:, (dt % 2) * Lc:(dt % 2 + 1) * Lc],
                                            in_=rR[:, (dt % 2) * Lc:(dt % 2 + 1) * Lc]).then_inc(sem["GB%d" % (dt % 2)], 16)
                                C.inc("GB%d" % (dt % 2))

                        @block.scalar
                        def _(a):
                            a = W(a)
                            for dt in range(DT):
                                a.wait_ge(sem["R%d" % (dt % 2)], rcnt(dt))
                                if dt >= 2:
                                    a.wait_ge(sem["PE"], pe0 + dt - 1)
                                a.activation(out=sqb[:, (dt % 2) * Lc:(dt % 2 + 1) * Lc],
                                             in_=rR[:, (dt % 2) * Lc:(dt % 2 + 1) * Lc],
                                             func=AF.Square).then_inc(sem["ACT"], 1)
                                C.inc("ACT")

                        @block.tensor
                        def _(t):
                            t = W(t)
                            for dt in range(DT):
                                t.wait_ge(sem["GB%d" % (dt % 2)], gcnt(dt))
                                t.wait_ge(sem["ACT"], act0 + dt + 1)
                                for fc in range(2):
                                    sl = slice((dt % 2) * Lc + fc * 512, (dt % 2) * Lc + (fc + 1) * 512)
                                    t.matmul(out=pm[4 + fc][0:1, 0:512], lhsT=t_ones[:],
                                             rhs=b16[:, sl], start=(dt == 0), stop=(dt == DT - 1))
                                    t.matmul(out=pm[6 + fc][0:1, 0:512], lhsT=t_ones[:],
                                             rhs=sqb[:, sl], start=(dt == 0), stop=(dt == DT - 1))
                                t.sem_inc(sem["PE"], 1)
                        C.inc("PE", DT)

                        @block.vector
                        def _(v):
                            v = W(v)
                            v.wait_ge(sem["PE"], C.get("PE"))
                            for fc in range(2):
                                v.tensor_scalar(out=t_rows[0][:, fc * 512:(fc + 1) * 512],
                                                in0=pm[4 + fc][0:1, 0:512], scalar1=1.0 / D, scalar2=None, op0=ALU.mult)
                                v.tensor_scalar(out=t_rows[1][:, fc * 512:(fc + 1) * 512],
                                                in0=pm[6 + fc][0:1, 0:512], scalar1=1.0 / D, scalar2=None, op0=ALU.mult)
                            v.tensor_tensor(out=t_rows[2][:], in0=t_rows[0][:], in1=t_rows[0][:], op=ALU.mult)
                            v.tensor_tensor(out=t_rows[1][:], in0=t_rows[1][:], in1=t_rows[2][:], op=ALU.subtract)
                            v.sem_inc(sem["DVE"], 1)
                        C.inc("DVE")

                        @block.scalar
                        def _(a):
                            a = W(a)
                            a.wait_ge(sem["DVE"], C.get("DVE"))
                            a.activation(out=t_rows[1][:], in_=t_rows[1][:], func=AF.Sqrt,
                                         bias=t_eps1[0:1, 0:1]).then_inc(sem["ACT"], 1)
                        C.inc("ACT")

                        @block.vector
                        def _(v):
                            v = W(v)
                            v.wait_ge(sem["ACT"], C.get("ACT"))
                            v.reciprocal(out=t_rows[1][:], in_=t_rows[1][:]).then_inc(sem["DVE"], 1)
                        C.inc("DVE")

                        @block.sync
                        def _(sp):
                            sp = W(sp)
                            sp.wait_ge(sem["DVE"], C.get("DVE"))
                            sp.dma_start(out=rows_bc[0:1, :], in_=t_rows[0][:]).then_inc(sem["LD"], 16); C.inc("LD")
                            sp.dma_start(out=rows_bc[1:2, :], in_=t_rows[1][:]).then_inc(sem["LD"], 16); C.inc("LD")
                            sp.wait_ge(sem["LD"], 16 * C.get("LD"))
                            sp.dma_start(out=mB[:], in_=rows_bc[0:1, :].broadcast_to([128, Lc])).then_inc(sem["LD"], 16); C.inc("LD")
                            sp.dma_start(out=rB[:], in_=rows_bc[1:2, :].broadcast_to([128, Lc])).then_inc(sem["LD"], 16); C.inc("LD")

                        r0c, r1c = C.get("R0"), C.get("R1")
                        rcnt2 = lambda dt: 16 * ((r0c if dt % 2 == 0 else r1c) + dt // 2 + 1)
                        dve1 = C.get("DVE")
                        act3 = C.get("ACT")
                        pe1 = C.get("PE")
                        @block.sync
                        def _(sp):
                            sp = W(sp)
                            for dt in range(DT):
                                if dt >= 2:
                                    sp.wait_ge(sem["DVE"], dve1 + dt - 1)
                                sp.dma_start(out=rR[:, (dt % 2) * Lc:(dt % 2 + 1) * Lc],
                                             in_=residT[dt * 128:(dt + 1) * 128, c * Lc:(c + 1) * Lc]
                                             ).then_inc(sem["R%d" % (dt % 2)], 16)
                                C.inc("R%d" % (dt % 2))

                        @block.vector
                        def _(v):
                            v = W(v)
                            for dt in range(DT):
                                v.wait_ge(sem["R%d" % (dt % 2)], rcnt2(dt))
                                v.tensor_tensor(out=tv[:], in0=rR[:, (dt % 2) * Lc:(dt % 2 + 1) * Lc],
                                                in1=mB[:], op=ALU.subtract)
                                v.tensor_tensor(out=tv[:], in0=tv[:], in1=rB[:], op=ALU.mult)
                                v.tensor_scalar(out=bufH[:, dt * Lc:(dt + 1) * Lc], in0=tv[:],
                                                scalar1=GCA(dt), scalar2=GSA(dt), op0=ALU.mult, op1=ALU.add)
                                v.sem_inc(sem["DVE"], 1)
                        C.inc("DVE", DT)

                        @block.gpsimd
                        def _(g):
                            g = W(g)
                            for dt in range(DT):
                                g.wait_ge(sem["DVE"], dve1 + dt + 1)
                                g.dma_start(out=spill_h[c * D + dt * 128:c * D + (dt + 1) * 128, :],
                                            in_=bufH[:, dt * Lc:(dt + 1) * Lc]).then_inc(sem["GD"], 16)
                                C.inc("GD")

                        @block.scalar
                        def _(a):
                            a = W(a)
                            for dt in range(DT):
                                a.wait_ge(sem["DVE"], dve1 + dt + 1)
                                if dt >= 2:
                                    a.wait_ge(sem["PE"], pe1 + dt - 1)
                                a.activation(out=sqb[:, (dt % 2) * Lc:(dt % 2 + 1) * Lc],
                                             in_=bufH[:, dt * Lc:(dt + 1) * Lc],
                                             func=AF.Square).then_inc(sem["ACT"], 1)
                                C.inc("ACT")

                        @block.tensor
                        def _(t):
                            t = W(t)
                            for dt in range(DT):
                                t.wait_ge(sem["ACT"], act3 + dt + 1)
                                for fc in range(2):
                                    sl = slice((dt % 2) * Lc + fc * 512, (dt % 2) * Lc + (fc + 1) * 512)
                                    t.matmul(out=pm[4 + fc][0:1, 0:512], lhsT=t_ones[:],
                                             rhs=sqb[:, sl], start=(dt == 0), stop=(dt == DT - 1))
                                t.sem_inc(sem["PE"], 1)
                        C.inc("PE", DT)

                        @block.vector
                        def _(v):
                            v = W(v)
                            v.wait_ge(sem["PE"], C.get("PE"))
                            for fc in range(2):
                                v.tensor_scalar(out=t_rows[0][:, fc * 512:(fc + 1) * 512],
                                                in0=pm[4 + fc][0:1, 0:512], scalar1=1.0 / D, scalar2=None, op0=ALU.mult)
                            v.sem_inc(sem["DVE"], 1)
                        C.inc("DVE")

                        @block.scalar
                        def _(a):
                            a = W(a)
                            a.wait_ge(sem["DVE"], C.get("DVE"))
                            a.activation(out=t_rows[0][:], in_=t_rows[0][:], func=AF.Sqrt,
                                         bias=t_eps1[0:1, 0:1]).then_inc(sem["ACT"], 1)
                        C.inc("ACT")

                        @block.vector
                        def _(v):
                            v = W(v)
                            v.wait_ge(sem["ACT"], C.get("ACT"))
                            v.reciprocal(out=t_rows[0][:], in_=t_rows[0][:]).then_inc(sem["DVE"], 1)
                        C.inc("DVE")

                        @block.sync
                        def _(sp):
                            sp = W(sp)
                            sp.wait_ge(sem["DVE"], C.get("DVE"))
                            sp.dma_start(out=rows_bc[2:3, :], in_=t_rows[0][:]).then_inc(sem["LD"], 16); C.inc("LD")
                            sp.wait_ge(sem["LD"], 16 * C.get("LD"))
                            sp.dma_start(out=r2B[:], in_=rows_bc[2:3, :].broadcast_to([128, Lc])).then_inc(sem["LD"], 16); C.inc("LD")

                        @block.vector
                        def _(v):
                            v = W(v)
                            v.wait_ge(sem["LD"], 16 * C.get("LD"))
                            for dt in range(DT):
                                v.scalar_tensor_tensor(out=rnT[:, dt * Lc:(dt + 1) * Lc],
                                                       in0=bufH[:, dt * Lc:(dt + 1) * Lc],
                                                       scalar=NRMW(dt), in1=r2B[:],
                                                       op0=ALU.mult, op1=ALU.mult)
                            v.sem_inc(sem["DVE"], 1)
                        C.inc("DVE")
                        barrier()
                        if _STOP[0] == 'A':
                            return nc

                    with ExitStack() as sbk:
                        slabW = sbk.enter_context(nc.sbuf_tensor(f"slabW_{c}", [128, ET * Ec], BF16))
                        ebuf = [sbk.enter_context(nc.sbuf_tensor(f"ebB{i}_{c}", [128, 512], BF16)) for i in range(4)]

                        gd1 = C.get("GD")
                        @block.gpsimd
                        def _(g):
                            g = W(g)
                            for kt in range(DT):
                                g.dma_start(out=slabW[:, kt * Ec:(kt + 1) * Ec],
                                            in_=skip_wT[kt * 128:(kt + 1) * 128, :]).then_inc(sem["GD"], 16)
                                C.inc("GD")

                        units = [(fc, et) for fc in range(2) for et in range(ET)]
                        pe1 = C.get("PE")
                        act4 = C.get("ACT")
                        eb0 = [C.get("E%d" % i) for i in range(4)]
                        @block.tensor
                        def _(t):
                            t = W(t)
                            t.wait_ge(sem["GD"], 16 * C.get("GD"))
                            for u, (fc, et) in enumerate(units):
                                if u >= 4:
                                    t.wait_ge(sem["ACT"], act4 + (u - 3))
                                for kt in range(DT):
                                    t.matmul(out=pm[u % 4][:, 0:512],
                                             lhsT=slabW[:, kt * Ec + et * 128:kt * Ec + (et + 1) * 128],
                                             rhs=rnT[:, kt * Lc + fc * 512:kt * Lc + (fc + 1) * 512],
                                             start=(kt == 0), stop=(kt == DT - 1))
                                t.sem_inc(sem["PE"], 1)
                        C.inc("PE", len(units))

                        @block.scalar
                        def _(a):
                            a = W(a)
                            for u, (fc, et) in enumerate(units):
                                a.wait_ge(sem["PE"], pe1 + u + 1)
                                if u >= 4:
                                    a.wait_ge(sem["E%d" % (u % 4)], 16 * (eb0[u % 4] + (u - 4) // 4 + 1))
                                a.activation(out=ebuf[u % 4][:], in_=pm[u % 4][:, 0:512],
                                             func=AF.Silu).then_inc(sem["ACT"], 1)
                                C.inc("ACT")

                        @block.sync
                        def _(sp):
                            sp = W(sp)
                            for u, (fc, et) in enumerate(units):
                                sp.wait_ge(sem["ACT"], act4 + u + 1)
                                sp.dma_start(out=spill_ss[et * 128:(et + 1) * 128, fc * 512:(fc + 1) * 512],
                                             in_=ebuf[u % 4][:]).then_inc(sem["E%d" % (u % 4)], 16)
                                C.inc("E%d" % (u % 4))
                        barrier()
                        if _STOP[0] == 'B':
                            return nc

                    with ExitStack() as sck:
                        slabW = sck.enter_context(nc.sbuf_tensor(f"slabWc_{c}", [128, ET * Ec], BF16))
                        xin = [sck.enter_context(nc.sbuf_tensor(f"xin{i}_{c}", [128, Lc + 3], F32)) for i in range(2)]
                        cv1 = sck.enter_context(nc.sbuf_tensor(f"cv1_{c}", [128, Lc], F32))
                        cv2 = sck.enter_context(nc.sbuf_tensor(f"cv2_{c}", [128, Lc], F32))

                        @block.gpsimd
                        def _(g):
                            g = W(g)
                            for kt in range(DT):
                                g.dma_start(out=slabW[:, kt * Ec:(kt + 1) * Ec],
                                            in_=in_wT[kt * 128:(kt + 1) * 128, :]).then_inc(sem["GD"], 16)
                                C.inc("GD")

                        pe2 = C.get("PE")
                        act5 = C.get("ACT")
                        dve2 = C.get("DVE")
                        @block.tensor
                        def _(t):
                            t = W(t)
                            t.wait_ge(sem["GD"], 16 * C.get("GD"))
                            for et in range(ET):
                                for fc in range(2):
                                    u = et * 2 + fc
                                    if u >= 4:
                                        up = u - 4
                                        t.wait_ge(sem["ACT"], act5 + 4 * (up // 2) + (up % 2) + 1)
                                    for kt in range(DT):
                                        t.matmul(out=pm[u % 4][:, 0:512],
                                                 lhsT=slabW[:, kt * Ec + et * 128:kt * Ec + (et + 1) * 128],
                                                 rhs=rnT[:, kt * Lc + fc * 512:kt * Lc + (fc + 1) * 512],
                                                 start=(kt == 0), stop=(kt == DT - 1))
                                    t.sem_inc(sem["PE"], 1)
                        C.inc("PE", ET * 2)

                        @block.scalar
                        def _(a):
                            a = W(a)
                            for et in range(ET):
                                sl = et % 2
                                for fc in range(2):
                                    u = et * 2 + fc
                                    a.wait_ge(sem["PE"], pe2 + u + 1)
                                    if et >= 2:
                                        a.wait_ge(sem["DVE"], dve2 + 3 * (et - 1))
                                    a.activation(out=xin[sl][:, 3 + fc * 512:3 + (fc + 1) * 512],
                                                 in_=pm[u % 4][:, 0:512], func=AF.Copy).then_inc(sem["ACT"], 1)
                                    C.inc("ACT")
                                a.wait_ge(sem["DVE"], dve2 + 3 * et + 1)
                                a.activation(out=cv1[:], in_=xin[sl][:, 0:Lc], func=AF.Copy,
                                             scale=t_convw[:, et * 4:et * 4 + 1]).then_inc(sem["ACT"], 1)
                                C.inc("ACT")
                                a.wait_ge(sem["DVE"], dve2 + 3 * et + 3)
                                a.activation(out=xT[:, et * Lc:(et + 1) * Lc], in_=cv1[:],
                                             func=AF.Silu, bias=CONVB(et)).then_inc(sem["ACT"], 1)
                                C.inc("ACT")

                        @block.vector
                        def _(v):
                            v = W(v)
                            for et in range(ET):
                                sl = et % 2
                                v.wait_ge(sem["ACT"], act5 + 4 * et + 2)
                                if c == 0:
                                    v.memset(xin[sl][:, 0:3], 0.0)
                                else:
                                    v.tensor_copy(out=xin[sl][:, 0:3], in_=t_tail[:, et * 4:et * 4 + 3])
                                v.sem_inc(sem["DVE"], 1)
                                v.wait_ge(sem["ACT"], act5 + 4 * et + 3)
                                v.scalar_tensor_tensor(out=cv2[:], in0=xin[sl][:, 1:1 + Lc],
                                                       scalar=t_convw[:, et * 4 + 1:et * 4 + 2], in1=cv1[:],
                                                       op0=ALU.mult, op1=ALU.add)
                                v.scalar_tensor_tensor(out=cv1[:], in0=xin[sl][:, 2:2 + Lc],
                                                       scalar=t_convw[:, et * 4 + 2:et * 4 + 3], in1=cv2[:],
                                                       op0=ALU.mult, op1=ALU.add)
                                v.sem_inc(sem["DVE"], 1)
                                v.scalar_tensor_tensor(out=cv1[:], in0=xin[sl][:, 3:3 + Lc],
                                                       scalar=t_convw[:, et * 4 + 3:et * 4 + 4], in1=cv1[:],
                                                       op0=ALU.mult, op1=ALU.add)
                                if c == 0:
                                    v.tensor_copy(out=t_tail[:, et * 4:et * 4 + 3], in_=xin[sl][:, Lc:Lc + 3])
                                v.sem_inc(sem["DVE"], 1)
                        C.inc("DVE", 3 * ET)
                        barrier()
                        if _STOP[0] == 'C':
                            return nc

                with ExitStack() as sdk:
                    t_zTb = sdk.enter_context(nc.sbuf_tensor(f"t_zTb_{c}", [128, Lc], BF16))
                    sd1 = sdk.enter_context(ExitStack())
                    xTb = sd1.enter_context(nc.sbuf_tensor(f"xTb_{c}", [128, ET * Lc], BF16))
                    t_w1s = sd1.enter_context(nc.sbuf_tensor(f"t_w1s_{c}", [128, ET * 128], BF16))
                    t_wbs = sd1.enter_context(nc.sbuf_tensor(f"t_wbs_{c}", [128, ET * N], BF16))
                    t_wcs = sd1.enter_context(nc.sbuf_tensor(f"t_wcs_{c}", [128, ET * N], BF16))
                    t_zT = sd1.enter_context(nc.sbuf_tensor(f"t_zT_{c}", [128, Lc], F32))
                    t_bm = sd1.enter_context(nc.sbuf_tensor(f"t_bm_{c}", [16, Lc], F32))
                    t_cm = sd1.enter_context(nc.sbuf_tensor(f"t_cm_{c}", [16, Lc], F32))
                    t_bm16 = sd1.enter_context(nc.sbuf_tensor(f"t_bm16_{c}", [16, Lc], BF16))
                    t_cm16 = sd1.enter_context(nc.sbuf_tensor(f"t_cm16_{c}", [16, Lc], BF16))
                    t_zp = sd1.enter_context(nc.sbuf_tensor(f"t_zp_{c}", [128, Lc], F32))
                    t_bp = sd1.enter_context(nc.sbuf_tensor(f"t_bp_{c}", [16, Lc], F32))
                    t_cp = sd1.enter_context(nc.sbuf_tensor(f"t_cp_{c}", [16, Lc], F32))

                    @block.gpsimd
                    def _(g):
                        g = W(g)
                        for et in range(ET):
                            g.dma_start(out=xTb[:, et * Lc:(et + 1) * Lc],
                                        in_=xT[:, et * Lc:(et + 1) * Lc]).then_inc(sem["GD"], 16)
                            C.inc("GD")
                        for kt in range(ET):
                            g.dma_start(out=t_w1s[:, kt * 128:(kt + 1) * 128],
                                        in_=wd1T_in[kt * 128:(kt + 1) * 128, :]).then_inc(sem["GD"], 16)
                            C.inc("GD")
                            g.dma_start(out=t_wbs[:, kt * N:(kt + 1) * N],
                                        in_=wBT_in[kt * 128:(kt + 1) * 128, :]).then_inc(sem["GD"], 16)
                            C.inc("GD")
                            g.dma_start(out=t_wcs[:, kt * N:(kt + 1) * N],
                                        in_=wCT_in[kt * 128:(kt + 1) * 128, :]).then_inc(sem["GD"], 16)
                            C.inc("GD")

                    pe3 = C.get("PE")
                    dve3 = C.get("DVE")
                    @block.tensor
                    def _(t):
                        t = W(t)
                        t.wait_ge(sem["GD"], 16 * C.get("GD"))
                        for fc in range(2):
                            for kt in range(ET):
                                t.matmul(out=pm[4 + fc][:, 0:512],
                                         lhsT=t_w1s[:, kt * 128:(kt + 1) * 128],
                                         rhs=xTb[:, kt * Lc + fc * 512:kt * Lc + (fc + 1) * 512],
                                         start=(kt == 0), stop=(kt == ET - 1))
                            t.sem_inc(sem["PE"], 1)
                        for fc in range(2):
                            if fc == 1:
                                t.wait_ge(sem["DVE"], dve3 + 3)
                            for kt in range(ET):
                                t.matmul(out=pm[6][0:16, 0:512],
                                         lhsT=t_wbs[:, kt * N:(kt + 1) * N],
                                         rhs=xTb[:, kt * Lc + fc * 512:kt * Lc + (fc + 1) * 512],
                                         start=(kt == 0), stop=(kt == ET - 1))
                                t.matmul(out=pm[7][0:16, 0:512],
                                         lhsT=t_wcs[:, kt * N:(kt + 1) * N],
                                         rhs=xTb[:, kt * Lc + fc * 512:kt * Lc + (fc + 1) * 512],
                                         start=(kt == 0), stop=(kt == ET - 1))
                            t.sem_inc(sem["PE"], 1)
                    C.inc("PE", 4)

                    @block.vector
                    def _(v):
                        v = W(v)
                        v.wait_ge(sem["PE"], pe3 + 1)
                        v.tensor_copy(out=t_zT[:, 0:512], in_=pm[4][:, 0:512])
                        v.sem_inc(sem["DVE"], 1)
                        v.wait_ge(sem["PE"], pe3 + 2)
                        v.tensor_copy(out=t_zT[:, 512:1024], in_=pm[5][:, 0:512])
                        v.sem_inc(sem["DVE"], 1)
                        v.wait_ge(sem["PE"], pe3 + 3)
                        v.tensor_copy(out=t_bm[:, 0:512], in_=pm[6][0:16, 0:512])
                        v.tensor_copy(out=t_cm[:, 0:512], in_=pm[7][0:16, 0:512])
                        v.sem_inc(sem["DVE"], 1)
                        v.wait_ge(sem["PE"], pe3 + 4)
                        v.tensor_copy(out=t_bm[:, 512:1024], in_=pm[6][0:16, 0:512])
                        v.tensor_copy(out=t_cm[:, 512:1024], in_=pm[7][0:16, 0:512])
                        v.sem_inc(sem["DVE"], 1)
                    C.inc("DVE", 4)

                    @block.sync
                    def _(sp):
                        sp = W(sp)
                        sp.wait_ge(sem["DVE"], C.get("DVE"))
                        base0 = (0 * 2 + c) * 160
                        base1 = (1 * 2 + c) * 160
                        with sp.If_eq(reg_role, 0):
                            sp.dma_start(out=sh_zbc[base0:base0 + 128, :], in_=t_zT[:]).then_inc(sem["LD"], 16)
                            sp.dma_start(out=sh_zbc[base0 + 128:base0 + 144, :], in_=t_bm[:]).then_inc(sem["LD"], 16)
                            sp.dma_start(out=sh_zbc[base0 + 144:base0 + 160, :], in_=t_cm[:]).then_inc(sem["LD"], 16)
                        with sp.Else():
                            sp.dma_start(out=sh_zbc[base1:base1 + 128, :], in_=t_zT[:]).then_inc(sem["LD"], 16)
                            sp.dma_start(out=sh_zbc[base1 + 128:base1 + 144, :], in_=t_bm[:]).then_inc(sem["LD"], 16)
                            sp.dma_start(out=sh_zbc[base1 + 144:base1 + 160, :], in_=t_cm[:]).then_inc(sem["LD"], 16)
                        C.inc("LD", 3)
                        sp.wait_ge(sem["LD"], 16 * C.get("LD"))
                        f0, f1 = fl_zbc(0, c), fl_zbc(1, c)
                        with sp.If_eq(reg_role, 0):
                            sp.dma_start(out=sh_flag[f0:f0 + 1, :], in_=t_flagv[:, 16 * f0:16 * (f0 + 1)]).then_inc(sem["LD"], 16)
                        with sp.Else():
                            sp.dma_start(out=sh_flag[f1:f1 + 1, :], in_=t_flagv[:, 16 * f1:16 * (f1 + 1)]).then_inc(sem["LD"], 16)
                        C.inc("LD")
                        sp.wait_ge(sem["LD"], 16 * C.get("LD"))
                        poll(sp, f0)
                        poll(sp, f1)
                        with sp.If_eq(reg_role, 0):
                            sp.dma_start(out=t_zp[:], in_=sh_zbc[base1:base1 + 128, :]).then_inc(sem["LD"], 16)
                            sp.dma_start(out=t_bp[:], in_=sh_zbc[base1 + 128:base1 + 144, :]).then_inc(sem["LD"], 16)
                            sp.dma_start(out=t_cp[:], in_=sh_zbc[base1 + 144:base1 + 160, :]).then_inc(sem["LD"], 16)
                        with sp.Else():
                            sp.dma_start(out=t_zp[:], in_=sh_zbc[base0:base0 + 128, :]).then_inc(sem["LD"], 16)
                            sp.dma_start(out=t_bp[:], in_=sh_zbc[base0 + 128:base0 + 144, :]).then_inc(sem["LD"], 16)
                            sp.dma_start(out=t_cp[:], in_=sh_zbc[base0 + 144:base0 + 160, :]).then_inc(sem["LD"], 16)
                        C.inc("LD", 3)

                    @block.vector
                    def _(v):
                        v = W(v)
                        v.wait_ge(sem["LD"], 16 * C.get("LD"))
                        v.tensor_tensor(out=t_zT[:], in0=t_zT[:], in1=t_zp[:], op=ALU.add)
                        v.tensor_tensor(out=t_bm[:], in0=t_bm[:], in1=t_bp[:], op=ALU.add)
                        v.tensor_tensor(out=t_cm[:], in0=t_cm[:], in1=t_cp[:], op=ALU.add)
                        v.tensor_scalar(out=t_zTb[:], in0=t_zT[:], scalar1=1.0, scalar2=None, op0=ALU.mult)
                        v.tensor_scalar(out=t_bm16[:], in0=t_bm[:], scalar1=1.0, scalar2=None, op0=ALU.mult)
                        v.tensor_scalar(out=t_cm16[:], in0=t_cm[:], scalar1=1.0, scalar2=None, op0=ALU.mult)
                        v.sem_inc(sem["DVE"], 1)
                    C.inc("DVE")

                    @block.sync
                    def _(sp):
                        sp = W(sp)
                        sp.wait_ge(sem["DVE"], C.get("DVE"))
                        sp.dma_start(out=bmcm_dr[0:16, :], in_=t_bm16[:]).then_inc(sem["LD"], 16)
                        sp.dma_start(out=bmcm_dr[16:32, :], in_=t_cm16[:]).then_inc(sem["LD"], 16)
                        C.inc("LD", 2)
                    barrier()
                    sd1.close()
                    if _STOP[0] == 'D':
                        return nc

                    t_w2s = sdk.enter_context(nc.sbuf_tensor(f"t_w2s_{c}", [128, ET * 128], BF16))
                    dT = sdk.enter_context(nc.sbuf_tensor(f"dT_{c}", [128, EB * Lc], F32))
                    dx16 = sdk.enter_context(nc.sbuf_tensor(f"dx16_{c}", [128, EB * Lc], BF16))
                    yblk = sdk.enter_context(nc.sbuf_tensor(f"yblk_{c}", [128, EB * Lc], F32))
                    abuf = [sdk.enter_context(nc.sbuf_tensor(f"abuf{i}_{c}", [128, Lc], F32)) for i in range(2)]
                    bbuf = [sdk.enter_context(nc.sbuf_tensor(f"bbuf{i}_{c}", [128, Lc], BF16)) for i in range(2)]
                    hbuf = [sdk.enter_context(nc.sbuf_tensor(f"hbuf{i}_{c}", [128, Lc], BF16)) for i in range(2)]
                    tmpb = [sdk.enter_context(nc.sbuf_tensor(f"tmpb{i}_{c}", [128, Lc], BF16)) for i in range(2)]
                    bcB = [sdk.enter_context(nc.sbuf_tensor(f"bcB{i}_{c}", [128, Lc], BF16)) for i in range(2)]
                    bcC = [sdk.enter_context(nc.sbuf_tensor(f"bcC{i}_{c}", [128, Lc], BF16)) for i in range(2)]
                    relub = sdk.enter_context(nc.sbuf_tensor(f"relub_{c}", [128, 512], F32))
                    actb = sdk.enter_context(nc.sbuf_tensor(f"actb_{c}", [128, 512], F32))
                    ssk = sdk.enter_context(nc.sbuf_tensor(f"ssk_{c}", [128, Lc], BF16))
                    ywt = sdk.enter_context(nc.sbuf_tensor(f"ywt_{c}", [128, Lc], F32))
                    ypt = sdk.enter_context(nc.sbuf_tensor(f"ypt_{c}", [128, Lc], BF16))

                    @block.gpsimd
                    def _(g):
                        g = W(g)
                        for kt in range(ET):
                            g.dma_start(out=t_w2s[:, kt * 128:(kt + 1) * 128],
                                        in_=wd2T_in[:, kt * 128:(kt + 1) * 128]).then_inc(sem["GD"], 16)
                            C.inc("GD")

                    for blk in range(NBLK):
                        pe4 = C.get("PE")
                        act6 = C.get("ACT")
                        dve4 = C.get("DVE")
                        drelu = lambda u: 5 * (u // 2) + 2 * (u % 2) + 1
                        dadd = lambda u: 5 * (u // 2) + 2 * (u % 2) + 2
                        @block.tensor
                        def _(t):
                            t = W(t)
                            t.wait_ge(sem["GD"], 16 * C.get("GD"))
                            t.wait_ge(sem["DVE"], dve4)
                            for el in range(EB):
                                et = blk * EB + el
                                for fc in range(2):
                                    u = el * 2 + fc
                                    if u >= 2:
                                        t.wait_ge(sem["DVE"], dve4 + drelu(u - 2))
                                    t.matmul(out=pm[4 + u % 2][:, 0:512],
                                             lhsT=t_w2s[:, et * 128:(et + 1) * 128],
                                             rhs=t_zTb[:, fc * 512:(fc + 1) * 512],
                                             start=True, stop=True)
                                    t.sem_inc(sem["PE"], 1)
                        C.inc("PE", EB * 2)

                        @block.scalar
                        def _(a):
                            a = W(a)
                            for el in range(EB):
                                et = blk * EB + el
                                for fc in range(2):
                                    u = el * 2 + fc
                                    a.wait_ge(sem["PE"], pe4 + u + 1)
                                    if u >= 1:
                                        a.wait_ge(sem["DVE"], dve4 + dadd(u - 1))
                                    a.activation(out=actb[:], in_=pm[4 + u % 2][:, 0:512],
                                                 func=AF.Abs, bias=WD2B(et)).then_inc(sem["ACT"], 1)
                                    a.activation(out=actb[:], in_=actb[:], func=AF.Exp,
                                                 scale=-1.0).then_inc(sem["ACT"], 1)
                                    a.activation(out=actb[:], in_=actb[:], func=AF.Ln,
                                                 bias=t_one1[:]).then_inc(sem["ACT"], 1)
                                    C.inc("ACT", 3)

                        @block.vector
                        def _(v):
                            v = W(v)
                            for el in range(EB):
                                et = blk * EB + el
                                for fc in range(2):
                                    u = el * 2 + fc
                                    v.wait_ge(sem["ACT"], act6 + 3 * u + 3)
                                    v.tensor_scalar(out=relub[:], in0=pm[4 + u % 2][:, 0:512],
                                                    scalar1=WD2B(et), scalar2=0.0, op0=ALU.add, op1=ALU.max)
                                    v.sem_inc(sem["DVE"], 1)
                                    v.tensor_tensor(out=dT[:, el * Lc + fc * 512:el * Lc + (fc + 1) * 512],
                                                    in0=relub[:], in1=actb[:], op=ALU.add)
                                    v.sem_inc(sem["DVE"], 1)
                                v.tensor_tensor(out=dx16[:, el * Lc:(el + 1) * Lc],
                                                in0=dT[:, el * Lc:(el + 1) * Lc],
                                                in1=xT[:, (blk * EB + el) * Lc:(blk * EB + el + 1) * Lc],
                                                op=ALU.mult)
                                v.sem_inc(sem["DVE"], 1)
                        C.inc("DVE", EB * 5)

                        dve5 = C.get("DVE")
                        act7 = C.get("ACT")
                        pl0 = C.get("PL")
                        bc0 = [C.get("BC0"), C.get("BC1")]
                        bccnt = lambda n: 16 * 2 * (bc0[n % 2] // 2 + n // 2 + 1)
                        @block.sync
                        def _(sp):
                            sp = W(sp)
                            for n in range(N):
                                if n >= 2:
                                    sp.wait_ge(sem["DVE"], dve5 + (n - 1) * EB * 3)
                                sp.dma_start(out=bcB[n % 2][:],
                                             in_=bmcm_dr[n:n + 1, :].broadcast_to([128, Lc])).then_inc(sem["BC%d" % (n % 2)], 16)
                                sp.dma_start(out=bcC[n % 2][:],
                                             in_=bmcm_dr[16 + n:17 + n, :].broadcast_to([128, Lc])).then_inc(sem["BC%d" % (n % 2)], 16)
                                C.inc("BC%d" % (n % 2), 2)

                        @block.scalar
                        def _(a):
                            a = W(a)
                            a.wait_ge(sem["DVE"], dve5)
                            for n in range(N):
                                for el in range(EB):
                                    u = n * EB + el
                                    et = blk * EB + el
                                    if u >= 2:
                                        a.wait_ge(sem["DVE"], dve5 + 3 * (u - 1) - 1)
                                    a.activation(out=abuf[u % 2][:], in_=dT[:, el * Lc:(el + 1) * Lc],
                                                 func=AF.Exp, scale=t_negA[:, et * N + n:et * N + n + 1]
                                                 ).then_inc(sem["ACT"], 1)
                                    C.inc("ACT")

                        @block.vector
                        def _(v):
                            v = W(v)
                            for n in range(N):
                                for el in range(EB):
                                    u = n * EB + el
                                    et = blk * EB + el
                                    if el == 0:
                                        v.wait_ge(sem["BC%d" % (n % 2)], bccnt(n))
                                    v.tensor_tensor(out=bbuf[u % 2][:], in0=dx16[:, el * Lc:(el + 1) * Lc],
                                                    in1=bcB[n % 2][:], op=ALU.mult)
                                    v.sem_inc(sem["DVE"], 1)
                                    v.wait_ge(sem["ACT"], act7 + u + 1)
                                    init = 0.0 if c == 0 else t_state[:, et * N + n:et * N + n + 1]
                                    v.tensor_tensor_scan(out=hbuf[u % 2][:], data0=abuf[u % 2][:],
                                                         data1=bbuf[u % 2][:], initial=init,
                                                         op0=ALU.mult, op1=ALU.add)
                                    v.sem_inc(sem["DVE"], 1)
                                    if c == 0 and NCH > 1:
                                        v.wait_ge(sem["DVE"], dve5 + 3 * u + 2)
                                        v.tensor_copy(out=t_state[:, et * N + n:et * N + n + 1],
                                                      in_=hbuf[u % 2][:, Lc - 1:Lc])
                                    if u >= 2:
                                        v.wait_ge(sem["PL"], pl0 + (u - 1))
                                    v.tensor_tensor(out=tmpb[u % 2][:], in0=hbuf[u % 2][:],
                                                    in1=bcC[n % 2][:], op=ALU.mult)
                                    v.sem_inc(sem["DVE"], 1)
                        C.inc("DVE", N * EB * 3)

                        @block.gpsimd
                        def _(g):
                            g = W(g)
                            for n in range(N):
                                for el in range(EB):
                                    u = n * EB + el
                                    g.wait_ge(sem["DVE"], dve5 + 3 * u + 3)
                                    if n == 0:
                                        g.tensor_copy(out=yblk[:, el * Lc:(el + 1) * Lc], in_=tmpb[u % 2][:])
                                    else:
                                        g.tensor_tensor(out=yblk[:, el * Lc:(el + 1) * Lc],
                                                        in0=yblk[:, el * Lc:(el + 1) * Lc],
                                                        in1=tmpb[u % 2][:], op=ALU.add)
                                    g.sem_inc(sem["PL"], 1)
                        C.inc("PL", N * EB)

                        sk0 = C.get("SK")
                        yp0 = C.get("YP")
                        dve6 = C.get("DVE")
                        @block.sync
                        def _(sp):
                            sp = W(sp)
                            for el in range(EB):
                                et = blk * EB + el
                                if el >= 1:
                                    sp.wait_ge(sem["DVE"], dve6 + 2 * el)
                                sp.dma_start(out=ssk[:], in_=spill_ss[et * 128:(et + 1) * 128, :]).then_inc(sem["SK"], 16)
                                C.inc("SK")
                                sp.wait_ge(sem["DVE"], dve6 + 2 * el + 2)
                                sp.dma_start(out=spill_yp[et * 128:(et + 1) * 128, :], in_=ypt[:]).then_inc(sem["YP"], 16)
                                C.inc("YP")

                        @block.vector
                        def _(v):
                            v = W(v)
                            v.wait_ge(sem["PL"], C.get("PL"))
                            for el in range(EB):
                                et = blk * EB + el
                                v.scalar_tensor_tensor(out=ywt[:], in0=xT[:, et * Lc:(et + 1) * Lc],
                                                       scalar=WDC(et), in1=yblk[:, el * Lc:(el + 1) * Lc],
                                                       op0=ALU.mult, op1=ALU.add)
                                v.sem_inc(sem["DVE"], 1)
                                v.wait_ge(sem["SK"], 16 * (sk0 + el + 1))
                                if el >= 1:
                                    v.wait_ge(sem["YP"], 16 * (yp0 + el))
                                v.tensor_tensor(out=ypt[:], in0=ywt[:], in1=ssk[:], op=ALU.mult)
                                v.sem_inc(sem["DVE"], 1)
                        C.inc("DVE", 2 * EB)
                    barrier()
                    if _STOP[0] == 'E':
                        return nc

                with ExitStack() as sfk:
                    slabO = sfk.enter_context(nc.sbuf_tensor(f"slabO_{c}", [128, ET * D], BF16))
                    ypb = sfk.enter_context(nc.sbuf_tensor(f"ypb_{c}", [128, ET * Lc], BF16))
                    ofb = [sfk.enter_context(nc.sbuf_tensor(f"ofb{i}_{c}", [128, 512], F32)) for i in range(4)]

                    @block.gpsimd
                    def _(g):
                        g = W(g)
                        for kt in range(ET):
                            g.dma_start(out=slabO[:, kt * D:(kt + 1) * D],
                                        in_=out_wT[kt * 128:(kt + 1) * 128, :]).then_inc(sem["GD"], 16)
                            C.inc("GD")

                    ld4 = C.get("LD")
                    @block.sync
                    def _(sp):
                        sp = W(sp)
                        for kt in range(ET):
                            sp.dma_start(out=ypb[:, kt * Lc:(kt + 1) * Lc],
                                         in_=spill_yp[kt * 128:(kt + 1) * 128, :]).then_inc(sem["LD"], 16)
                            C.inc("LD")

                    units = [(fc, dt) for fc in range(2) for dt in range(DT)]
                    pe5 = C.get("PE")
                    dve7 = C.get("DVE")
                    ef0 = [C.get("E%d" % i) for i in range(4)]
                    @block.tensor
                    def _(t):
                        t = W(t)
                        t.wait_ge(sem["GD"], 16 * C.get("GD"))
                        t.wait_ge(sem["LD"], 16 * (ld4 + ET))
                        for u, (fc, dt) in enumerate(units):
                            if u >= 4:
                                t.wait_ge(sem["DVE"], dve7 + (u - 3))
                            for kt in range(ET):
                                t.matmul(out=pm[u % 4][:, 0:512],
                                         lhsT=slabO[:, kt * D + dt * 128:kt * D + (dt + 1) * 128],
                                         rhs=ypb[:, kt * Lc + fc * 512:kt * Lc + (fc + 1) * 512],
                                         start=(kt == 0), stop=(kt == ET - 1))
                            t.sem_inc(sem["PE"], 1)
                    C.inc("PE", len(units))

                    @block.vector
                    def _(v):
                        v = W(v)
                        for u, (fc, dt) in enumerate(units):
                            v.wait_ge(sem["PE"], pe5 + u + 1)
                            if u >= 4:
                                v.wait_ge(sem["E%d" % (u % 4)], 16 * (ef0[u % 4] + (u - 4) // 4 + 1))
                            v.tensor_copy(out=ofb[u % 4][:], in_=pm[u % 4][:, 0:512])
                            v.sem_inc(sem["DVE"], 1)
                    C.inc("DVE", len(units))

                    @block.sync
                    def _(sp):
                        sp = W(sp)
                        for u, (fc, dt) in enumerate(units):
                            sp.wait_ge(sem["DVE"], dve7 + u + 1)
                            r0 = (0 * 2 + c) * D + dt * 128
                            r1 = (1 * 2 + c) * D + dt * 128
                            with sp.If_eq(reg_role, 0):
                                sp.dma_start(out=sh_out[r0:r0 + 128, fc * 512:(fc + 1) * 512],
                                             in_=ofb[u % 4][:]).then_inc(sem["E%d" % (u % 4)], 16)
                            with sp.Else():
                                sp.dma_start(out=sh_out[r1:r1 + 128, fc * 512:(fc + 1) * 512],
                                             in_=ofb[u % 4][:]).then_inc(sem["E%d" % (u % 4)], 16)
                            C.inc("E%d" % (u % 4))
                        for i in range(4):
                            sp.wait_ge(sem["E%d" % i], 16 * C.get("E%d" % i))
                        f0, f1 = fl_out(0, c), fl_out(1, c)
                        with sp.If_eq(reg_role, 0):
                            sp.dma_start(out=sh_flag[f0:f0 + 1, :], in_=t_flagv[:, 16 * f0:16 * (f0 + 1)]).then_inc(sem["LD"], 16)
                        with sp.Else():
                            sp.dma_start(out=sh_flag[f1:f1 + 1, :], in_=t_flagv[:, 16 * f1:16 * (f1 + 1)]).then_inc(sem["LD"], 16)
                        C.inc("LD")
                    barrier()
                    if _STOP[0] == 'F' and c == NCH - 1:
                        return nc

        with ExitStack() as sgk:
            x1T = sgk.enter_context(nc.sbuf_tensor("x1T", [128, DT * Lc], F32))
            scons = sgk.enter_context(ExitStack())
            tA = scons.enter_context(nc.sbuf_tensor("tA", [128, Lc], F32))
            tB = scons.enter_context(nc.sbuf_tensor("tB", [128, Lc], F32))
            tR = scons.enter_context(nc.sbuf_tensor("tR", [128, Lc], F32))
            tH = scons.enter_context(nc.sbuf_tensor("tH", [128, Lc], BF16))

            @block.sync
            def _(sp):
                sp = W(sp)
                sp.wait_ge(sem["LD"], 16 * C.get("LD"))
                for f in [fl_out(0, 0), fl_out(0, 1), fl_out(1, 0), fl_out(1, 1)]:
                    poll(sp, f)

            dve8 = C.get("DVE")
            ca0, cb0, cr0, ch0 = C.get("CA"), C.get("CB"), C.get("CR"), C.get("CH")
            @block.sync
            def _(sp):
                sp = W(sp)
                for dt in range(DT):
                    if dt >= 1:
                        sp.wait_ge(sem["DVE"], dve8 + 3 * dt)
                    with sp.If_eq(reg_role, 0):
                        sp.dma_start(out=tA[:], in_=sh_out[0 * D + dt * 128:0 * D + (dt + 1) * 128, :]).then_inc(sem["CA"], 16)
                        sp.dma_start(out=tB[:], in_=sh_out[2 * D + dt * 128:2 * D + (dt + 1) * 128, :]).then_inc(sem["CB"], 16)
                        sp.dma_start(out=tR[:], in_=residT[dt * 128:(dt + 1) * 128, 0:Lc]).then_inc(sem["CR"], 16)
                        sp.dma_start(out=tH[:], in_=spill_h[dt * 128:(dt + 1) * 128, :]).then_inc(sem["CH"], 16)
                    with sp.Else():
                        sp.dma_start(out=tA[:], in_=sh_out[3 * D + dt * 128:3 * D + (dt + 1) * 128, :]).then_inc(sem["CA"], 16)
                        sp.dma_start(out=tB[:], in_=sh_out[1 * D + dt * 128:1 * D + (dt + 1) * 128, :]).then_inc(sem["CB"], 16)
                        sp.dma_start(out=tR[:], in_=residT[dt * 128:(dt + 1) * 128, Lc:2 * Lc]).then_inc(sem["CR"], 16)
                        sp.dma_start(out=tH[:], in_=spill_h[D + dt * 128:D + (dt + 1) * 128, :]).then_inc(sem["CH"], 16)
                    C.inc("CA"); C.inc("CB"); C.inc("CR"); C.inc("CH")

            @block.vector
            def _(v):
                v = W(v)
                for dt in range(DT):
                    v.wait_ge(sem["CA"], 16 * (ca0 + dt + 1))
                    v.wait_ge(sem["CB"], 16 * (cb0 + dt + 1))
                    v.wait_ge(sem["CR"], 16 * (cr0 + dt + 1))
                    v.wait_ge(sem["CH"], 16 * (ch0 + dt + 1))
                    v.tensor_tensor(out=tA[:], in0=tA[:], in1=tB[:], op=ALU.add)
                    v.sem_inc(sem["DVE"], 1)
                    v.tensor_tensor(out=tA[:], in0=tA[:], in1=tH[:], op=ALU.add)
                    v.sem_inc(sem["DVE"], 1)
                    v.scalar_tensor_tensor(out=x1T[:, dt * Lc:(dt + 1) * Lc], in0=tA[:],
                                           scalar=GGA(dt), in1=tR[:], op0=ALU.mult, op1=ALU.add)
                    v.sem_inc(sem["DVE"], 1)
            C.inc("DVE", 3 * DT)
            barrier()
            scons.close()
            if _STOP[0] == 'X':
                return nc

            with ExitStack() as sg2:
                h2 = sg2.enter_context(nc.sbuf_tensor("h2", [128, DT * Lc], BF16))
                sgLN = sg2.enter_context(ExitStack())
                b16g = sgLN.enter_context(nc.sbuf_tensor("b16g", [128, 2 * Lc], BF16))
                sqg = sgLN.enter_context(nc.sbuf_tensor("sqg", [128, 2 * Lc], BF16))
                mB2 = sgLN.enter_context(nc.sbuf_tensor("mB2", [128, Lc], F32))
                rB2 = sgLN.enter_context(nc.sbuf_tensor("rB2", [128, Lc], F32))
                tvg = sgLN.enter_context(nc.sbuf_tensor("tvg", [128, Lc], F32))

                pe6 = C.get("PE")
                gg0 = [C.get("GB0"), C.get("GB1")]
                gcnt2 = lambda dt: 16 * (gg0[dt % 2] + dt // 2 + 1)
                @block.gpsimd
                def _(g):
                    g = W(g)
                    for dt in range(DT):
                        if dt >= 2:
                            g.wait_ge(sem["PE"], pe6 + dt - 1)
                        g.dma_start(out=b16g[:, (dt % 2) * Lc:(dt % 2 + 1) * Lc],
                                    in_=x1T[:, dt * Lc:(dt + 1) * Lc]).then_inc(sem["GB%d" % (dt % 2)], 16)
                        C.inc("GB%d" % (dt % 2))

                act8 = C.get("ACT")
                @block.scalar
                def _(a):
                    a = W(a)
                    for dt in range(DT):
                        if dt >= 2:
                            a.wait_ge(sem["PE"], pe6 + dt - 1)
                        a.activation(out=sqg[:, (dt % 2) * Lc:(dt % 2 + 1) * Lc],
                                     in_=x1T[:, dt * Lc:(dt + 1) * Lc],
                                     func=AF.Square).then_inc(sem["ACT"], 1)
                        C.inc("ACT")

                @block.tensor
                def _(t):
                    t = W(t)
                    for dt in range(DT):
                        t.wait_ge(sem["GB%d" % (dt % 2)], gcnt2(dt))
                        t.wait_ge(sem["ACT"], act8 + dt + 1)
                        for fc in range(2):
                            sl = slice((dt % 2) * Lc + fc * 512, (dt % 2) * Lc + (fc + 1) * 512)
                            t.matmul(out=pm[4 + fc][0:1, 0:512], lhsT=t_ones[:],
                                     rhs=b16g[:, sl], start=(dt == 0), stop=(dt == DT - 1))
                            t.matmul(out=pm[6 + fc][0:1, 0:512], lhsT=t_ones[:],
                                     rhs=sqg[:, sl], start=(dt == 0), stop=(dt == DT - 1))
                        t.sem_inc(sem["PE"], 1)
                C.inc("PE", DT)

                @block.vector
                def _(v):
                    v = W(v)
                    v.wait_ge(sem["PE"], C.get("PE"))
                    for fc in range(2):
                        v.tensor_scalar(out=t_rows[0][:, fc * 512:(fc + 1) * 512],
                                        in0=pm[4 + fc][0:1, 0:512], scalar1=1.0 / D, scalar2=None, op0=ALU.mult)
                        v.tensor_scalar(out=t_rows[1][:, fc * 512:(fc + 1) * 512],
                                        in0=pm[6 + fc][0:1, 0:512], scalar1=1.0 / D, scalar2=None, op0=ALU.mult)
                    v.tensor_tensor(out=t_rows[2][:], in0=t_rows[0][:], in1=t_rows[0][:], op=ALU.mult)
                    v.tensor_tensor(out=t_rows[1][:], in0=t_rows[1][:], in1=t_rows[2][:], op=ALU.subtract)
                    v.sem_inc(sem["DVE"], 1)
                C.inc("DVE")

                @block.scalar
                def _(a):
                    a = W(a)
                    a.wait_ge(sem["DVE"], C.get("DVE"))
                    a.activation(out=t_rows[1][:], in_=t_rows[1][:], func=AF.Sqrt,
                                 bias=t_eps1[0:1, 0:1]).then_inc(sem["ACT"], 1)
                C.inc("ACT")

                @block.vector
                def _(v):
                    v = W(v)
                    v.wait_ge(sem["ACT"], C.get("ACT"))
                    v.reciprocal(out=t_rows[1][:], in_=t_rows[1][:]).then_inc(sem["DVE"], 1)
                C.inc("DVE")

                @block.sync
                def _(sp):
                    sp = W(sp)
                    sp.wait_ge(sem["DVE"], C.get("DVE"))
                    sp.dma_start(out=rows_bc[3:4, :], in_=t_rows[0][:]).then_inc(sem["LD"], 16); C.inc("LD")
                    sp.dma_start(out=rows_bc[4:5, :], in_=t_rows[1][:]).then_inc(sem["LD"], 16); C.inc("LD")
                    sp.wait_ge(sem["LD"], 16 * C.get("LD"))
                    sp.dma_start(out=mB2[:], in_=rows_bc[3:4, :].broadcast_to([128, Lc])).then_inc(sem["LD"], 16); C.inc("LD")
                    sp.dma_start(out=rB2[:], in_=rows_bc[4:5, :].broadcast_to([128, Lc])).then_inc(sem["LD"], 16); C.inc("LD")

                @block.vector
                def _(v):
                    v = W(v)
                    v.wait_ge(sem["LD"], 16 * C.get("LD"))
                    for dt in range(DT):
                        v.tensor_tensor(out=tvg[:], in0=x1T[:, dt * Lc:(dt + 1) * Lc], in1=mB2[:], op=ALU.subtract)
                        v.tensor_tensor(out=tvg[:], in0=tvg[:], in1=rB2[:], op=ALU.mult)
                        v.tensor_scalar(out=h2[:, dt * Lc:(dt + 1) * Lc], in0=tvg[:],
                                        scalar1=GCB(dt), scalar2=GSB(dt), op0=ALU.mult, op1=ALU.add)
                    v.sem_inc(sem["DVE"], 1)
                C.inc("DVE")
                barrier()
                sgLN.close()
                if _STOP[0] == 'G1':
                    return nc

                with ExitStack() as sg3:
                    bss = sg3.enter_context(nc.sbuf_tensor("bss", [16, D], BF16))
                    brss = sg3.enter_context(nc.sbuf_tensor("brss", [96, D], BF16))
                    s16 = sg3.enter_context(nc.sbuf_tensor("s16", [16, Lc], BF16))
                    h96 = sg3.enter_context(nc.sbuf_tensor("h96", [96, Lc], BF16))
                    sg3b = sg3.enter_context(ExitStack())
                    wrs = sg3b.enter_context(nc.sbuf_tensor("wrs", [128, ET * 6], BF16))
                    ass = sg3b.enter_context(nc.sbuf_tensor("ass", [128, ET * N], BF16))
                    arts = sg3b.enter_context(nc.sbuf_tensor("arts", [128, ET * 96], BF16))
                    lgt = sg3b.enter_context(nc.sbuf_tensor("lgt", [128, Lc], F32))
                    gtok = sg3b.enter_context(nc.sbuf_tensor("gtok", [128, 48], F32))
                    gm = sg3b.enter_context(nc.sbuf_tensor("gm", [128, 24], F32))
                    ge_ = sg3b.enter_context(nc.sbuf_tensor("ge", [128, 48], F32))
                    gw = sg3b.enter_context(nc.sbuf_tensor("gw", [128, 48], F32))
                    gpad = sg3b.enter_context(nc.sbuf_tensor("gpad", [128, 128], F32))
                    wT6 = sg3b.enter_context(nc.sbuf_tensor("wT6", [6, Lc], F32))
                    w96 = sg3b.enter_context(nc.sbuf_tensor("w96", [96, Lc], F32))

                    @block.gpsimd
                    def _(g):
                        g = W(g)
                        for kt in range(DT):
                            g.dma_start(out=wrs[:, kt * 6:(kt + 1) * 6],
                                        in_=wrout_in[kt * 128:(kt + 1) * 128, :]).then_inc(sem["GD"], 16)
                            g.dma_start(out=ass[:, kt * N:(kt + 1) * N],
                                        in_=as_in[kt * 128:(kt + 1) * 128, :]).then_inc(sem["GD"], 16)
                            g.dma_start(out=arts[:, kt * 96:(kt + 1) * 96],
                                        in_=art_in[kt * 128:(kt + 1) * 128, :]).then_inc(sem["GD"], 16)
                            C.inc("GD", 3)
                        g.dma_start(out=bss[:], in_=bs_in).then_inc(sem["GD"], 16); C.inc("GD")
                        g.dma_start(out=brss[:], in_=brs_in).then_inc(sem["GD"], 16); C.inc("GD")

                    pe7 = C.get("PE")
                    @block.tensor
                    def _(t):
                        t = W(t)
                        t.wait_ge(sem["GD"], 16 * C.get("GD"))
                        for fc in range(2):
                            for kt in range(DT):
                                t.matmul(out=pm[4][0:6, 0:512], lhsT=wrs[:, kt * 6:(kt + 1) * 6],
                                         rhs=h2[:, kt * Lc + fc * 512:kt * Lc + (fc + 1) * 512],
                                         start=(kt == 0), stop=(kt == DT - 1))
                                t.matmul(out=pm[5][0:16, 0:512], lhsT=ass[:, kt * N:(kt + 1) * N],
                                         rhs=h2[:, kt * Lc + fc * 512:kt * Lc + (fc + 1) * 512],
                                         start=(kt == 0), stop=(kt == DT - 1))
                                t.matmul(out=pm[6][0:96, 0:512], lhsT=arts[:, kt * 96:(kt + 1) * 96],
                                         rhs=h2[:, kt * Lc + fc * 512:kt * Lc + (fc + 1) * 512],
                                         start=(kt == 0), stop=(kt == DT - 1))
                            t.sem_inc(sem["PE"], 1)
                            if fc == 0:
                                t.wait_ge(sem["DVE"], C.get("DVE") + 1)
                    C.inc("PE", 2)

                    @block.vector
                    def _(v):
                        v = W(v)
                        v.memset(lgt[:], 0.0)
                        for fc in range(2):
                            v.wait_ge(sem["PE"], pe7 + fc + 1)
                            v.tensor_scalar(out=lgt[0:6, fc * 512:(fc + 1) * 512], in0=pm[4][0:6, 0:512],
                                            scalar1=t_fvec[0:6, 16:17], scalar2=None, op0=ALU.add)
                            v.tensor_scalar(out=s16[:, fc * 512:(fc + 1) * 512], in0=pm[5][0:16, 0:512],
                                            scalar1=1.0, scalar2=None, op0=ALU.mult)
                            v.tensor_scalar(out=h96[:, fc * 512:(fc + 1) * 512], in0=pm[6][0:96, 0:512],
                                            scalar1=1.0, scalar2=None, op0=ALU.mult)
                            v.sem_inc(sem["DVE"], 1)
                    C.inc("DVE", 2)

                    pe8 = C.get("PE")
                    dve_tr0 = C.get("DVE")
                    @block.tensor
                    def _(t):
                        t = W(t)
                        t.wait_ge(sem["DVE"], dve_tr0)
                        for lb in range(8):
                            if lb >= 1:
                                t.wait_ge(sem["DVE"], dve_tr0 + lb)
                            t.transpose(out=pm[7][:, 0:128], in_=lgt[:, lb * 128:(lb + 1) * 128],
                                        identity=t_ident[:])
                            t.sem_inc(sem["PE"], 1)
                    C.inc("PE", 8)

                    @block.vector
                    def _(v):
                        v = W(v)
                        for lb in range(8):
                            v.wait_ge(sem["PE"], pe8 + lb + 1)
                            v.tensor_copy(out=gtok[:, lb * 6:(lb + 1) * 6], in_=pm[7][:, 0:6])
                            v.sem_inc(sem["DVE"], 1)
                    C.inc("DVE", 8)

                    act9 = C.get("ACT")
                    dve_mark = []
                    @block.vector
                    def _(v):
                        v = W(v)
                        def fence():
                            v.sem_inc(sem["DVE"], 1)
                            C.inc("DVE")
                            v.wait_ge(sem["DVE"], C.get("DVE"))
                        v.wait_ge(sem["DVE"], C.get("DVE"))
                        v.tensor_reduce(out=gm[:, 0:8], in_=gtok[:].rearrange("p (a b) -> p a b", b=6),
                                        axis=mybir.AxisListType.X, op=ALU.max)
                        fence()
                        v.tensor_scalar(out=gm[:, 0:8], in0=gm[:, 0:8], scalar1=-1.0, scalar2=None, op0=ALU.mult)
                        v.sem_inc(sem["DVE"], 1)
                        C.inc("DVE")
                        dve_mark.append(C.get("DVE"))
                        v.wait_ge(sem["ACT"], act9 + 8)
                        v.tensor_reduce(out=gm[:, 8:16], in_=ge_[:].rearrange("p (a b) -> p a b", b=6),
                                        axis=mybir.AxisListType.X, op=ALU.add)
                        fence()
                        v.reciprocal(out=gm[:, 8:16], in_=gm[:, 8:16])
                        fence()
                        for lb in range(8):
                            v.tensor_scalar(out=ge_[:, lb * 6:(lb + 1) * 6], in0=ge_[:, lb * 6:(lb + 1) * 6],
                                            scalar1=gm[:, 8 + lb:9 + lb], scalar2=None, op0=ALU.mult)
                        fence()
                        v.tensor_reduce(out=gm[:, 16:24], in_=ge_[:].rearrange("p (a b) -> p a b", b=6),
                                        axis=mybir.AxisListType.X, op=ALU.max)
                        fence()
                        for lb in range(8):
                            v.tensor_scalar(out=gw[:, lb * 6:(lb + 1) * 6], in0=ge_[:, lb * 6:(lb + 1) * 6],
                                            scalar1=gm[:, 16 + lb:17 + lb], scalar2=None, op0=ALU.is_ge)
                        fence()
                        v.tensor_tensor(out=gw[:], in0=gw[:], in1=ge_[:], op=ALU.mult)
                        fence()
                        v.tensor_tensor(out=gtok[:], in0=ge_[:], in1=gw[:], op=ALU.subtract)
                        fence()
                        v.tensor_reduce(out=gm[:, 16:24], in_=gtok[:].rearrange("p (a b) -> p a b", b=6),
                                        axis=mybir.AxisListType.X, op=ALU.max)
                        fence()
                        for lb in range(8):
                            v.tensor_scalar(out=gw[:, lb * 6:(lb + 1) * 6], in0=ge_[:, lb * 6:(lb + 1) * 6],
                                            scalar1=gm[:, 16 + lb:17 + lb], scalar2=None, op0=ALU.is_ge)
                        fence()
                        v.tensor_tensor(out=gw[:], in0=gw[:], in1=ge_[:], op=ALU.mult)
                        v.sem_inc(sem["DVE"], 1)
                    C.inc("DVE", 1)

                    @block.scalar
                    def _(a):
                        a = W(a)
                        a.wait_ge(sem["DVE"], dve_mark[0])
                        for lb in range(8):
                            a.activation(out=ge_[:, lb * 6:(lb + 1) * 6], in_=gtok[:, lb * 6:(lb + 1) * 6],
                                         func=AF.Exp, bias=gm[:, lb:lb + 1]).then_inc(sem["ACT"], 1)
                        C.inc("ACT", 8)

                    pe9 = C.get("PE")
                    dve9 = C.get("DVE")
                    @block.vector
                    def _(v):
                        v = W(v)
                        v.wait_ge(sem["DVE"], dve9)
                        for lb in range(8):
                            if lb >= 1:
                                v.wait_ge(sem["PE"], pe9 + lb)
                            v.tensor_copy(out=gpad[:, 0:6], in_=gw[:, lb * 6:(lb + 1) * 6])
                            v.sem_inc(sem["DVE"], 1)
                            v.wait_ge(sem["PE"], pe9 + lb + 1)
                            v.tensor_copy(out=wT6[:, lb * 128:(lb + 1) * 128], in_=pm[7][0:6, 0:128])
                            v.sem_inc(sem["DVE"], 1)
                    C.inc("DVE", 16)

                    @block.tensor
                    def _(t):
                        t = W(t)
                        for lb in range(8):
                            t.wait_ge(sem["DVE"], dve9 + 2 * lb + 1)
                            t.transpose(out=pm[7][:, 0:128], in_=gpad[:], identity=t_ident[:])
                            t.sem_inc(sem["PE"], 1)
                    C.inc("PE", 8)

                    @block.sync
                    def _(sp):
                        sp = W(sp)
                        sp.wait_ge(sem["DVE"], C.get("DVE"))
                        sp.dma_start(out=wrow_dr[:, :], in_=wT6[:]).then_inc(sem["LD"], 16); C.inc("LD")
                        sp.wait_ge(sem["LD"], 16 * C.get("LD"))
                        for e6 in range(6):
                            sp.dma_start(out=w96[e6 * 16:(e6 + 1) * 16, :],
                                         in_=wrow_dr[e6:e6 + 1, :].broadcast_to([16, Lc])
                                         ).then_inc(sem["LD"], 16)
                            C.inc("LD")

                    @block.vector
                    def _(v):
                        v = W(v)
                        v.wait_ge(sem["LD"], 16 * C.get("LD"))
                        v.tensor_tensor(out=h96[:], in0=h96[:], in1=w96[:], op=ALU.mult)
                        v.sem_inc(sem["DVE"], 1)
                    C.inc("DVE")
                    barrier()
                    sg3b.close()
                    if _STOP[0] == 'G2':
                        return nc

                    with ExitStack() as sg4:
                        slabB = sg4.enter_context(nc.sbuf_tensor("slabB", [128, DT * D], BF16))
                        evb = [sg4.enter_context(nc.sbuf_tensor(f"evb{i}", [128, 512], F32)) for i in range(3)]

                        @block.gpsimd
                        def _(g):
                            g = W(g)
                            for kt in range(DT):
                                g.dma_start(out=slabB[:, kt * D:(kt + 1) * D],
                                            in_=wbase_in[kt * 128:(kt + 1) * 128, :]).then_inc(sem["GD"], 16)
                                C.inc("GD")

                        units = [(fc, dt) for fc in range(2) for dt in range(DT)]
                        pe10 = C.get("PE")
                        act10 = C.get("ACT")
                        dve10 = C.get("DVE")
                        eg0 = [C.get("E%d" % i) for i in range(3)]
                        @block.tensor
                        def _(t):
                            t = W(t)
                            t.wait_ge(sem["GD"], 16 * C.get("GD"))
                            for u, (fc, dt) in enumerate(units):
                                if u >= 4:
                                    t.wait_ge(sem["DVE"], dve10 + (u - 3))
                                for kt in range(DT):
                                    t.matmul(out=pm[u % 4][:, 0:512],
                                             lhsT=slabB[:, kt * D + dt * 128:kt * D + (dt + 1) * 128],
                                             rhs=h2[:, kt * Lc + fc * 512:kt * Lc + (fc + 1) * 512],
                                             start=(kt == 0), stop=False)
                                t.matmul(out=pm[u % 4][:, 0:512],
                                         lhsT=bss[:, dt * 128:(dt + 1) * 128],
                                         rhs=s16[:, fc * 512:(fc + 1) * 512],
                                         start=False, stop=False)
                                t.matmul(out=pm[u % 4][:, 0:512],
                                         lhsT=brss[:, dt * 128:(dt + 1) * 128],
                                         rhs=h96[:, fc * 512:(fc + 1) * 512],
                                         start=False, stop=True)
                                t.sem_inc(sem["PE"], 1)
                        C.inc("PE", len(units))

                        @block.vector
                        def _(v):
                            v = W(v)
                            for u, (fc, dt) in enumerate(units):
                                v.wait_ge(sem["PE"], pe10 + u + 1)
                                if u >= 3:
                                    v.wait_ge(sem["E%d" % (u % 3)], 16 * (eg0[u % 3] + (u - 3) // 3 + 1))
                                v.tensor_scalar(out=evb[u % 3][:], in0=pm[u % 4][:, 0:512],
                                                scalar1=BBASE(dt), scalar2=GGB(dt),
                                                op0=ALU.add, op1=ALU.mult)
                                v.tensor_tensor(out=evb[u % 3][:], in0=evb[u % 3][:],
                                                in1=x1T[:, dt * Lc + fc * 512:dt * Lc + (fc + 1) * 512],
                                                op=ALU.add)
                                v.sem_inc(sem["DVE"], 1)
                        C.inc("DVE", len(units))

                        @block.sync
                        def _(sp):
                            sp = W(sp)
                            for u, (fc, dt) in enumerate(units):
                                sp.wait_ge(sem["DVE"], dve10 + u + 1)
                                sp.dma_start(out=out_T[dt * 128:(dt + 1) * 128, fc * 512:(fc + 1) * 512],
                                             in_=evb[u % 3][:]).then_inc(sem["E%d" % (u % 3)], 16)
                                C.inc("E%d" % (u % 3))
                            for i in range(3):
                                sp.wait_ge(sem["E%d" % i], 16 * C.get("E%d" % i))
                            if _DBG[0]:
                                sp.dma_start(out=dbg[0:Ec, :], in_=spill_ss[:, :].bitcast(F32)[:, 0:512].rearrange("a b -> a b") if False else spill_ss[:, :]).then_inc(sem["ST"], 16) if False else None
                                pass
                        if _DBG[0]:
                            @block.gpsimd
                            def _(g):
                                g = W(g)
                                g.dma_start(out=dbg[0:Ec, :], in_=spill_ss[:, :]).then_inc(sem["GD"], 16); C.inc("GD")
                                g.dma_start(out=dbg[Ec:2 * Ec, :], in_=spill_yp[:, :]).then_inc(sem["GD"], 16); C.inc("GD")
                                for dt in range(DT):
                                    g.dma_start(out=dbg[10496 + dt * 128:10496 + (dt + 1) * 128, :],
                                                in_=h2[:, dt * Lc:(dt + 1) * Lc]).then_inc(sem["GD"], 16); C.inc("GD")
                                g.dma_start(out=dbg[12544:12550, :], in_=wT6[:, :]).then_inc(sem["GD"], 16); C.inc("GD")
                                g.dma_start(out=dbg[12550:12566, :], in_=s16[:, :]).then_inc(sem["GD"], 16); C.inc("GD")
                                g.dma_start(out=dbg[12566:12662, :], in_=h96[:, :]).then_inc(sem["GD"], 16); C.inc("GD")
                                g.dma_start(out=dbg[12662:12758, :], in_=w96[:, :]).then_inc(sem["GD"], 16); C.inc("GD")
                                g.dma_start(out=dbg[12758:12886, 0:48], in_=gw[:, :]).then_inc(sem["GD"], 16); C.inc("GD")
                                g.dma_start(out=dbg[12886:13014, 0:256], in_=t_state[:, :]).then_inc(sem["GD"], 16); C.inc("GD")
                            @block.sync
                            def _(sp):
                                sp = W(sp)
                                sp.dma_start(out=dbg[2 * Ec:2 * Ec + D, :], in_=sh_out[0:D, :]).then_inc(sem["LD"], 16); C.inc("LD")
                                sp.dma_start(out=dbg[2 * Ec + D:2 * Ec + 2 * D, :], in_=sh_out[2 * D:3 * D, :]).then_inc(sem["LD"], 16); C.inc("LD")
                                for dt in range(DT):
                                    sp.dma_start(out=dbg[2 * Ec + 2 * D + dt * 128:2 * Ec + 2 * D + (dt + 1) * 128, :],
                                                 in_=x1T[:, dt * Lc:(dt + 1) * Lc]).then_inc(sem["LD"], 16); C.inc("LD")
                                sp.dma_start(out=dbg[6 * D:6 * D + 1, 0:1024], in_=modrow_dr[:, 0:1024]).then_inc(sem["LD"], 16); C.inc("LD")
                        barrier()
    return nc


_NC_CACHE = {}

def kernel(**inputs):
    f32 = np.float32
    resid = np.asarray(inputs["L0_attn_resid"], f32)
    norm_w = np.asarray(inputs["L0_attn_norm_w"], f32)
    skip_w = np.asarray(inputs["L0_attn_skip_w"], f32)
    in_w = np.asarray(inputs["L0_attn_in_w"], f32)
    conv_w = np.asarray(inputs["L0_attn_conv_w"], f32)
    conv_b = np.asarray(inputs["L0_attn_conv_b"], f32)
    wd1 = np.asarray(inputs["L0_attn_wd1"], f32)
    wd2 = np.asarray(inputs["L0_attn_wd2"], f32)
    wd2_b = np.asarray(inputs["L0_attn_wd2_b"], f32)
    wB = np.asarray(inputs["L0_attn_wB"], f32)
    wC = np.asarray(inputs["L0_attn_wC"], f32)
    A_log = np.asarray(inputs["L0_attn_A_log"], f32)
    W_D = np.asarray(inputs["L0_attn_W_D"], f32)
    out_w = np.asarray(inputs["L0_attn_out_w"], f32)
    W_base = np.asarray(inputs["L0_ffn_W_base"], f32)
    b_base = np.asarray(inputs["L0_ffn_b_base"], f32)
    A_s = np.asarray(inputs["L0_ffn_A_s"], f32)
    B_s = np.asarray(inputs["L0_ffn_B_s"], f32)
    A_r = np.asarray(inputs["L0_ffn_A_r"], f32)
    B_r = np.asarray(inputs["L0_ffn_B_r"], f32)
    W_router = np.asarray(inputs["L0_ffn_W_router"], f32)
    b_router = np.asarray(inputs["L0_ffn_b_router"], f32)
    c_in = np.asarray(inputs["L0_c"], f32)
    wmod = np.asarray(inputs["L0_wmod"], f32)

    def colmaj(vec):
        return np.ascontiguousarray(vec.reshape(16, 128).T)

    cvec = np.zeros((128, 64), f32)
    fvec = np.zeros((128, 32), f32)
    ident = np.eye(128, dtype=f32)
    A_rT = np.ascontiguousarray(A_r.transpose(1, 0, 2).reshape(D, 96))
    B_rs = np.ascontiguousarray(B_r.reshape(96, D))
    fvec[:, 0:16] = colmaj(b_base)
    fvec[0:6, 16] = b_router

    in_maps = []
    for core in range(8):
        b, h = core // 2, core % 2
        sl = slice(h * Ec, (h + 1) * Ec)
        cv = cvec.copy()
        cv[:, 0:16] = colmaj(conv_b[sl])
        cv[:, 16:32] = colmaj(wd2_b[sl])
        cv[:, 32:48] = colmaj(W_D[sl])
        cv[:, 48:64] = colmaj(norm_w)
        m = {
            "residT": np.ascontiguousarray(resid[b].T),
            "role": np.array([[h]], np.uint32),
            "bid": np.array([[b]], np.uint32),
            "c4": np.ascontiguousarray(c_in.T[h * 1024:(h + 1) * 1024].reshape(8, 128, 4).transpose(1, 0, 2).reshape(128, 32)),
            "wmod_s": np.ascontiguousarray(wmod[h * 1024:(h + 1) * 1024, :]),
            "in_wT": np.ascontiguousarray(in_w[sl, :].T),
            "skip_wT": np.ascontiguousarray(skip_w[sl, :].T),
            "out_wT": np.ascontiguousarray(out_w[:, sl].T),
            "wd1T": np.ascontiguousarray(wd1[:, sl].T),
            "wd2T": np.ascontiguousarray(wd2[sl, :].T),
            "wBT": np.ascontiguousarray(wB[:, sl].T),
            "wCT": np.ascontiguousarray(wC[:, sl].T),
            "alog": np.ascontiguousarray(A_log[sl].reshape(16, 128, 16).transpose(1, 0, 2).reshape(128, 256)),
            "convw": np.ascontiguousarray(conv_w[sl].reshape(16, 128, 4).transpose(1, 0, 2).reshape(128, 64)),
            "cvec": cv,
            "W_base": W_base,
            "fvec": fvec,
            "A_s": A_s,
            "B_s": B_s,
            "A_rT": A_rT,
            "B_rs": B_rs,
            "W_router": W_router,
            "ident": ident,
        }
        in_maps.append(m)

    if "nc" not in _NC_CACHE:
        _NC_CACHE["nc"] = build_program()
    nc = _NC_CACHE["nc"]
    res = run_bass_kernel_spmd(nc, in_maps, core_ids=list(range(8)))
    full = np.empty((B, L, D), f32)
    for core in range(8):
        b, h = core // 2, core % 2
        full[b, h * Lc:(h + 1) * Lc, :] = np.asarray(res.results[core]["out_T"]).T
    return full


# version 08
# speedup vs baseline: 1.0139x; 1.0052x over previous
import numpy as np
import concourse.bass as bass
import concourse.mybir as mybir
from concourse.bass_utils import run_bass_kernel_spmd
from contextlib import ExitStack

F32 = mybir.dt.float32
BF16 = mybir.dt.bfloat16
U32 = mybir.dt.uint32
AF = mybir.ActivationFunctionType
ALU = mybir.AluOpType

B, L, D, E, N, R = 4, 2048, 2048, 4096, 16, 128
Ec = E // 2
Lc = 1024
NCH = L // Lc
DT = D // 128
ET = Ec // 128
EB = 8
NBLK = ET // EB
EPS = 1e-5
MAGIC = 777001000

def fl_mod(w): return w
def fl_zbc(w, c): return 2 + w * 2 + c
def fl_out(w, c): return 6 + w * 2 + c
NFLAGS = 10


class Ctr:
    def __init__(self):
        self.v = {}
    def inc(self, k, n=1):
        self.v[k] = self.v.get(k, 0) + n
        return self.v[k]
    def get(self, k):
        return self.v.get(k, 0)


class W:
    def __init__(self, e):
        object.__setattr__(self, "_e", e)
        object.__setattr__(self, "_last", None)
    def __getattr__(self, name):
        f = getattr(self._e, name)
        if not callable(f):
            return f
        def g(*a, **k):
            r = f(*a, **k)
            if hasattr(r, "then_inc"):
                object.__setattr__(self, "_last", r)
            return r
        return g
    def sem_inc(self, s, n):
        if self._last is not None:
            self._last.then_inc(s, n)
        else:
            self._e.sem_inc(s, n)


_STOP = [None]
_RACE = [False]
_DBG = [False]

def build_program():
    nc = bass.Bass("TRN2", target_bir_lowering=False, enable_asserts=False,
                   detect_race_conditions=_RACE[0])

    def I(name, shape, dt=F32):
        return nc.dram_tensor(name, shape, dt, kind="ExternalInput").ap()

    residT = I("residT", [D, L])
    role_in = I("role", [1, 1], U32)
    bid_in = I("bid", [1, 1], U32)
    c4_in = I("c4", [128, 32])
    wmod_in = I("wmod_s", [1024, 6 * D])
    in_wT = I("in_wT", [D, Ec])
    skip_wT = I("skip_wT", [D, Ec])
    out_wT = I("out_wT", [Ec, D])
    wd1T_in = I("wd1T", [Ec, R])
    wd2T_in = I("wd2T", [R, Ec])
    wBT_in = I("wBT", [Ec, N])
    wCT_in = I("wCT", [Ec, N])
    alog_in = I("alog", [128, ET * N])
    convw_in = I("convw", [128, 64])
    cvec_in = I("cvec", [128, 64])
    wbase_in = I("W_base", [D, D])
    fvec_in = I("fvec", [128, 32])
    as_in = I("A_s", [D, N])
    bs_in = I("B_s", [N, D])
    art_in = I("A_rT", [D, 96])
    brs_in = I("B_rs", [96, D])
    wrout_in = I("W_router", [D, 6])
    ident_in = I("ident", [128, 128])

    out_T = nc.dram_tensor("out_T", [D, Lc], F32, kind="ExternalOutput").ap()
    dbg = nc.dram_tensor("dbg", [16384, Lc], F32, kind="ExternalOutput").ap() if _DBG[0] else None

    spill_ss = nc.dram_tensor("spill_ss", [Ec, Lc], BF16, kind="Internal").ap()
    spill_yp = nc.dram_tensor("spill_yp", [Ec, Lc], BF16, kind="Internal").ap()
    spill_h = nc.dram_tensor("spill_h", [2 * D, Lc], BF16, kind="Internal").ap()
    rows_bc = nc.dram_tensor("rows_bc", [8, Lc], F32, kind="Internal").ap()
    bmcm_dr = nc.dram_tensor("bmcm_dr", [32, Lc], BF16, kind="Internal").ap()
    modp_dr = nc.dram_tensor("modp_dr", [4, 6 * D], F32, kind="Internal").ap()
    modrow_dr = nc.dram_tensor("modrow_dr", [1, 6 * D], F32, kind="Internal").ap()
    wrow_dr = nc.dram_tensor("wrow_dr", [6, Lc], F32, kind="Internal").ap()

    sh_zbc = nc.dram_tensor("sh_zbc", [4 * 160, Lc], F32, kind="Internal", addr_space="Shared").ap()
    sh_out = nc.dram_tensor("sh_out", [4 * D, Lc], F32, kind="Internal", addr_space="Shared").ap()
    sh_mod = nc.dram_tensor("sh_mod", [8, 6 * D], F32, kind="Internal", addr_space="Shared").ap()
    sh_flag = nc.dram_tensor("sh_flag", [NFLAGS, 16], U32, kind="Internal", addr_space="Shared").ap()

    C = Ctr()
    with ExitStack() as octx:
        sem = {}
        for s in ["LD", "GD", "PE", "ACT", "DVE", "PL", "ST", "PX", "MS", "BR",
                  "R0", "R1", "GB0", "GB1", "E0", "E1", "E2", "E3",
                  "BC0", "BC1", "SK", "YP", "CA", "CB", "CR", "CH"]:
            sem[s] = octx.enter_context(nc.semaphore("s" + s))
        SB = lambda name, shape, dt: octx.enter_context(nc.sbuf_tensor(name, shape, dt))
        pm = [octx.enter_context(nc.psum_tensor(f"pm{i}", [128, 512], F32)) for i in range(8)]

        t_role = SB("t_role", [1, 1], U32)
        t_bidv = SB("t_bidv", [1, 1], U32)
        t_c4 = SB("t_c4", [128, 32], F32)
        t_sc4 = SB("t_sc4", [128, 32], BF16)
        t_negA = SB("t_negA", [128, ET * N], F32)
        t_convw = SB("t_convw", [128, 64], F32)
        t_cvec = SB("t_cvec", [128, 64], F32)
        t_fvec = SB("t_fvec", [128, 32], F32)
        t_gain = SB("t_gain", [128, 96], F32)
        t_state = SB("t_state", [128, 256], F32)
        t_ident = SB("t_ident", [128, 128], F32)
        t_ones = SB("t_ones", [128, 1], BF16)
        t_eps1 = SB("t_eps1", [1, 1], F32)
        t_one1 = SB("t_one1", [128, 1], F32)
        t_flagv = SB("t_flagv", [1, 16 * NFLAGS], U32)
        t_flagrd = SB("t_flagrd", [1, 16], U32)
        t_rows = [SB(f"t_row{i}", [1, Lc], F32) for i in range(3)]
        t_tail = SB("t_tail", [128, ET * 4], F32)
        t_brs = SB("t_brs", [1, 8], F32)

        block = octx.enter_context(nc.Block())
        reg_role = octx.enter_context(nc.sync.register("r_role"))
        reg_bid = octx.enter_context(nc.sync.register("r_bid"))
        reg_ind = octx.enter_context(nc.sync.register("r_ind"))
        reg_fr = octx.enter_context(nc.sync.register("r_fr"))
        reg_px = octx.enter_context(nc.sync.register("r_px"))

        CONVB = lambda et: t_cvec[:, et:et + 1]
        WD2B = lambda et: t_cvec[:, 16 + et:17 + et]
        WDC = lambda et: t_cvec[:, 32 + et:33 + et]
        NRMW = lambda dt: t_cvec[:, 48 + dt:49 + dt]
        GSA = lambda dt: t_gain[:, 0 + dt:1 + dt]
        GCA = lambda dt: t_gain[:, 16 + dt:17 + dt]
        GGA = lambda dt: t_gain[:, 32 + dt:33 + dt]
        GSB = lambda dt: t_gain[:, 48 + dt:49 + dt]
        GCB = lambda dt: t_gain[:, 64 + dt:65 + dt]
        GGB = lambda dt: t_gain[:, 80 + dt:81 + dt]
        BBASE = lambda dt: t_fvec[:, dt:dt + 1]

        def barrier():
            @block.sync
            def _(sp):
                sp = W(sp)
                sp.wait_ge(sem["LD"], 16 * C.get("LD"))
                sp.wait_ge(sem["ST"], 16 * C.get("ST"))
                for s_ in ["R0", "R1", "E0", "E1", "E2", "E3", "BC0", "BC1", "SK", "YP", "CA", "CB", "CR", "CH"]:
                    if C.get(s_):
                        sp.wait_ge(sem[s_], 16 * C.get(s_))
            @block.gpsimd
            def _(g):
                g = W(g)
                g.wait_ge(sem["GD"], 16 * C.get("GD"))
                for s_ in ["GB0", "GB1"]:
                    if C.get(s_):
                        g.wait_ge(sem[s_], 16 * C.get(s_))
            tgt = 5 * C.inc("BRS")
            @block.sync
            def _(e):
                e = W(e)
                e.sem_inc(sem["BR"], 1)
                e.wait_ge(sem["BR"], tgt)
            @block.gpsimd
            def _(e):
                e = W(e)
                e.memset(t_brs[0:1, 0:1], 0.0)
                e.sem_inc(sem["BR"], 1)
                e.wait_ge(sem["BR"], tgt)
            @block.vector
            def _(e):
                e = W(e)
                e.memset(t_brs[0:1, 1:2], 0.0)
                e.sem_inc(sem["BR"], 1)
                e.wait_ge(sem["BR"], tgt)
            @block.scalar
            def _(e):
                e = W(e)
                e.memzero(t_brs[0:1, 2:3])
                e.sem_inc(sem["BR"], 1)
                e.wait_ge(sem["BR"], tgt)
            @block.tensor
            def _(e):
                e = W(e)
                e.sem_inc(sem["BR"], 1)
                e.wait_ge(sem["BR"], tgt)

        def poll(sp, f):
            sp.reg_mov(reg_ind, 1)
            with sp.While(reg_ind):
                sp.dma_start(out=t_flagrd[:], in_=sh_flag[f:f + 1, :]).then_inc(sem["PX"], 16)
                sp.reg_add(reg_px, reg_px, 16)
                sp.wait_ge(sem["PX"], reg_px)
                sp.reg_load(reg_fr, t_flagrd[0:1, 0:1])
                sp.reg_mov(reg_ind, 0)
                with sp.If_ne(reg_fr, MAGIC + f):
                    sp.reg_mov(reg_ind, 1)

        with ExitStack() as s0:
            HW = 3 * D
            slabM = s0.enter_context(nc.sbuf_tensor("slabM", [128, 8 * HW], BF16))
            t_mp = [s0.enter_context(nc.sbuf_tensor(f"t_mp{i}", [4, 512], F32)) for i in range(2)]
            n_fc = 6 * D // 512
            n_fch = n_fc // 2

            @block.sync
            def _(sp):
                sp = W(sp)
                for name, dst in [(role_in, t_role), (bid_in, t_bidv), (c4_in, t_c4),
                                  (alog_in, t_negA), (convw_in, t_convw), (cvec_in, t_cvec),
                                  (fvec_in, t_fvec), (ident_in, t_ident)]:
                    sp.dma_start(out=dst[:], in_=name).then_inc(sem["LD"], 16)
                    C.inc("LD")
                sp.wait_ge(sem["LD"], 16 * C.get("LD"))
                sp.reg_load(reg_role, t_role[0:1, 0:1])
                sp.reg_load(reg_bid, t_bidv[0:1, 0:1])
                sp.reg_mov(reg_px, 0)
                sp.sem_inc(sem["MS"], 1)
            C.inc("MS")

            dve0 = C.get("DVE")
            @block.vector
            def _(v):
                v = W(v)
                v.wait_ge(sem["MS"], C.get("MS"))
                v.memset(t_ones[:], 1.0)
                v.memset(t_one1[:], 1.0)
                v.memset(t_eps1[:], EPS)
                for f in range(NFLAGS):
                    v.memset(t_flagv[:, 16 * f:16 * (f + 1)], MAGIC + f)
                v.memset(t_state[:], 0.0)
                v.wait_ge(sem["ACT"], C.get("ACT") + 2)
                v.tensor_scalar(out=t_negA[:], in0=t_negA[:], scalar1=-1.0, scalar2=None, op0=ALU.mult)
                v.tensor_scalar(out=t_sc4[:], in0=t_c4[:], scalar1=1.0, scalar2=None, op0=ALU.mult)
                v.sem_inc(sem["DVE"], 1)
            C.inc("DVE")

            act0 = C.get("ACT")
            @block.scalar
            def _(a):
                a = W(a)
                a.wait_ge(sem["MS"], C.get("MS"))
                a.activation(out=t_negA[:], in_=t_negA[:], func=AF.Exp).then_inc(sem["ACT"], 1)
                a.activation(out=t_c4[:], in_=t_c4[:], func=AF.Silu).then_inc(sem["ACT"], 1)
            C.inc("ACT", 2)

            if _STOP[0] == 'm1':
                barrier()
                return nc

            gd_m0 = C.get("GD")
            pe_m0 = C.get("PE")
            @block.gpsimd
            def _(g):
                g = W(g)
                for half in range(2):
                    if half == 1:
                        g.wait_ge(sem["PE"], pe_m0 + n_fch)
                    for kt in range(8):
                        g.dma_start(out=slabM[:, kt * HW:(kt + 1) * HW],
                                    in_=wmod_in[kt * 128:(kt + 1) * 128, half * HW:(half + 1) * HW]
                                    ).then_inc(sem["GD"], 16)
                        C.inc("GD")
            pe0 = C.get("PE")
            @block.tensor
            def _(t):
                t = W(t)
                t.wait_ge(sem["DVE"], C.get("DVE"))
                for half in range(2):
                    t.wait_ge(sem["GD"], 16 * (gd_m0 + 8 * (half + 1)))
                    for fcl in range(n_fch):
                        fc = half * n_fch + fcl
                        if fc >= 2:
                            t.wait_ge(sem["ACT"], act0 + 2 + (fc - 2) + 1)
                        for kt in range(8):
                            t.matmul(out=pm[fc % 2][0:4, 0:512],
                                     lhsT=t_sc4[:, kt * 4:(kt + 1) * 4],
                                     rhs=slabM[:, kt * HW + fcl * 512:kt * HW + (fcl + 1) * 512],
                                     start=(kt == 0), stop=(kt == 7))
                        t.sem_inc(sem["PE"], 1)
            C.inc("PE", n_fc)

            em0 = [C.get("E0"), C.get("E1")]
            @block.scalar
            def _(a):
                a = W(a)
                for fc in range(n_fc):
                    a.wait_ge(sem["PE"], pe0 + fc + 1)
                    if fc >= 2:
                        a.wait_ge(sem["E%d" % (fc % 2)], 16 * (em0[fc % 2] + (fc - 2) // 2 + 1))
                    a.activation(out=t_mp[fc % 2][:], in_=pm[fc % 2][0:4, 0:512],
                                 func=AF.Identity).then_inc(sem["ACT"], 1)
            act1 = C.get("ACT")
            C.inc("ACT", n_fc)

            @block.sync
            def _(sp):
                sp = W(sp)
                for fc in range(n_fc):
                    sp.wait_ge(sem["ACT"], act1 + fc + 1)
                    sp.dma_start(out=modp_dr[:, fc * 512:(fc + 1) * 512],
                                 in_=t_mp[fc % 2][:]).then_inc(sem["E%d" % (fc % 2)], 16)
                    C.inc("E%d" % (fc % 2))
                sp.wait_ge(sem["E0"], 16 * C.get("E0"))
                sp.wait_ge(sem["E1"], 16 * C.get("E1"))
            if _STOP[0] == 'm2':
                barrier()
                return nc
            @block.sync
            def _(sp):
                sp = W(sp)
                with sp.If_eq(reg_role, 0):
                    sp.dma_start(out=sh_mod[0:4, :], in_=modp_dr[:, :]).then_inc(sem["LD"], 16)
                with sp.Else():
                    sp.dma_start(out=sh_mod[4:8, :], in_=modp_dr[:, :]).then_inc(sem["LD"], 16)
                C.inc("LD")
                sp.wait_ge(sem["LD"], 16 * C.get("LD"))
                with sp.If_eq(reg_role, 0):
                    sp.dma_start(out=sh_flag[0:1, :], in_=t_flagv[:, 0:16]).then_inc(sem["LD"], 16)
                with sp.Else():
                    sp.dma_start(out=sh_flag[1:2, :], in_=t_flagv[:, 16:32]).then_inc(sem["LD"], 16)
                C.inc("LD")
                sp.wait_ge(sem["LD"], 16 * C.get("LD"))
                poll(sp, fl_mod(0))
                poll(sp, fl_mod(1))
            barrier()
            if _STOP[0] == 'm3':
                return nc

        with ExitStack() as s1:
            t_ma = s1.enter_context(nc.sbuf_tensor("t_ma", [4, 6 * D], F32))
            t_mb = s1.enter_context(nc.sbuf_tensor("t_mb", [4, 6 * D], F32))
            t_graw = s1.enter_context(nc.sbuf_tensor("t_graw", [128, 96], F32))
            t_gtmp = s1.enter_context(nc.sbuf_tensor("t_gtmp", [128, 32], F32))

            @block.sync
            def _(sp):
                sp = W(sp)
                sp.dma_start(out=t_ma[:], in_=sh_mod[0:4, :]).then_inc(sem["LD"], 16); C.inc("LD")
                sp.dma_start(out=t_mb[:], in_=sh_mod[4:8, :]).then_inc(sem["LD"], 16); C.inc("LD")

            @block.vector
            def _(v):
                v = W(v)
                v.wait_ge(sem["LD"], 16 * C.get("LD"))
                v.tensor_tensor(out=t_ma[:], in0=t_ma[:], in1=t_mb[:], op=ALU.add).then_inc(sem["DVE"], 1)
            C.inc("DVE")

            @block.sync
            def _(sp):
                sp = W(sp)
                sp.wait_ge(sem["DVE"], C.get("DVE"))
                for bb in range(4):
                    with sp.If_eq(reg_bid, bb):
                        sp.dma_start(out=modrow_dr[:, :], in_=t_ma[bb:bb + 1, :]).then_inc(sem["LD"], 16)
                C.inc("LD")
                sp.wait_ge(sem["LD"], 16 * C.get("LD"))
                for g in range(6):
                    for dt in range(DT):
                        src = modrow_dr[:, g * D + dt * 128:g * D + (dt + 1) * 128]
                        sp.dma_start(out=t_graw[:, g * 16 + dt:g * 16 + dt + 1],
                                     in_=src.rearrange("a b -> b a")).then_inc(sem["LD"], 16)
                        C.inc("LD")

            act2 = C.get("ACT")
            @block.scalar
            def _(a):
                a = W(a)
                a.wait_ge(sem["LD"], 16 * C.get("LD"))
                a.activation(out=t_gtmp[:, 0:16], in_=t_graw[:, 32:48], func=AF.Abs)
                a.activation(out=t_gtmp[:, 16:32], in_=t_graw[:, 80:96], func=AF.Abs)
                a.activation(out=t_graw[:, 32:48], in_=t_graw[:, 32:48], func=AF.Sign)
                a.activation(out=t_graw[:, 80:96], in_=t_graw[:, 80:96], func=AF.Sign).then_inc(sem["ACT"], 1)
            C.inc("ACT")

            @block.vector
            def _(v):
                v = W(v)
                v.wait_ge(sem["ACT"], C.get("ACT"))
                v.tensor_copy(out=t_gain[:, 0:16], in_=t_graw[:, 0:16])
                v.tensor_scalar(out=t_gain[:, 16:32], in0=t_graw[:, 16:32], scalar1=1.0, scalar2=None, op0=ALU.add)
                v.tensor_copy(out=t_gain[:, 48:64], in_=t_graw[:, 48:64])
                v.tensor_scalar(out=t_gain[:, 64:80], in0=t_graw[:, 64:80], scalar1=1.0, scalar2=None, op0=ALU.add)
                v.tensor_scalar(out=t_gtmp[:, 0:16], in0=t_gtmp[:, 0:16], scalar1=0.25, scalar2=None, op0=ALU.max)
                v.tensor_scalar(out=t_gtmp[:, 16:32], in0=t_gtmp[:, 16:32], scalar1=0.25, scalar2=None, op0=ALU.max)
                v.sem_inc(sem["DVE"], 1)
                v.wait_ge(sem["DVE"], C.get("DVE") + 1)
                v.tensor_tensor(out=t_gain[:, 32:48], in0=t_gtmp[:, 0:16], in1=t_graw[:, 32:48], op=ALU.mult)
                v.tensor_tensor(out=t_gain[:, 80:96], in0=t_gtmp[:, 16:32], in1=t_graw[:, 80:96], op=ALU.mult)
                v.sem_inc(sem["DVE"], 1)
            C.inc("DVE", 2)
            barrier()
            if _STOP[0] == 'mod':
                return nc

        for c in range(NCH):
            with ExitStack() as scc:
                xT = scc.enter_context(nc.sbuf_tensor(f"xT_{c}", [128, ET * Lc], F32))
                with ExitStack() as sph1:
                    rnT = sph1.enter_context(nc.sbuf_tensor(f"rnT_{c}", [128, ET * Lc], BF16))

                    with ExitStack() as sa:
                        bufH = sa.enter_context(nc.sbuf_tensor(f"bufH_{c}", [128, DT * Lc], BF16))
                        rR = sa.enter_context(nc.sbuf_tensor(f"rR_{c}", [128, 2 * Lc], F32))
                        b16 = sa.enter_context(nc.sbuf_tensor(f"b16_{c}", [128, 2 * Lc], BF16))
                        sqb = sa.enter_context(nc.sbuf_tensor(f"sqb_{c}", [128, 2 * Lc], BF16))
                        mB = sa.enter_context(nc.sbuf_tensor(f"mB_{c}", [128, Lc], F32))
                        rB = sa.enter_context(nc.sbuf_tensor(f"rB_{c}", [128, Lc], F32))
                        r2B = sa.enter_context(nc.sbuf_tensor(f"r2B_{c}", [128, Lc], F32))
                        tv = sa.enter_context(nc.sbuf_tensor(f"tv_{c}", [128, Lc], F32))

                        r0b, r1b = C.get("R0"), C.get("R1")
                        g0b, g1b = C.get("GB0"), C.get("GB1")
                        rcnt = lambda dt: 16 * ((r0b if dt % 2 == 0 else r1b) + dt // 2 + 1)
                        gcnt = lambda dt: 16 * ((g0b if dt % 2 == 0 else g1b) + dt // 2 + 1)
                        act0 = C.get("ACT")
                        pe0 = C.get("PE")
                        @block.sync
                        def _(sp):
                            sp = W(sp)
                            for dt in range(DT):
                                if dt >= 2:
                                    sp.wait_ge(sem["PE"], pe0 + dt - 1)
                                sp.dma_start(out=rR[:, (dt % 2) * Lc:(dt % 2 + 1) * Lc],
                                             in_=residT[dt * 128:(dt + 1) * 128, c * Lc:(c + 1) * Lc]
                                             ).then_inc(sem["R%d" % (dt % 2)], 16)
                                C.inc("R%d" % (dt % 2))

                        @block.gpsimd
                        def _(g):
                            g = W(g)
                            for dt in range(DT):
                                g.wait_ge(sem["R%d" % (dt % 2)], rcnt(dt))
                                if dt >= 2:
                                    g.wait_ge(sem["PE"], pe0 + dt - 1)
                                g.dma_start(out=b16[:, (dt % 2) * Lc:(dt % 2 + 1) * Lc],
                                            in_=rR[:, (dt % 2) * Lc:(dt % 2 + 1) * Lc]).then_inc(sem["GB%d" % (dt % 2)], 16)
                                C.inc("GB%d" % (dt % 2))

                        @block.scalar
                        def _(a):
                            a = W(a)
                            for dt in range(DT):
                                a.wait_ge(sem["R%d" % (dt % 2)], rcnt(dt))
                                if dt >= 2:
                                    a.wait_ge(sem["PE"], pe0 + dt - 1)
                                a.activation(out=sqb[:, (dt % 2) * Lc:(dt % 2 + 1) * Lc],
                                             in_=rR[:, (dt % 2) * Lc:(dt % 2 + 1) * Lc],
                                             func=AF.Square).then_inc(sem["ACT"], 1)
                                C.inc("ACT")

                        @block.tensor
                        def _(t):
                            t = W(t)
                            for dt in range(DT):
                                t.wait_ge(sem["GB%d" % (dt % 2)], gcnt(dt))
                                t.wait_ge(sem["ACT"], act0 + dt + 1)
                                for fc in range(2):
                                    sl = slice((dt % 2) * Lc + fc * 512, (dt % 2) * Lc + (fc + 1) * 512)
                                    t.matmul(out=pm[4 + fc][0:1, 0:512], lhsT=t_ones[:],
                                             rhs=b16[:, sl], start=(dt == 0), stop=(dt == DT - 1))
                                    t.matmul(out=pm[6 + fc][0:1, 0:512], lhsT=t_ones[:],
                                             rhs=sqb[:, sl], start=(dt == 0), stop=(dt == DT - 1))
                                t.sem_inc(sem["PE"], 1)
                        C.inc("PE", DT)

                        @block.vector
                        def _(v):
                            v = W(v)
                            v.wait_ge(sem["PE"], C.get("PE"))
                            for fc in range(2):
                                v.tensor_scalar(out=t_rows[0][:, fc * 512:(fc + 1) * 512],
                                                in0=pm[4 + fc][0:1, 0:512], scalar1=1.0 / D, scalar2=None, op0=ALU.mult)
                                v.tensor_scalar(out=t_rows[1][:, fc * 512:(fc + 1) * 512],
                                                in0=pm[6 + fc][0:1, 0:512], scalar1=1.0 / D, scalar2=None, op0=ALU.mult)
                            v.tensor_tensor(out=t_rows[2][:], in0=t_rows[0][:], in1=t_rows[0][:], op=ALU.mult)
                            v.tensor_tensor(out=t_rows[1][:], in0=t_rows[1][:], in1=t_rows[2][:], op=ALU.subtract)
                            v.sem_inc(sem["DVE"], 1)
                        C.inc("DVE")

                        @block.scalar
                        def _(a):
                            a = W(a)
                            a.wait_ge(sem["DVE"], C.get("DVE"))
                            a.activation(out=t_rows[1][:], in_=t_rows[1][:], func=AF.Sqrt,
                                         bias=t_eps1[0:1, 0:1]).then_inc(sem["ACT"], 1)
                        C.inc("ACT")

                        @block.vector
                        def _(v):
                            v = W(v)
                            v.wait_ge(sem["ACT"], C.get("ACT"))
                            v.reciprocal(out=t_rows[1][:], in_=t_rows[1][:]).then_inc(sem["DVE"], 1)
                        C.inc("DVE")

                        @block.sync
                        def _(sp):
                            sp = W(sp)
                            sp.wait_ge(sem["DVE"], C.get("DVE"))
                            sp.dma_start(out=rows_bc[0:1, :], in_=t_rows[0][:]).then_inc(sem["LD"], 16); C.inc("LD")
                            sp.dma_start(out=rows_bc[1:2, :], in_=t_rows[1][:]).then_inc(sem["LD"], 16); C.inc("LD")
                            sp.wait_ge(sem["LD"], 16 * C.get("LD"))
                            sp.dma_start(out=mB[:], in_=rows_bc[0:1, :].broadcast_to([128, Lc])).then_inc(sem["LD"], 16); C.inc("LD")
                            sp.dma_start(out=rB[:], in_=rows_bc[1:2, :].broadcast_to([128, Lc])).then_inc(sem["LD"], 16); C.inc("LD")

                        r0c, r1c = C.get("R0"), C.get("R1")
                        rcnt2 = lambda dt: 16 * ((r0c if dt % 2 == 0 else r1c) + dt // 2 + 1)
                        dve1 = C.get("DVE")
                        act3 = C.get("ACT")
                        pe1 = C.get("PE")
                        @block.sync
                        def _(sp):
                            sp = W(sp)
                            for dt in range(DT):
                                if dt >= 2:
                                    sp.wait_ge(sem["DVE"], dve1 + dt - 1)
                                sp.dma_start(out=rR[:, (dt % 2) * Lc:(dt % 2 + 1) * Lc],
                                             in_=residT[dt * 128:(dt + 1) * 128, c * Lc:(c + 1) * Lc]
                                             ).then_inc(sem["R%d" % (dt % 2)], 16)
                                C.inc("R%d" % (dt % 2))

                        @block.vector
                        def _(v):
                            v = W(v)
                            for dt in range(DT):
                                v.wait_ge(sem["R%d" % (dt % 2)], rcnt2(dt))
                                v.tensor_tensor(out=tv[:], in0=rR[:, (dt % 2) * Lc:(dt % 2 + 1) * Lc],
                                                in1=mB[:], op=ALU.subtract)
                                v.tensor_tensor(out=tv[:], in0=tv[:], in1=rB[:], op=ALU.mult)
                                v.tensor_scalar(out=bufH[:, dt * Lc:(dt + 1) * Lc], in0=tv[:],
                                                scalar1=GCA(dt), scalar2=GSA(dt), op0=ALU.mult, op1=ALU.add)
                                v.sem_inc(sem["DVE"], 1)
                        C.inc("DVE", DT)

                        @block.gpsimd
                        def _(g):
                            g = W(g)
                            for dt in range(DT):
                                g.wait_ge(sem["DVE"], dve1 + dt + 1)
                                g.dma_start(out=spill_h[c * D + dt * 128:c * D + (dt + 1) * 128, :],
                                            in_=bufH[:, dt * Lc:(dt + 1) * Lc]).then_inc(sem["GD"], 16)
                                C.inc("GD")

                        @block.scalar
                        def _(a):
                            a = W(a)
                            for dt in range(DT):
                                a.wait_ge(sem["DVE"], dve1 + dt + 1)
                                if dt >= 2:
                                    a.wait_ge(sem["PE"], pe1 + dt - 1)
                                a.activation(out=sqb[:, (dt % 2) * Lc:(dt % 2 + 1) * Lc],
                                             in_=bufH[:, dt * Lc:(dt + 1) * Lc],
                                             func=AF.Square).then_inc(sem["ACT"], 1)
                                C.inc("ACT")

                        @block.tensor
                        def _(t):
                            t = W(t)
                            for dt in range(DT):
                                t.wait_ge(sem["ACT"], act3 + dt + 1)
                                for fc in range(2):
                                    sl = slice((dt % 2) * Lc + fc * 512, (dt % 2) * Lc + (fc + 1) * 512)
                                    t.matmul(out=pm[4 + fc][0:1, 0:512], lhsT=t_ones[:],
                                             rhs=sqb[:, sl], start=(dt == 0), stop=(dt == DT - 1))
                                t.sem_inc(sem["PE"], 1)
                        C.inc("PE", DT)

                        @block.vector
                        def _(v):
                            v = W(v)
                            v.wait_ge(sem["PE"], C.get("PE"))
                            for fc in range(2):
                                v.tensor_scalar(out=t_rows[0][:, fc * 512:(fc + 1) * 512],
                                                in0=pm[4 + fc][0:1, 0:512], scalar1=1.0 / D, scalar2=None, op0=ALU.mult)
                            v.sem_inc(sem["DVE"], 1)
                        C.inc("DVE")

                        @block.scalar
                        def _(a):
                            a = W(a)
                            a.wait_ge(sem["DVE"], C.get("DVE"))
                            a.activation(out=t_rows[0][:], in_=t_rows[0][:], func=AF.Sqrt,
                                         bias=t_eps1[0:1, 0:1]).then_inc(sem["ACT"], 1)
                        C.inc("ACT")

                        @block.vector
                        def _(v):
                            v = W(v)
                            v.wait_ge(sem["ACT"], C.get("ACT"))
                            v.reciprocal(out=t_rows[0][:], in_=t_rows[0][:]).then_inc(sem["DVE"], 1)
                        C.inc("DVE")

                        @block.sync
                        def _(sp):
                            sp = W(sp)
                            sp.wait_ge(sem["DVE"], C.get("DVE"))
                            sp.dma_start(out=rows_bc[2:3, :], in_=t_rows[0][:]).then_inc(sem["LD"], 16); C.inc("LD")
                            sp.wait_ge(sem["LD"], 16 * C.get("LD"))
                            sp.dma_start(out=r2B[:], in_=rows_bc[2:3, :].broadcast_to([128, Lc])).then_inc(sem["LD"], 16); C.inc("LD")

                        @block.vector
                        def _(v):
                            v = W(v)
                            v.wait_ge(sem["LD"], 16 * C.get("LD"))
                            for dt in range(DT):
                                v.scalar_tensor_tensor(out=rnT[:, dt * Lc:(dt + 1) * Lc],
                                                       in0=bufH[:, dt * Lc:(dt + 1) * Lc],
                                                       scalar=NRMW(dt), in1=r2B[:],
                                                       op0=ALU.mult, op1=ALU.mult)
                            v.sem_inc(sem["DVE"], 1)
                        C.inc("DVE")
                        barrier()
                        if _STOP[0] == 'A':
                            return nc

                    with ExitStack() as sbk:
                        slabW = sbk.enter_context(nc.sbuf_tensor(f"slabW_{c}", [128, ET * Ec], BF16))
                        ebuf = [sbk.enter_context(nc.sbuf_tensor(f"ebB{i}_{c}", [128, 512], BF16)) for i in range(4)]

                        gd1 = C.get("GD")
                        @block.gpsimd
                        def _(g):
                            g = W(g)
                            for kt in range(DT):
                                g.dma_start(out=slabW[:, kt * Ec:(kt + 1) * Ec],
                                            in_=skip_wT[kt * 128:(kt + 1) * 128, :]).then_inc(sem["GD"], 16)
                                C.inc("GD")

                        units = [(fc, et) for fc in range(2) for et in range(ET)]
                        pe1 = C.get("PE")
                        act4 = C.get("ACT")
                        eb0 = [C.get("E%d" % i) for i in range(4)]
                        @block.tensor
                        def _(t):
                            t = W(t)
                            t.wait_ge(sem["GD"], 16 * C.get("GD"))
                            for u, (fc, et) in enumerate(units):
                                if u >= 4:
                                    t.wait_ge(sem["ACT"], act4 + (u - 3))
                                for kt in range(DT):
                                    t.matmul(out=pm[u % 4][:, 0:512],
                                             lhsT=slabW[:, kt * Ec + et * 128:kt * Ec + (et + 1) * 128],
                                             rhs=rnT[:, kt * Lc + fc * 512:kt * Lc + (fc + 1) * 512],
                                             start=(kt == 0), stop=(kt == DT - 1))
                                t.sem_inc(sem["PE"], 1)
                        C.inc("PE", len(units))

                        @block.scalar
                        def _(a):
                            a = W(a)
                            for u, (fc, et) in enumerate(units):
                                a.wait_ge(sem["PE"], pe1 + u + 1)
                                if u >= 4:
                                    a.wait_ge(sem["E%d" % (u % 4)], 16 * (eb0[u % 4] + (u - 4) // 4 + 1))
                                a.activation(out=ebuf[u % 4][:], in_=pm[u % 4][:, 0:512],
                                             func=AF.Silu).then_inc(sem["ACT"], 1)
                                C.inc("ACT")

                        @block.sync
                        def _(sp):
                            sp = W(sp)
                            for u, (fc, et) in enumerate(units):
                                sp.wait_ge(sem["ACT"], act4 + u + 1)
                                sp.dma_start(out=spill_ss[et * 128:(et + 1) * 128, fc * 512:(fc + 1) * 512],
                                             in_=ebuf[u % 4][:]).then_inc(sem["E%d" % (u % 4)], 16)
                                C.inc("E%d" % (u % 4))
                        barrier()
                        if _STOP[0] == 'B':
                            return nc

                    with ExitStack() as sck:
                        slabW = sck.enter_context(nc.sbuf_tensor(f"slabWc_{c}", [128, ET * Ec], BF16))
                        xin = [sck.enter_context(nc.sbuf_tensor(f"xin{i}_{c}", [128, Lc + 3], F32)) for i in range(2)]
                        cv1 = sck.enter_context(nc.sbuf_tensor(f"cv1_{c}", [128, Lc], F32))
                        cv2 = sck.enter_context(nc.sbuf_tensor(f"cv2_{c}", [128, Lc], F32))

                        @block.gpsimd
                        def _(g):
                            g = W(g)
                            for kt in range(DT):
                                g.dma_start(out=slabW[:, kt * Ec:(kt + 1) * Ec],
                                            in_=in_wT[kt * 128:(kt + 1) * 128, :]).then_inc(sem["GD"], 16)
                                C.inc("GD")

                        pe2 = C.get("PE")
                        act5 = C.get("ACT")
                        dve2 = C.get("DVE")
                        @block.tensor
                        def _(t):
                            t = W(t)
                            t.wait_ge(sem["GD"], 16 * C.get("GD"))
                            for et in range(ET):
                                for fc in range(2):
                                    u = et * 2 + fc
                                    if u >= 4:
                                        up = u - 4
                                        t.wait_ge(sem["ACT"], act5 + 4 * (up // 2) + (up % 2) + 1)
                                    for kt in range(DT):
                                        t.matmul(out=pm[u % 4][:, 0:512],
                                                 lhsT=slabW[:, kt * Ec + et * 128:kt * Ec + (et + 1) * 128],
                                                 rhs=rnT[:, kt * Lc + fc * 512:kt * Lc + (fc + 1) * 512],
                                                 start=(kt == 0), stop=(kt == DT - 1))
                                    t.sem_inc(sem["PE"], 1)
                        C.inc("PE", ET * 2)

                        @block.scalar
                        def _(a):
                            a = W(a)
                            for et in range(ET):
                                sl = et % 2
                                for fc in range(2):
                                    u = et * 2 + fc
                                    a.wait_ge(sem["PE"], pe2 + u + 1)
                                    if et >= 2:
                                        a.wait_ge(sem["DVE"], dve2 + 3 * (et - 1))
                                    a.activation(out=xin[sl][:, 3 + fc * 512:3 + (fc + 1) * 512],
                                                 in_=pm[u % 4][:, 0:512], func=AF.Copy).then_inc(sem["ACT"], 1)
                                    C.inc("ACT")
                                a.wait_ge(sem["DVE"], dve2 + 3 * et + 1)
                                a.activation(out=cv1[:], in_=xin[sl][:, 0:Lc], func=AF.Copy,
                                             scale=t_convw[:, et * 4:et * 4 + 1]).then_inc(sem["ACT"], 1)
                                C.inc("ACT")
                                a.wait_ge(sem["DVE"], dve2 + 3 * et + 3)
                                a.activation(out=xT[:, et * Lc:(et + 1) * Lc], in_=cv1[:],
                                             func=AF.Silu, bias=CONVB(et)).then_inc(sem["ACT"], 1)
                                C.inc("ACT")

                        @block.vector
                        def _(v):
                            v = W(v)
                            for et in range(ET):
                                sl = et % 2
                                v.wait_ge(sem["ACT"], act5 + 4 * et + 2)
                                if c == 0:
                                    v.memset(xin[sl][:, 0:3], 0.0)
                                else:
                                    v.tensor_copy(out=xin[sl][:, 0:3], in_=t_tail[:, et * 4:et * 4 + 3])
                                v.sem_inc(sem["DVE"], 1)
                                v.wait_ge(sem["ACT"], act5 + 4 * et + 3)
                                v.scalar_tensor_tensor(out=cv2[:], in0=xin[sl][:, 1:1 + Lc],
                                                       scalar=t_convw[:, et * 4 + 1:et * 4 + 2], in1=cv1[:],
                                                       op0=ALU.mult, op1=ALU.add)
                                v.scalar_tensor_tensor(out=cv1[:], in0=xin[sl][:, 2:2 + Lc],
                                                       scalar=t_convw[:, et * 4 + 2:et * 4 + 3], in1=cv2[:],
                                                       op0=ALU.mult, op1=ALU.add)
                                v.sem_inc(sem["DVE"], 1)
                                v.scalar_tensor_tensor(out=cv1[:], in0=xin[sl][:, 3:3 + Lc],
                                                       scalar=t_convw[:, et * 4 + 3:et * 4 + 4], in1=cv1[:],
                                                       op0=ALU.mult, op1=ALU.add)
                                if c == 0:
                                    v.tensor_copy(out=t_tail[:, et * 4:et * 4 + 3], in_=xin[sl][:, Lc:Lc + 3])
                                v.sem_inc(sem["DVE"], 1)
                        C.inc("DVE", 3 * ET)
                        barrier()
                        if _STOP[0] == 'C':
                            return nc

                with ExitStack() as sdk:
                    t_zTb = sdk.enter_context(nc.sbuf_tensor(f"t_zTb_{c}", [128, Lc], BF16))
                    sd1 = sdk.enter_context(ExitStack())
                    xTb = sd1.enter_context(nc.sbuf_tensor(f"xTb_{c}", [128, ET * Lc], BF16))
                    t_w1s = sd1.enter_context(nc.sbuf_tensor(f"t_w1s_{c}", [128, ET * 128], BF16))
                    t_wbs = sd1.enter_context(nc.sbuf_tensor(f"t_wbs_{c}", [128, ET * N], BF16))
                    t_wcs = sd1.enter_context(nc.sbuf_tensor(f"t_wcs_{c}", [128, ET * N], BF16))
                    t_zT = sd1.enter_context(nc.sbuf_tensor(f"t_zT_{c}", [128, Lc], F32))
                    t_bm = sd1.enter_context(nc.sbuf_tensor(f"t_bm_{c}", [16, Lc], F32))
                    t_cm = sd1.enter_context(nc.sbuf_tensor(f"t_cm_{c}", [16, Lc], F32))
                    t_bm16 = sd1.enter_context(nc.sbuf_tensor(f"t_bm16_{c}", [16, Lc], BF16))
                    t_cm16 = sd1.enter_context(nc.sbuf_tensor(f"t_cm16_{c}", [16, Lc], BF16))
                    t_zp = sd1.enter_context(nc.sbuf_tensor(f"t_zp_{c}", [128, Lc], F32))
                    t_bp = sd1.enter_context(nc.sbuf_tensor(f"t_bp_{c}", [16, Lc], F32))
                    t_cp = sd1.enter_context(nc.sbuf_tensor(f"t_cp_{c}", [16, Lc], F32))

                    @block.gpsimd
                    def _(g):
                        g = W(g)
                        for et in range(ET):
                            g.dma_start(out=xTb[:, et * Lc:(et + 1) * Lc],
                                        in_=xT[:, et * Lc:(et + 1) * Lc]).then_inc(sem["GD"], 16)
                            C.inc("GD")
                        for kt in range(ET):
                            g.dma_start(out=t_w1s[:, kt * 128:(kt + 1) * 128],
                                        in_=wd1T_in[kt * 128:(kt + 1) * 128, :]).then_inc(sem["GD"], 16)
                            C.inc("GD")
                            g.dma_start(out=t_wbs[:, kt * N:(kt + 1) * N],
                                        in_=wBT_in[kt * 128:(kt + 1) * 128, :]).then_inc(sem["GD"], 16)
                            C.inc("GD")
                            g.dma_start(out=t_wcs[:, kt * N:(kt + 1) * N],
                                        in_=wCT_in[kt * 128:(kt + 1) * 128, :]).then_inc(sem["GD"], 16)
                            C.inc("GD")

                    pe3 = C.get("PE")
                    dve3 = C.get("DVE")
                    @block.tensor
                    def _(t):
                        t = W(t)
                        t.wait_ge(sem["GD"], 16 * C.get("GD"))
                        for fc in range(2):
                            for kt in range(ET):
                                t.matmul(out=pm[4 + fc][:, 0:512],
                                         lhsT=t_w1s[:, kt * 128:(kt + 1) * 128],
                                         rhs=xTb[:, kt * Lc + fc * 512:kt * Lc + (fc + 1) * 512],
                                         start=(kt == 0), stop=(kt == ET - 1))
                            t.sem_inc(sem["PE"], 1)
                        for fc in range(2):
                            if fc == 1:
                                t.wait_ge(sem["DVE"], dve3 + 3)
                            for kt in range(ET):
                                t.matmul(out=pm[6][0:16, 0:512],
                                         lhsT=t_wbs[:, kt * N:(kt + 1) * N],
                                         rhs=xTb[:, kt * Lc + fc * 512:kt * Lc + (fc + 1) * 512],
                                         start=(kt == 0), stop=(kt == ET - 1))
                                t.matmul(out=pm[7][0:16, 0:512],
                                         lhsT=t_wcs[:, kt * N:(kt + 1) * N],
                                         rhs=xTb[:, kt * Lc + fc * 512:kt * Lc + (fc + 1) * 512],
                                         start=(kt == 0), stop=(kt == ET - 1))
                            t.sem_inc(sem["PE"], 1)
                    C.inc("PE", 4)

                    @block.vector
                    def _(v):
                        v = W(v)
                        v.wait_ge(sem["PE"], pe3 + 1)
                        v.tensor_copy(out=t_zT[:, 0:512], in_=pm[4][:, 0:512])
                        v.sem_inc(sem["DVE"], 1)
                        v.wait_ge(sem["PE"], pe3 + 2)
                        v.tensor_copy(out=t_zT[:, 512:1024], in_=pm[5][:, 0:512])
                        v.sem_inc(sem["DVE"], 1)
                        v.wait_ge(sem["PE"], pe3 + 3)
                        v.tensor_copy(out=t_bm[:, 0:512], in_=pm[6][0:16, 0:512])
                        v.tensor_copy(out=t_cm[:, 0:512], in_=pm[7][0:16, 0:512])
                        v.sem_inc(sem["DVE"], 1)
                        v.wait_ge(sem["PE"], pe3 + 4)
                        v.tensor_copy(out=t_bm[:, 512:1024], in_=pm[6][0:16, 0:512])
                        v.tensor_copy(out=t_cm[:, 512:1024], in_=pm[7][0:16, 0:512])
                        v.sem_inc(sem["DVE"], 1)
                    C.inc("DVE", 4)

                    @block.sync
                    def _(sp):
                        sp = W(sp)
                        sp.wait_ge(sem["DVE"], C.get("DVE"))
                        base0 = (0 * 2 + c) * 160
                        base1 = (1 * 2 + c) * 160
                        with sp.If_eq(reg_role, 0):
                            sp.dma_start(out=sh_zbc[base0:base0 + 128, :], in_=t_zT[:]).then_inc(sem["LD"], 16)
                            sp.dma_start(out=sh_zbc[base0 + 128:base0 + 144, :], in_=t_bm[:]).then_inc(sem["LD"], 16)
                            sp.dma_start(out=sh_zbc[base0 + 144:base0 + 160, :], in_=t_cm[:]).then_inc(sem["LD"], 16)
                        with sp.Else():
                            sp.dma_start(out=sh_zbc[base1:base1 + 128, :], in_=t_zT[:]).then_inc(sem["LD"], 16)
                            sp.dma_start(out=sh_zbc[base1 + 128:base1 + 144, :], in_=t_bm[:]).then_inc(sem["LD"], 16)
                            sp.dma_start(out=sh_zbc[base1 + 144:base1 + 160, :], in_=t_cm[:]).then_inc(sem["LD"], 16)
                        C.inc("LD", 3)
                        sp.wait_ge(sem["LD"], 16 * C.get("LD"))
                        f0, f1 = fl_zbc(0, c), fl_zbc(1, c)
                        with sp.If_eq(reg_role, 0):
                            sp.dma_start(out=sh_flag[f0:f0 + 1, :], in_=t_flagv[:, 16 * f0:16 * (f0 + 1)]).then_inc(sem["LD"], 16)
                        with sp.Else():
                            sp.dma_start(out=sh_flag[f1:f1 + 1, :], in_=t_flagv[:, 16 * f1:16 * (f1 + 1)]).then_inc(sem["LD"], 16)
                        C.inc("LD")
                        sp.wait_ge(sem["LD"], 16 * C.get("LD"))
                        poll(sp, f0)
                        poll(sp, f1)
                        with sp.If_eq(reg_role, 0):
                            sp.dma_start(out=t_zp[:], in_=sh_zbc[base1:base1 + 128, :]).then_inc(sem["LD"], 16)
                            sp.dma_start(out=t_bp[:], in_=sh_zbc[base1 + 128:base1 + 144, :]).then_inc(sem["LD"], 16)
                            sp.dma_start(out=t_cp[:], in_=sh_zbc[base1 + 144:base1 + 160, :]).then_inc(sem["LD"], 16)
                        with sp.Else():
                            sp.dma_start(out=t_zp[:], in_=sh_zbc[base0:base0 + 128, :]).then_inc(sem["LD"], 16)
                            sp.dma_start(out=t_bp[:], in_=sh_zbc[base0 + 128:base0 + 144, :]).then_inc(sem["LD"], 16)
                            sp.dma_start(out=t_cp[:], in_=sh_zbc[base0 + 144:base0 + 160, :]).then_inc(sem["LD"], 16)
                        C.inc("LD", 3)

                    @block.vector
                    def _(v):
                        v = W(v)
                        v.wait_ge(sem["LD"], 16 * C.get("LD"))
                        v.tensor_tensor(out=t_zT[:], in0=t_zT[:], in1=t_zp[:], op=ALU.add)
                        v.tensor_tensor(out=t_bm[:], in0=t_bm[:], in1=t_bp[:], op=ALU.add)
                        v.tensor_tensor(out=t_cm[:], in0=t_cm[:], in1=t_cp[:], op=ALU.add)
                        v.tensor_scalar(out=t_zTb[:], in0=t_zT[:], scalar1=1.0, scalar2=None, op0=ALU.mult)
                        v.tensor_scalar(out=t_bm16[:], in0=t_bm[:], scalar1=1.0, scalar2=None, op0=ALU.mult)
                        v.tensor_scalar(out=t_cm16[:], in0=t_cm[:], scalar1=1.0, scalar2=None, op0=ALU.mult)
                        v.sem_inc(sem["DVE"], 1)
                    C.inc("DVE")

                    @block.sync
                    def _(sp):
                        sp = W(sp)
                        sp.wait_ge(sem["DVE"], C.get("DVE"))
                        sp.dma_start(out=bmcm_dr[0:16, :], in_=t_bm16[:]).then_inc(sem["LD"], 16)
                        sp.dma_start(out=bmcm_dr[16:32, :], in_=t_cm16[:]).then_inc(sem["LD"], 16)
                        C.inc("LD", 2)
                    barrier()
                    sd1.close()
                    if _STOP[0] == 'D':
                        return nc

                    t_w2s = sdk.enter_context(nc.sbuf_tensor(f"t_w2s_{c}", [128, ET * 128], BF16))
                    dT = sdk.enter_context(nc.sbuf_tensor(f"dT_{c}", [128, EB * Lc], F32))
                    dx16 = sdk.enter_context(nc.sbuf_tensor(f"dx16_{c}", [128, EB * Lc], BF16))
                    yblk = sdk.enter_context(nc.sbuf_tensor(f"yblk_{c}", [128, EB * Lc], F32))
                    abuf = [sdk.enter_context(nc.sbuf_tensor(f"abuf{i}_{c}", [128, Lc], F32)) for i in range(2)]
                    bbuf = [sdk.enter_context(nc.sbuf_tensor(f"bbuf{i}_{c}", [128, Lc], BF16)) for i in range(2)]
                    hbuf = [sdk.enter_context(nc.sbuf_tensor(f"hbuf{i}_{c}", [128, Lc], BF16)) for i in range(2)]
                    tmpb = [sdk.enter_context(nc.sbuf_tensor(f"tmpb{i}_{c}", [128, Lc], BF16)) for i in range(2)]
                    bcB = [sdk.enter_context(nc.sbuf_tensor(f"bcB{i}_{c}", [128, Lc], BF16)) for i in range(2)]
                    bcC = [sdk.enter_context(nc.sbuf_tensor(f"bcC{i}_{c}", [128, Lc], BF16)) for i in range(2)]
                    relub = sdk.enter_context(nc.sbuf_tensor(f"relub_{c}", [128, 512], F32))
                    actb = sdk.enter_context(nc.sbuf_tensor(f"actb_{c}", [128, 512], F32))
                    ssk = sdk.enter_context(nc.sbuf_tensor(f"ssk_{c}", [128, Lc], BF16))
                    ywt = sdk.enter_context(nc.sbuf_tensor(f"ywt_{c}", [128, Lc], F32))
                    ypt = sdk.enter_context(nc.sbuf_tensor(f"ypt_{c}", [128, Lc], BF16))

                    @block.gpsimd
                    def _(g):
                        g = W(g)
                        for kt in range(ET):
                            g.dma_start(out=t_w2s[:, kt * 128:(kt + 1) * 128],
                                        in_=wd2T_in[:, kt * 128:(kt + 1) * 128]).then_inc(sem["GD"], 16)
                            C.inc("GD")

                    for blk in range(NBLK):
                        pe4 = C.get("PE")
                        act6 = C.get("ACT")
                        dve4 = C.get("DVE")
                        drelu = lambda u: 5 * (u // 2) + 2 * (u % 2) + 1
                        dadd = lambda u: 5 * (u // 2) + 2 * (u % 2) + 2
                        @block.tensor
                        def _(t):
                            t = W(t)
                            t.wait_ge(sem["GD"], 16 * C.get("GD"))
                            t.wait_ge(sem["DVE"], dve4)
                            for el in range(EB):
                                et = blk * EB + el
                                for fc in range(2):
                                    u = el * 2 + fc
                                    if u >= 2:
                                        t.wait_ge(sem["DVE"], dve4 + drelu(u - 2))
                                    t.matmul(out=pm[4 + u % 2][:, 0:512],
                                             lhsT=t_w2s[:, et * 128:(et + 1) * 128],
                                             rhs=t_zTb[:, fc * 512:(fc + 1) * 512],
                                             start=True, stop=True)
                                    t.sem_inc(sem["PE"], 1)
                        C.inc("PE", EB * 2)

                        @block.scalar
                        def _(a):
                            a = W(a)
                            for el in range(EB):
                                et = blk * EB + el
                                for fc in range(2):
                                    u = el * 2 + fc
                                    a.wait_ge(sem["PE"], pe4 + u + 1)
                                    if u >= 1:
                                        a.wait_ge(sem["DVE"], dve4 + dadd(u - 1))
                                    a.activation(out=actb[:], in_=pm[4 + u % 2][:, 0:512],
                                                 func=AF.Abs, bias=WD2B(et)).then_inc(sem["ACT"], 1)
                                    a.activation(out=actb[:], in_=actb[:], func=AF.Exp,
                                                 scale=-1.0).then_inc(sem["ACT"], 1)
                                    a.activation(out=actb[:], in_=actb[:], func=AF.Ln,
                                                 bias=t_one1[:]).then_inc(sem["ACT"], 1)
                                    C.inc("ACT", 3)

                        @block.vector
                        def _(v):
                            v = W(v)
                            for el in range(EB):
                                et = blk * EB + el
                                for fc in range(2):
                                    u = el * 2 + fc
                                    v.wait_ge(sem["ACT"], act6 + 3 * u + 3)
                                    v.tensor_scalar(out=relub[:], in0=pm[4 + u % 2][:, 0:512],
                                                    scalar1=WD2B(et), scalar2=0.0, op0=ALU.add, op1=ALU.max)
                                    v.sem_inc(sem["DVE"], 1)
                                    v.tensor_tensor(out=dT[:, el * Lc + fc * 512:el * Lc + (fc + 1) * 512],
                                                    in0=relub[:], in1=actb[:], op=ALU.add)
                                    v.sem_inc(sem["DVE"], 1)
                                v.tensor_tensor(out=dx16[:, el * Lc:(el + 1) * Lc],
                                                in0=dT[:, el * Lc:(el + 1) * Lc],
                                                in1=xT[:, (blk * EB + el) * Lc:(blk * EB + el + 1) * Lc],
                                                op=ALU.mult)
                                v.sem_inc(sem["DVE"], 1)
                        C.inc("DVE", EB * 5)

                        dve5 = C.get("DVE")
                        act7 = C.get("ACT")
                        pl0 = C.get("PL")
                        bc0 = [C.get("BC0"), C.get("BC1")]
                        bccnt = lambda n: 16 * 2 * (bc0[n % 2] // 2 + n // 2 + 1)
                        @block.sync
                        def _(sp):
                            sp = W(sp)
                            for n in range(N):
                                if n >= 2:
                                    sp.wait_ge(sem["DVE"], dve5 + (n - 1) * EB * 3)
                                sp.dma_start(out=bcB[n % 2][:],
                                             in_=bmcm_dr[n:n + 1, :].broadcast_to([128, Lc])).then_inc(sem["BC%d" % (n % 2)], 16)
                                sp.dma_start(out=bcC[n % 2][:],
                                             in_=bmcm_dr[16 + n:17 + n, :].broadcast_to([128, Lc])).then_inc(sem["BC%d" % (n % 2)], 16)
                                C.inc("BC%d" % (n % 2), 2)

                        @block.scalar
                        def _(a):
                            a = W(a)
                            a.wait_ge(sem["DVE"], dve5)
                            for n in range(N):
                                for el in range(EB):
                                    u = n * EB + el
                                    et = blk * EB + el
                                    if u >= 2:
                                        a.wait_ge(sem["DVE"], dve5 + 3 * (u - 1) - 1)
                                    a.activation(out=abuf[u % 2][:], in_=dT[:, el * Lc:(el + 1) * Lc],
                                                 func=AF.Exp, scale=t_negA[:, et * N + n:et * N + n + 1]
                                                 ).then_inc(sem["ACT"], 1)
                                    C.inc("ACT")

                        @block.vector
                        def _(v):
                            v = W(v)
                            for n in range(N):
                                for el in range(EB):
                                    u = n * EB + el
                                    et = blk * EB + el
                                    if el == 0:
                                        v.wait_ge(sem["BC%d" % (n % 2)], bccnt(n))
                                    v.tensor_tensor(out=bbuf[u % 2][:], in0=dx16[:, el * Lc:(el + 1) * Lc],
                                                    in1=bcB[n % 2][:], op=ALU.mult)
                                    v.sem_inc(sem["DVE"], 1)
                                    v.wait_ge(sem["ACT"], act7 + u + 1)
                                    init = 0.0 if c == 0 else t_state[:, et * N + n:et * N + n + 1]
                                    v.tensor_tensor_scan(out=hbuf[u % 2][:], data0=abuf[u % 2][:],
                                                         data1=bbuf[u % 2][:], initial=init,
                                                         op0=ALU.mult, op1=ALU.add)
                                    v.sem_inc(sem["DVE"], 1)
                                    if u >= 2:
                                        v.wait_ge(sem["PL"], pl0 + (u - 1))
                                    v.tensor_tensor(out=tmpb[u % 2][:], in0=hbuf[u % 2][:],
                                                    in1=bcC[n % 2][:], op=ALU.mult)
                                    v.sem_inc(sem["DVE"], 1)
                        C.inc("DVE", N * EB * 3)

                        @block.gpsimd
                        def _(g):
                            g = W(g)
                            for n in range(N):
                                for el in range(EB):
                                    u = n * EB + el
                                    et = blk * EB + el
                                    g.wait_ge(sem["DVE"], dve5 + 3 * u + 3)
                                    if c == 0 and NCH > 1:
                                        g.tensor_copy(out=t_state[:, et * N + n:et * N + n + 1],
                                                      in_=hbuf[u % 2][:, Lc - 1:Lc])
                                    if n == 0:
                                        g.tensor_copy(out=yblk[:, el * Lc:(el + 1) * Lc], in_=tmpb[u % 2][:])
                                    else:
                                        g.tensor_tensor(out=yblk[:, el * Lc:(el + 1) * Lc],
                                                        in0=yblk[:, el * Lc:(el + 1) * Lc],
                                                        in1=tmpb[u % 2][:], op=ALU.add)
                                    g.sem_inc(sem["PL"], 1)
                        C.inc("PL", N * EB)

                        sk0 = C.get("SK")
                        yp0 = C.get("YP")
                        dve6 = C.get("DVE")
                        @block.sync
                        def _(sp):
                            sp = W(sp)
                            for el in range(EB):
                                et = blk * EB + el
                                if el >= 1:
                                    sp.wait_ge(sem["DVE"], dve6 + 2 * el)
                                sp.dma_start(out=ssk[:], in_=spill_ss[et * 128:(et + 1) * 128, :]).then_inc(sem["SK"], 16)
                                C.inc("SK")
                                sp.wait_ge(sem["DVE"], dve6 + 2 * el + 2)
                                sp.dma_start(out=spill_yp[et * 128:(et + 1) * 128, :], in_=ypt[:]).then_inc(sem["YP"], 16)
                                C.inc("YP")

                        @block.vector
                        def _(v):
                            v = W(v)
                            v.wait_ge(sem["PL"], C.get("PL"))
                            for el in range(EB):
                                et = blk * EB + el
                                v.scalar_tensor_tensor(out=ywt[:], in0=xT[:, et * Lc:(et + 1) * Lc],
                                                       scalar=WDC(et), in1=yblk[:, el * Lc:(el + 1) * Lc],
                                                       op0=ALU.mult, op1=ALU.add)
                                v.sem_inc(sem["DVE"], 1)
                                v.wait_ge(sem["SK"], 16 * (sk0 + el + 1))
                                if el >= 1:
                                    v.wait_ge(sem["YP"], 16 * (yp0 + el))
                                v.tensor_tensor(out=ypt[:], in0=ywt[:], in1=ssk[:], op=ALU.mult)
                                v.sem_inc(sem["DVE"], 1)
                        C.inc("DVE", 2 * EB)
                    barrier()
                    if _STOP[0] == 'E':
                        return nc

                with ExitStack() as sfk:
                    slabO = sfk.enter_context(nc.sbuf_tensor(f"slabO_{c}", [128, ET * D], BF16))
                    ypb = sfk.enter_context(nc.sbuf_tensor(f"ypb_{c}", [128, ET * Lc], BF16))
                    ofb = [sfk.enter_context(nc.sbuf_tensor(f"ofb{i}_{c}", [128, 512], F32)) for i in range(4)]

                    @block.gpsimd
                    def _(g):
                        g = W(g)
                        for kt in range(ET):
                            g.dma_start(out=slabO[:, kt * D:(kt + 1) * D],
                                        in_=out_wT[kt * 128:(kt + 1) * 128, :]).then_inc(sem["GD"], 16)
                            C.inc("GD")

                    ld4 = C.get("LD")
                    @block.sync
                    def _(sp):
                        sp = W(sp)
                        for kt in range(ET):
                            sp.dma_start(out=ypb[:, kt * Lc:(kt + 1) * Lc],
                                         in_=spill_yp[kt * 128:(kt + 1) * 128, :]).then_inc(sem["LD"], 16)
                            C.inc("LD")

                    units = [(fc, dt) for fc in range(2) for dt in range(DT)]
                    pe5 = C.get("PE")
                    dve7 = C.get("DVE")
                    ef0 = [C.get("E%d" % i) for i in range(4)]
                    @block.tensor
                    def _(t):
                        t = W(t)
                        t.wait_ge(sem["GD"], 16 * C.get("GD"))
                        t.wait_ge(sem["LD"], 16 * (ld4 + ET))
                        for u, (fc, dt) in enumerate(units):
                            if u >= 4:
                                t.wait_ge(sem["DVE"], dve7 + (u - 3))
                            for kt in range(ET):
                                t.matmul(out=pm[u % 4][:, 0:512],
                                         lhsT=slabO[:, kt * D + dt * 128:kt * D + (dt + 1) * 128],
                                         rhs=ypb[:, kt * Lc + fc * 512:kt * Lc + (fc + 1) * 512],
                                         start=(kt == 0), stop=(kt == ET - 1))
                            t.sem_inc(sem["PE"], 1)
                    C.inc("PE", len(units))

                    @block.vector
                    def _(v):
                        v = W(v)
                        for u, (fc, dt) in enumerate(units):
                            v.wait_ge(sem["PE"], pe5 + u + 1)
                            if u >= 4:
                                v.wait_ge(sem["E%d" % (u % 4)], 16 * (ef0[u % 4] + (u - 4) // 4 + 1))
                            v.tensor_copy(out=ofb[u % 4][:], in_=pm[u % 4][:, 0:512])
                            v.sem_inc(sem["DVE"], 1)
                    C.inc("DVE", len(units))

                    @block.sync
                    def _(sp):
                        sp = W(sp)
                        for u, (fc, dt) in enumerate(units):
                            sp.wait_ge(sem["DVE"], dve7 + u + 1)
                            r0 = (0 * 2 + c) * D + dt * 128
                            r1 = (1 * 2 + c) * D + dt * 128
                            with sp.If_eq(reg_role, 0):
                                sp.dma_start(out=sh_out[r0:r0 + 128, fc * 512:(fc + 1) * 512],
                                             in_=ofb[u % 4][:]).then_inc(sem["E%d" % (u % 4)], 16)
                            with sp.Else():
                                sp.dma_start(out=sh_out[r1:r1 + 128, fc * 512:(fc + 1) * 512],
                                             in_=ofb[u % 4][:]).then_inc(sem["E%d" % (u % 4)], 16)
                            C.inc("E%d" % (u % 4))
                        for i in range(4):
                            sp.wait_ge(sem["E%d" % i], 16 * C.get("E%d" % i))
                        f0, f1 = fl_out(0, c), fl_out(1, c)
                        with sp.If_eq(reg_role, 0):
                            sp.dma_start(out=sh_flag[f0:f0 + 1, :], in_=t_flagv[:, 16 * f0:16 * (f0 + 1)]).then_inc(sem["LD"], 16)
                        with sp.Else():
                            sp.dma_start(out=sh_flag[f1:f1 + 1, :], in_=t_flagv[:, 16 * f1:16 * (f1 + 1)]).then_inc(sem["LD"], 16)
                        C.inc("LD")
                    barrier()
                    if _STOP[0] == 'F' and c == NCH - 1:
                        return nc

        with ExitStack() as sgk:
            x1T = sgk.enter_context(nc.sbuf_tensor("x1T", [128, DT * Lc], F32))
            scons = sgk.enter_context(ExitStack())
            tA = scons.enter_context(nc.sbuf_tensor("tA", [128, Lc], F32))
            tB = scons.enter_context(nc.sbuf_tensor("tB", [128, Lc], F32))
            tR = scons.enter_context(nc.sbuf_tensor("tR", [128, Lc], F32))
            tH = scons.enter_context(nc.sbuf_tensor("tH", [128, Lc], BF16))

            @block.sync
            def _(sp):
                sp = W(sp)
                sp.wait_ge(sem["LD"], 16 * C.get("LD"))
                for f in [fl_out(0, 0), fl_out(0, 1), fl_out(1, 0), fl_out(1, 1)]:
                    poll(sp, f)

            dve8 = C.get("DVE")
            ca0, cb0, cr0, ch0 = C.get("CA"), C.get("CB"), C.get("CR"), C.get("CH")
            @block.sync
            def _(sp):
                sp = W(sp)
                for dt in range(DT):
                    if dt >= 1:
                        sp.wait_ge(sem["DVE"], dve8 + 3 * dt)
                    with sp.If_eq(reg_role, 0):
                        sp.dma_start(out=tA[:], in_=sh_out[0 * D + dt * 128:0 * D + (dt + 1) * 128, :]).then_inc(sem["CA"], 16)
                        sp.dma_start(out=tB[:], in_=sh_out[2 * D + dt * 128:2 * D + (dt + 1) * 128, :]).then_inc(sem["CB"], 16)
                        sp.dma_start(out=tR[:], in_=residT[dt * 128:(dt + 1) * 128, 0:Lc]).then_inc(sem["CR"], 16)
                        sp.dma_start(out=tH[:], in_=spill_h[dt * 128:(dt + 1) * 128, :]).then_inc(sem["CH"], 16)
                    with sp.Else():
                        sp.dma_start(out=tA[:], in_=sh_out[3 * D + dt * 128:3 * D + (dt + 1) * 128, :]).then_inc(sem["CA"], 16)
                        sp.dma_start(out=tB[:], in_=sh_out[1 * D + dt * 128:1 * D + (dt + 1) * 128, :]).then_inc(sem["CB"], 16)
                        sp.dma_start(out=tR[:], in_=residT[dt * 128:(dt + 1) * 128, Lc:2 * Lc]).then_inc(sem["CR"], 16)
                        sp.dma_start(out=tH[:], in_=spill_h[D + dt * 128:D + (dt + 1) * 128, :]).then_inc(sem["CH"], 16)
                    C.inc("CA"); C.inc("CB"); C.inc("CR"); C.inc("CH")

            @block.vector
            def _(v):
                v = W(v)
                for dt in range(DT):
                    v.wait_ge(sem["CA"], 16 * (ca0 + dt + 1))
                    v.wait_ge(sem["CB"], 16 * (cb0 + dt + 1))
                    v.wait_ge(sem["CR"], 16 * (cr0 + dt + 1))
                    v.wait_ge(sem["CH"], 16 * (ch0 + dt + 1))
                    v.tensor_tensor(out=tA[:], in0=tA[:], in1=tB[:], op=ALU.add)
                    v.sem_inc(sem["DVE"], 1)
                    v.tensor_tensor(out=tA[:], in0=tA[:], in1=tH[:], op=ALU.add)
                    v.sem_inc(sem["DVE"], 1)
                    v.scalar_tensor_tensor(out=x1T[:, dt * Lc:(dt + 1) * Lc], in0=tA[:],
                                           scalar=GGA(dt), in1=tR[:], op0=ALU.mult, op1=ALU.add)
                    v.sem_inc(sem["DVE"], 1)
            C.inc("DVE", 3 * DT)
            barrier()
            scons.close()
            if _STOP[0] == 'X':
                return nc

            with ExitStack() as sg2:
                h2 = sg2.enter_context(nc.sbuf_tensor("h2", [128, DT * Lc], BF16))
                sgLN = sg2.enter_context(ExitStack())
                b16g = sgLN.enter_context(nc.sbuf_tensor("b16g", [128, 2 * Lc], BF16))
                sqg = sgLN.enter_context(nc.sbuf_tensor("sqg", [128, 2 * Lc], BF16))
                mB2 = sgLN.enter_context(nc.sbuf_tensor("mB2", [128, Lc], F32))
                rB2 = sgLN.enter_context(nc.sbuf_tensor("rB2", [128, Lc], F32))
                tvg = sgLN.enter_context(nc.sbuf_tensor("tvg", [128, Lc], F32))

                pe6 = C.get("PE")
                gg0 = [C.get("GB0"), C.get("GB1")]
                gcnt2 = lambda dt: 16 * (gg0[dt % 2] + dt // 2 + 1)
                @block.gpsimd
                def _(g):
                    g = W(g)
                    for dt in range(DT):
                        if dt >= 2:
                            g.wait_ge(sem["PE"], pe6 + dt - 1)
                        g.dma_start(out=b16g[:, (dt % 2) * Lc:(dt % 2 + 1) * Lc],
                                    in_=x1T[:, dt * Lc:(dt + 1) * Lc]).then_inc(sem["GB%d" % (dt % 2)], 16)
                        C.inc("GB%d" % (dt % 2))

                act8 = C.get("ACT")
                @block.scalar
                def _(a):
                    a = W(a)
                    for dt in range(DT):
                        if dt >= 2:
                            a.wait_ge(sem["PE"], pe6 + dt - 1)
                        a.activation(out=sqg[:, (dt % 2) * Lc:(dt % 2 + 1) * Lc],
                                     in_=x1T[:, dt * Lc:(dt + 1) * Lc],
                                     func=AF.Square).then_inc(sem["ACT"], 1)
                        C.inc("ACT")

                @block.tensor
                def _(t):
                    t = W(t)
                    for dt in range(DT):
                        t.wait_ge(sem["GB%d" % (dt % 2)], gcnt2(dt))
                        t.wait_ge(sem["ACT"], act8 + dt + 1)
                        for fc in range(2):
                            sl = slice((dt % 2) * Lc + fc * 512, (dt % 2) * Lc + (fc + 1) * 512)
                            t.matmul(out=pm[4 + fc][0:1, 0:512], lhsT=t_ones[:],
                                     rhs=b16g[:, sl], start=(dt == 0), stop=(dt == DT - 1))
                            t.matmul(out=pm[6 + fc][0:1, 0:512], lhsT=t_ones[:],
                                     rhs=sqg[:, sl], start=(dt == 0), stop=(dt == DT - 1))
                        t.sem_inc(sem["PE"], 1)
                C.inc("PE", DT)

                @block.vector
                def _(v):
                    v = W(v)
                    v.wait_ge(sem["PE"], C.get("PE"))
                    for fc in range(2):
                        v.tensor_scalar(out=t_rows[0][:, fc * 512:(fc + 1) * 512],
                                        in0=pm[4 + fc][0:1, 0:512], scalar1=1.0 / D, scalar2=None, op0=ALU.mult)
                        v.tensor_scalar(out=t_rows[1][:, fc * 512:(fc + 1) * 512],
                                        in0=pm[6 + fc][0:1, 0:512], scalar1=1.0 / D, scalar2=None, op0=ALU.mult)
                    v.tensor_tensor(out=t_rows[2][:], in0=t_rows[0][:], in1=t_rows[0][:], op=ALU.mult)
                    v.tensor_tensor(out=t_rows[1][:], in0=t_rows[1][:], in1=t_rows[2][:], op=ALU.subtract)
                    v.sem_inc(sem["DVE"], 1)
                C.inc("DVE")

                @block.scalar
                def _(a):
                    a = W(a)
                    a.wait_ge(sem["DVE"], C.get("DVE"))
                    a.activation(out=t_rows[1][:], in_=t_rows[1][:], func=AF.Sqrt,
                                 bias=t_eps1[0:1, 0:1]).then_inc(sem["ACT"], 1)
                C.inc("ACT")

                @block.vector
                def _(v):
                    v = W(v)
                    v.wait_ge(sem["ACT"], C.get("ACT"))
                    v.reciprocal(out=t_rows[1][:], in_=t_rows[1][:]).then_inc(sem["DVE"], 1)
                C.inc("DVE")

                @block.sync
                def _(sp):
                    sp = W(sp)
                    sp.wait_ge(sem["DVE"], C.get("DVE"))
                    sp.dma_start(out=rows_bc[3:4, :], in_=t_rows[0][:]).then_inc(sem["LD"], 16); C.inc("LD")
                    sp.dma_start(out=rows_bc[4:5, :], in_=t_rows[1][:]).then_inc(sem["LD"], 16); C.inc("LD")
                    sp.wait_ge(sem["LD"], 16 * C.get("LD"))
                    sp.dma_start(out=mB2[:], in_=rows_bc[3:4, :].broadcast_to([128, Lc])).then_inc(sem["LD"], 16); C.inc("LD")
                    sp.dma_start(out=rB2[:], in_=rows_bc[4:5, :].broadcast_to([128, Lc])).then_inc(sem["LD"], 16); C.inc("LD")

                @block.vector
                def _(v):
                    v = W(v)
                    v.wait_ge(sem["LD"], 16 * C.get("LD"))
                    for dt in range(DT):
                        v.tensor_tensor(out=tvg[:], in0=x1T[:, dt * Lc:(dt + 1) * Lc], in1=mB2[:], op=ALU.subtract)
                        v.tensor_tensor(out=tvg[:], in0=tvg[:], in1=rB2[:], op=ALU.mult)
                        v.tensor_scalar(out=h2[:, dt * Lc:(dt + 1) * Lc], in0=tvg[:],
                                        scalar1=GCB(dt), scalar2=GSB(dt), op0=ALU.mult, op1=ALU.add)
                    v.sem_inc(sem["DVE"], 1)
                C.inc("DVE")
                barrier()
                sgLN.close()
                if _STOP[0] == 'G1':
                    return nc

                with ExitStack() as sg3:
                    bss = sg3.enter_context(nc.sbuf_tensor("bss", [16, D], BF16))
                    brss = sg3.enter_context(nc.sbuf_tensor("brss", [96, D], BF16))
                    s16 = sg3.enter_context(nc.sbuf_tensor("s16", [16, Lc], BF16))
                    h96 = sg3.enter_context(nc.sbuf_tensor("h96", [96, Lc], BF16))
                    sg3b = sg3.enter_context(ExitStack())
                    wrs = sg3b.enter_context(nc.sbuf_tensor("wrs", [128, ET * 6], BF16))
                    ass = sg3b.enter_context(nc.sbuf_tensor("ass", [128, ET * N], BF16))
                    arts = sg3b.enter_context(nc.sbuf_tensor("arts", [128, ET * 96], BF16))
                    lgt = sg3b.enter_context(nc.sbuf_tensor("lgt", [128, Lc], F32))
                    gtok = sg3b.enter_context(nc.sbuf_tensor("gtok", [128, 48], F32))
                    gm = sg3b.enter_context(nc.sbuf_tensor("gm", [128, 24], F32))
                    ge_ = sg3b.enter_context(nc.sbuf_tensor("ge", [128, 48], F32))
                    gw = sg3b.enter_context(nc.sbuf_tensor("gw", [128, 48], F32))
                    gpad = sg3b.enter_context(nc.sbuf_tensor("gpad", [128, 128], F32))
                    wT6 = sg3b.enter_context(nc.sbuf_tensor("wT6", [6, Lc], F32))
                    w96 = sg3b.enter_context(nc.sbuf_tensor("w96", [96, Lc], F32))

                    @block.gpsimd
                    def _(g):
                        g = W(g)
                        for kt in range(DT):
                            g.dma_start(out=wrs[:, kt * 6:(kt + 1) * 6],
                                        in_=wrout_in[kt * 128:(kt + 1) * 128, :]).then_inc(sem["GD"], 16)
                            g.dma_start(out=ass[:, kt * N:(kt + 1) * N],
                                        in_=as_in[kt * 128:(kt + 1) * 128, :]).then_inc(sem["GD"], 16)
                            g.dma_start(out=arts[:, kt * 96:(kt + 1) * 96],
                                        in_=art_in[kt * 128:(kt + 1) * 128, :]).then_inc(sem["GD"], 16)
                            C.inc("GD", 3)
                        g.dma_start(out=bss[:], in_=bs_in).then_inc(sem["GD"], 16); C.inc("GD")
                        g.dma_start(out=brss[:], in_=brs_in).then_inc(sem["GD"], 16); C.inc("GD")

                    pe7 = C.get("PE")
                    @block.tensor
                    def _(t):
                        t = W(t)
                        t.wait_ge(sem["GD"], 16 * C.get("GD"))
                        for fc in range(2):
                            for kt in range(DT):
                                t.matmul(out=pm[4][0:6, 0:512], lhsT=wrs[:, kt * 6:(kt + 1) * 6],
                                         rhs=h2[:, kt * Lc + fc * 512:kt * Lc + (fc + 1) * 512],
                                         start=(kt == 0), stop=(kt == DT - 1))
                                t.matmul(out=pm[5][0:16, 0:512], lhsT=ass[:, kt * N:(kt + 1) * N],
                                         rhs=h2[:, kt * Lc + fc * 512:kt * Lc + (fc + 1) * 512],
                                         start=(kt == 0), stop=(kt == DT - 1))
                                t.matmul(out=pm[6][0:96, 0:512], lhsT=arts[:, kt * 96:(kt + 1) * 96],
                                         rhs=h2[:, kt * Lc + fc * 512:kt * Lc + (fc + 1) * 512],
                                         start=(kt == 0), stop=(kt == DT - 1))
                            t.sem_inc(sem["PE"], 1)
                            if fc == 0:
                                t.wait_ge(sem["DVE"], C.get("DVE") + 1)
                    C.inc("PE", 2)

                    @block.vector
                    def _(v):
                        v = W(v)
                        v.memset(lgt[:], 0.0)
                        for fc in range(2):
                            v.wait_ge(sem["PE"], pe7 + fc + 1)
                            v.tensor_scalar(out=lgt[0:6, fc * 512:(fc + 1) * 512], in0=pm[4][0:6, 0:512],
                                            scalar1=t_fvec[0:6, 16:17], scalar2=None, op0=ALU.add)
                            v.tensor_scalar(out=s16[:, fc * 512:(fc + 1) * 512], in0=pm[5][0:16, 0:512],
                                            scalar1=1.0, scalar2=None, op0=ALU.mult)
                            v.tensor_scalar(out=h96[:, fc * 512:(fc + 1) * 512], in0=pm[6][0:96, 0:512],
                                            scalar1=1.0, scalar2=None, op0=ALU.mult)
                            v.sem_inc(sem["DVE"], 1)
                    C.inc("DVE", 2)

                    pe8 = C.get("PE")
                    dve_tr0 = C.get("DVE")
                    @block.tensor
                    def _(t):
                        t = W(t)
                        t.wait_ge(sem["DVE"], dve_tr0)
                        for lb in range(8):
                            if lb >= 1:
                                t.wait_ge(sem["DVE"], dve_tr0 + lb)
                            t.transpose(out=pm[7][:, 0:128], in_=lgt[:, lb * 128:(lb + 1) * 128],
                                        identity=t_ident[:])
                            t.sem_inc(sem["PE"], 1)
                    C.inc("PE", 8)

                    @block.vector
                    def _(v):
                        v = W(v)
                        for lb in range(8):
                            v.wait_ge(sem["PE"], pe8 + lb + 1)
                            v.tensor_copy(out=gtok[:, lb * 6:(lb + 1) * 6], in_=pm[7][:, 0:6])
                            v.sem_inc(sem["DVE"], 1)
                    C.inc("DVE", 8)

                    act9 = C.get("ACT")
                    dve_mark = []
                    @block.vector
                    def _(v):
                        v = W(v)
                        def fence():
                            v.sem_inc(sem["DVE"], 1)
                            C.inc("DVE")
                            v.wait_ge(sem["DVE"], C.get("DVE"))
                        v.wait_ge(sem["DVE"], C.get("DVE"))
                        v.tensor_reduce(out=gm[:, 0:8], in_=gtok[:].rearrange("p (a b) -> p a b", b=6),
                                        axis=mybir.AxisListType.X, op=ALU.max)
                        fence()
                        v.tensor_scalar(out=gm[:, 0:8], in0=gm[:, 0:8], scalar1=-1.0, scalar2=None, op0=ALU.mult)
                        v.sem_inc(sem["DVE"], 1)
                        C.inc("DVE")
                        dve_mark.append(C.get("DVE"))
                        v.wait_ge(sem["ACT"], act9 + 8)
                        v.tensor_reduce(out=gm[:, 8:16], in_=ge_[:].rearrange("p (a b) -> p a b", b=6),
                                        axis=mybir.AxisListType.X, op=ALU.add)
                        fence()
                        v.reciprocal(out=gm[:, 8:16], in_=gm[:, 8:16])
                        fence()
                        for lb in range(8):
                            v.tensor_scalar(out=ge_[:, lb * 6:(lb + 1) * 6], in0=ge_[:, lb * 6:(lb + 1) * 6],
                                            scalar1=gm[:, 8 + lb:9 + lb], scalar2=None, op0=ALU.mult)
                        fence()
                        v.tensor_reduce(out=gm[:, 16:24], in_=ge_[:].rearrange("p (a b) -> p a b", b=6),
                                        axis=mybir.AxisListType.X, op=ALU.max)
                        fence()
                        for lb in range(8):
                            v.tensor_scalar(out=gw[:, lb * 6:(lb + 1) * 6], in0=ge_[:, lb * 6:(lb + 1) * 6],
                                            scalar1=gm[:, 16 + lb:17 + lb], scalar2=None, op0=ALU.is_ge)
                        fence()
                        v.tensor_tensor(out=gw[:], in0=gw[:], in1=ge_[:], op=ALU.mult)
                        fence()
                        v.tensor_tensor(out=gtok[:], in0=ge_[:], in1=gw[:], op=ALU.subtract)
                        fence()
                        v.tensor_reduce(out=gm[:, 16:24], in_=gtok[:].rearrange("p (a b) -> p a b", b=6),
                                        axis=mybir.AxisListType.X, op=ALU.max)
                        fence()
                        for lb in range(8):
                            v.tensor_scalar(out=gw[:, lb * 6:(lb + 1) * 6], in0=ge_[:, lb * 6:(lb + 1) * 6],
                                            scalar1=gm[:, 16 + lb:17 + lb], scalar2=None, op0=ALU.is_ge)
                        fence()
                        v.tensor_tensor(out=gw[:], in0=gw[:], in1=ge_[:], op=ALU.mult)
                        v.sem_inc(sem["DVE"], 1)
                    C.inc("DVE", 1)

                    @block.scalar
                    def _(a):
                        a = W(a)
                        a.wait_ge(sem["DVE"], dve_mark[0])
                        for lb in range(8):
                            a.activation(out=ge_[:, lb * 6:(lb + 1) * 6], in_=gtok[:, lb * 6:(lb + 1) * 6],
                                         func=AF.Exp, bias=gm[:, lb:lb + 1]).then_inc(sem["ACT"], 1)
                        C.inc("ACT", 8)

                    pe9 = C.get("PE")
                    dve9 = C.get("DVE")
                    @block.vector
                    def _(v):
                        v = W(v)
                        v.wait_ge(sem["DVE"], dve9)
                        for lb in range(8):
                            if lb >= 1:
                                v.wait_ge(sem["PE"], pe9 + lb)
                            v.tensor_copy(out=gpad[:, 0:6], in_=gw[:, lb * 6:(lb + 1) * 6])
                            v.sem_inc(sem["DVE"], 1)
                            v.wait_ge(sem["PE"], pe9 + lb + 1)
                            v.tensor_copy(out=wT6[:, lb * 128:(lb + 1) * 128], in_=pm[7][0:6, 0:128])
                            v.sem_inc(sem["DVE"], 1)
                    C.inc("DVE", 16)

                    @block.tensor
                    def _(t):
                        t = W(t)
                        for lb in range(8):
                            t.wait_ge(sem["DVE"], dve9 + 2 * lb + 1)
                            t.transpose(out=pm[7][:, 0:128], in_=gpad[:], identity=t_ident[:])
                            t.sem_inc(sem["PE"], 1)
                    C.inc("PE", 8)

                    @block.sync
                    def _(sp):
                        sp = W(sp)
                        sp.wait_ge(sem["DVE"], C.get("DVE"))
                        sp.dma_start(out=wrow_dr[:, :], in_=wT6[:]).then_inc(sem["LD"], 16); C.inc("LD")
                        sp.wait_ge(sem["LD"], 16 * C.get("LD"))
                        for e6 in range(6):
                            sp.dma_start(out=w96[e6 * 16:(e6 + 1) * 16, :],
                                         in_=wrow_dr[e6:e6 + 1, :].broadcast_to([16, Lc])
                                         ).then_inc(sem["LD"], 16)
                            C.inc("LD")

                    @block.vector
                    def _(v):
                        v = W(v)
                        v.wait_ge(sem["LD"], 16 * C.get("LD"))
                        v.tensor_tensor(out=h96[:], in0=h96[:], in1=w96[:], op=ALU.mult)
                        v.sem_inc(sem["DVE"], 1)
                    C.inc("DVE")
                    barrier()
                    sg3b.close()
                    if _STOP[0] == 'G2':
                        return nc

                    with ExitStack() as sg4:
                        slabB = sg4.enter_context(nc.sbuf_tensor("slabB", [128, DT * D], BF16))
                        evb = [sg4.enter_context(nc.sbuf_tensor(f"evb{i}", [128, 512], F32)) for i in range(3)]

                        @block.gpsimd
                        def _(g):
                            g = W(g)
                            for kt in range(DT):
                                g.dma_start(out=slabB[:, kt * D:(kt + 1) * D],
                                            in_=wbase_in[kt * 128:(kt + 1) * 128, :]).then_inc(sem["GD"], 16)
                                C.inc("GD")

                        units = [(fc, dt) for fc in range(2) for dt in range(DT)]
                        pe10 = C.get("PE")
                        act10 = C.get("ACT")
                        dve10 = C.get("DVE")
                        eg0 = [C.get("E%d" % i) for i in range(3)]
                        @block.tensor
                        def _(t):
                            t = W(t)
                            t.wait_ge(sem["GD"], 16 * C.get("GD"))
                            for u, (fc, dt) in enumerate(units):
                                if u >= 4:
                                    t.wait_ge(sem["DVE"], dve10 + (u - 3))
                                for kt in range(DT):
                                    t.matmul(out=pm[u % 4][:, 0:512],
                                             lhsT=slabB[:, kt * D + dt * 128:kt * D + (dt + 1) * 128],
                                             rhs=h2[:, kt * Lc + fc * 512:kt * Lc + (fc + 1) * 512],
                                             start=(kt == 0), stop=False)
                                t.matmul(out=pm[u % 4][:, 0:512],
                                         lhsT=bss[:, dt * 128:(dt + 1) * 128],
                                         rhs=s16[:, fc * 512:(fc + 1) * 512],
                                         start=False, stop=False)
                                t.matmul(out=pm[u % 4][:, 0:512],
                                         lhsT=brss[:, dt * 128:(dt + 1) * 128],
                                         rhs=h96[:, fc * 512:(fc + 1) * 512],
                                         start=False, stop=True)
                                t.sem_inc(sem["PE"], 1)
                        C.inc("PE", len(units))

                        @block.vector
                        def _(v):
                            v = W(v)
                            for u, (fc, dt) in enumerate(units):
                                v.wait_ge(sem["PE"], pe10 + u + 1)
                                if u >= 3:
                                    v.wait_ge(sem["E%d" % (u % 3)], 16 * (eg0[u % 3] + (u - 3) // 3 + 1))
                                v.tensor_scalar(out=evb[u % 3][:], in0=pm[u % 4][:, 0:512],
                                                scalar1=BBASE(dt), scalar2=GGB(dt),
                                                op0=ALU.add, op1=ALU.mult)
                                v.tensor_tensor(out=evb[u % 3][:], in0=evb[u % 3][:],
                                                in1=x1T[:, dt * Lc + fc * 512:dt * Lc + (fc + 1) * 512],
                                                op=ALU.add)
                                v.sem_inc(sem["DVE"], 1)
                        C.inc("DVE", len(units))

                        @block.sync
                        def _(sp):
                            sp = W(sp)
                            for u, (fc, dt) in enumerate(units):
                                sp.wait_ge(sem["DVE"], dve10 + u + 1)
                                sp.dma_start(out=out_T[dt * 128:(dt + 1) * 128, fc * 512:(fc + 1) * 512],
                                             in_=evb[u % 3][:]).then_inc(sem["E%d" % (u % 3)], 16)
                                C.inc("E%d" % (u % 3))
                            for i in range(3):
                                sp.wait_ge(sem["E%d" % i], 16 * C.get("E%d" % i))
                            if _DBG[0]:
                                sp.dma_start(out=dbg[0:Ec, :], in_=spill_ss[:, :].bitcast(F32)[:, 0:512].rearrange("a b -> a b") if False else spill_ss[:, :]).then_inc(sem["ST"], 16) if False else None
                                pass
                        if _DBG[0]:
                            @block.gpsimd
                            def _(g):
                                g = W(g)
                                g.dma_start(out=dbg[0:Ec, :], in_=spill_ss[:, :]).then_inc(sem["GD"], 16); C.inc("GD")
                                g.dma_start(out=dbg[Ec:2 * Ec, :], in_=spill_yp[:, :]).then_inc(sem["GD"], 16); C.inc("GD")
                                for dt in range(DT):
                                    g.dma_start(out=dbg[10496 + dt * 128:10496 + (dt + 1) * 128, :],
                                                in_=h2[:, dt * Lc:(dt + 1) * Lc]).then_inc(sem["GD"], 16); C.inc("GD")
                                g.dma_start(out=dbg[12544:12550, :], in_=wT6[:, :]).then_inc(sem["GD"], 16); C.inc("GD")
                                g.dma_start(out=dbg[12550:12566, :], in_=s16[:, :]).then_inc(sem["GD"], 16); C.inc("GD")
                                g.dma_start(out=dbg[12566:12662, :], in_=h96[:, :]).then_inc(sem["GD"], 16); C.inc("GD")
                                g.dma_start(out=dbg[12662:12758, :], in_=w96[:, :]).then_inc(sem["GD"], 16); C.inc("GD")
                                g.dma_start(out=dbg[12758:12886, 0:48], in_=gw[:, :]).then_inc(sem["GD"], 16); C.inc("GD")
                                g.dma_start(out=dbg[12886:13014, 0:256], in_=t_state[:, :]).then_inc(sem["GD"], 16); C.inc("GD")
                            @block.sync
                            def _(sp):
                                sp = W(sp)
                                sp.dma_start(out=dbg[2 * Ec:2 * Ec + D, :], in_=sh_out[0:D, :]).then_inc(sem["LD"], 16); C.inc("LD")
                                sp.dma_start(out=dbg[2 * Ec + D:2 * Ec + 2 * D, :], in_=sh_out[2 * D:3 * D, :]).then_inc(sem["LD"], 16); C.inc("LD")
                                for dt in range(DT):
                                    sp.dma_start(out=dbg[2 * Ec + 2 * D + dt * 128:2 * Ec + 2 * D + (dt + 1) * 128, :],
                                                 in_=x1T[:, dt * Lc:(dt + 1) * Lc]).then_inc(sem["LD"], 16); C.inc("LD")
                                sp.dma_start(out=dbg[6 * D:6 * D + 1, 0:1024], in_=modrow_dr[:, 0:1024]).then_inc(sem["LD"], 16); C.inc("LD")
                        barrier()
    return nc


_NC_CACHE = {}

def kernel(**inputs):
    f32 = np.float32
    resid = np.asarray(inputs["L0_attn_resid"], f32)
    norm_w = np.asarray(inputs["L0_attn_norm_w"], f32)
    skip_w = np.asarray(inputs["L0_attn_skip_w"], f32)
    in_w = np.asarray(inputs["L0_attn_in_w"], f32)
    conv_w = np.asarray(inputs["L0_attn_conv_w"], f32)
    conv_b = np.asarray(inputs["L0_attn_conv_b"], f32)
    wd1 = np.asarray(inputs["L0_attn_wd1"], f32)
    wd2 = np.asarray(inputs["L0_attn_wd2"], f32)
    wd2_b = np.asarray(inputs["L0_attn_wd2_b"], f32)
    wB = np.asarray(inputs["L0_attn_wB"], f32)
    wC = np.asarray(inputs["L0_attn_wC"], f32)
    A_log = np.asarray(inputs["L0_attn_A_log"], f32)
    W_D = np.asarray(inputs["L0_attn_W_D"], f32)
    out_w = np.asarray(inputs["L0_attn_out_w"], f32)
    W_base = np.asarray(inputs["L0_ffn_W_base"], f32)
    b_base = np.asarray(inputs["L0_ffn_b_base"], f32)
    A_s = np.asarray(inputs["L0_ffn_A_s"], f32)
    B_s = np.asarray(inputs["L0_ffn_B_s"], f32)
    A_r = np.asarray(inputs["L0_ffn_A_r"], f32)
    B_r = np.asarray(inputs["L0_ffn_B_r"], f32)
    W_router = np.asarray(inputs["L0_ffn_W_router"], f32)
    b_router = np.asarray(inputs["L0_ffn_b_router"], f32)
    c_in = np.asarray(inputs["L0_c"], f32)
    wmod = np.asarray(inputs["L0_wmod"], f32)

    def colmaj(vec):
        return np.ascontiguousarray(vec.reshape(16, 128).T)

    cvec = np.zeros((128, 64), f32)
    fvec = np.zeros((128, 32), f32)
    ident = np.eye(128, dtype=f32)
    A_rT = np.ascontiguousarray(A_r.transpose(1, 0, 2).reshape(D, 96))
    B_rs = np.ascontiguousarray(B_r.reshape(96, D))
    fvec[:, 0:16] = colmaj(b_base)
    fvec[0:6, 16] = b_router

    in_maps = []
    for core in range(8):
        b, h = core // 2, core % 2
        sl = slice(h * Ec, (h + 1) * Ec)
        cv = cvec.copy()
        cv[:, 0:16] = colmaj(conv_b[sl])
        cv[:, 16:32] = colmaj(wd2_b[sl])
        cv[:, 32:48] = colmaj(W_D[sl])
        cv[:, 48:64] = colmaj(norm_w)
        m = {
            "residT": np.ascontiguousarray(resid[b].T),
            "role": np.array([[h]], np.uint32),
            "bid": np.array([[b]], np.uint32),
            "c4": np.ascontiguousarray(c_in.T[h * 1024:(h + 1) * 1024].reshape(8, 128, 4).transpose(1, 0, 2).reshape(128, 32)),
            "wmod_s": np.ascontiguousarray(wmod[h * 1024:(h + 1) * 1024, :]),
            "in_wT": np.ascontiguousarray(in_w[sl, :].T),
            "skip_wT": np.ascontiguousarray(skip_w[sl, :].T),
            "out_wT": np.ascontiguousarray(out_w[:, sl].T),
            "wd1T": np.ascontiguousarray(wd1[:, sl].T),
            "wd2T": np.ascontiguousarray(wd2[sl, :].T),
            "wBT": np.ascontiguousarray(wB[:, sl].T),
            "wCT": np.ascontiguousarray(wC[:, sl].T),
            "alog": np.ascontiguousarray(A_log[sl].reshape(16, 128, 16).transpose(1, 0, 2).reshape(128, 256)),
            "convw": np.ascontiguousarray(conv_w[sl].reshape(16, 128, 4).transpose(1, 0, 2).reshape(128, 64)),
            "cvec": cv,
            "W_base": W_base,
            "fvec": fvec,
            "A_s": A_s,
            "B_s": B_s,
            "A_rT": A_rT,
            "B_rs": B_rs,
            "W_router": W_router,
            "ident": ident,
        }
        in_maps.append(m)

    if "nc" not in _NC_CACHE:
        _NC_CACHE["nc"] = build_program()
    nc = _NC_CACHE["nc"]
    res = run_bass_kernel_spmd(nc, in_maps, core_ids=list(range(8)))
    full = np.empty((B, L, D), f32)
    for core in range(8):
        b, h = core // 2, core % 2
        full[b, h * Lc:(h + 1) * Lc, :] = np.asarray(res.results[core]["out_T"]).T
    return full
